# version 02
# baseline (speedup 1.0000x reference)
import numpy as np
from contextlib import ExitStack

import concourse.bass as bass
import concourse.mybir as mybir
from concourse.bass_utils import run_bass_kernel_spmd
import ml_dtypes

F32 = mybir.dt.float32
BF16 = mybir.dt.bfloat16
AF = mybir.ActivationFunctionType
ALU = mybir.AluOpType
AX = mybir.AxisListType

ENGINES = ("tensor", "vector", "scalar", "gpsimd", "sync")
T = 4096
D = 1024
NCORES = 8


def cdiv(a, b):
    return (a + b - 1) // b



class Prog:

    NPOOL = 13

    def __init__(self, nc):
        self.nc = nc
        self.ops = {e: [] for e in ENGINES}
        self.log_cnt = {}
        self.binding = {}
        self.phys_cnt = {}
        self.pool_rr = {}
        self.sem_handle = {}
        self.sem_names = []

    def sem(self, name):
        if name not in self.log_cnt:
            self.log_cnt[name] = 0
        return name

    def cnt(self, name):
        return self.log_cnt[self.sem(name)]

    def _phys_new(self, key):
        i = self.pool_rr.get(key, 0)
        self.pool_rr[key] = i + 1
        pname = f"px_{key[0]}_{key[1]}_{i % self.NPOOL}"
        if pname not in self.phys_cnt:
            self.phys_cnt[pname] = 0
            self.sem_names.append(pname)
        return pname

    def _bind(self, name, eng, is_dma):
        if name not in self.binding:
            self.binding[name] = self._phys_new((eng, "d" if is_dma else "c"))
        return self.binding[name]

    def _resolve(self, waits):
        wl = []
        for wspec in waits:
            if isinstance(wspec, tuple):
                s, v = wspec
            else:
                s, v = wspec, self.cnt(wspec)
            self.sem(s)
            if v <= 0 or s not in self.binding:
                continue
            phys = self.binding[s]
            tgt = self.phys_cnt[phys] - (self.log_cnt[s] - v)
            if tgt > 0:
                wl.append((phys, tgt))
        out = {}
        for p, v in wl:
            out[p] = max(out.get(p, 0), v)
        return list(out.items())

    def do(self, eng, build_fn, waits=(), inc=None, inc_n=1):
        wl = self._resolve(waits)
        if inc is not None:
            self.sem(inc)
            phys = self._bind(inc, eng, inc_n >= 16)
            self.log_cnt[inc] += inc_n
            self.phys_cnt[phys] += inc_n
        handles = self.sem_handle

        def thunk(e, wl=wl, build_fn=build_fn, inc=inc, inc_n=inc_n,
                  phys=(self.binding[inc] if inc is not None else None)):
            for s, v in wl:
                e.wait_ge(handles[s], v)
            instr = build_fn(e)
            if inc is not None:
                instr.then_inc(handles[phys], inc_n)
        self.ops[eng].append(thunk)

    def wait(self, eng, *waits):
        wl = self._resolve(waits)
        handles = self.sem_handle

        def thunk(e, wl=wl):
            for s, v in wl:
                e.wait_ge(handles[s], v)
        self.ops[eng].append(thunk)

    def barrier(self, engines=ENGINES):
        marks = []
        for e in engines:
            pname = "bar_" + e
            if pname not in self.phys_cnt:
                self.phys_cnt[pname] = 0
                self.sem_names.append(pname)
            self.phys_cnt[pname] += 1
            marks.append((e, pname, self.phys_cnt[pname]))
        handles = self.sem_handle
        for e, s, v in marks:
            def thunk(eng, s=s):
                eng.nop().then_inc(handles[s], 1)
            self.ops[e].append(thunk)
        allw = tuple((s, v) for (_, s, v) in marks)
        for e in engines:
            def thunk(eng, allw=allw):
                for s, v in allw:
                    eng.wait_ge(handles[s], v)
            self.ops[e].append(thunk)

    def finish(self, es):
        for name in self.sem_names:
            self.sem_handle[name] = es.enter_context(self.nc.semaphore(name))
        with self.nc.Block() as block:
            for ename in ENGINES:
                getattr(block, ename)(
                    lambda eng, stream=self.ops[ename]:
                        [t(eng) for t in stream])


class NS:
    pass



INPUT_SHAPES = {
    "L0_attn_x": (T, D),
    "L0_attn_in_proj_w": (2, 4256, 1024),
    "L0_attn_conv_w": (2, 2176, 4),
    "L0_attn_conv_b": (2, 2176),
    "L0_attn_dt_bias": (2, 32),
    "L0_attn_A_log": (2, 32),
    "L0_attn_D": (2, 32),
    "L0_attn_gnorm_w": (2, 2048),
    "L0_attn_out_proj_w": (2, 1024, 2048),
    "L0_attn_norm_w": (2, 1024),
    "L0_ffn_w_gate": (2816, 1024),
    "L0_ffn_w_up": (2816, 1024),
    "L0_ffn_w_down": (1024, 2816),
    "L1_attn_attn_norm_w": (1024,),
    "L1_attn_Wq": (1024, 512),
    "L1_attn_Wk": (1024, 512),
    "L1_attn_Wv": (1024, 1024),
    "L1_attn_Wg": (1024, 1024),
    "L1_attn_Wgk1": (1024, 16),
    "L1_attn_Wgk2": (16, 512),
    "L1_attn_bgk2": (512,),
    "L1_attn_gnorm_w": (256,),
    "L1_attn_Wo": (1024, 1024),
    "L1_attn_mlp_norm_w": (1024,),
    "L1_attn_W_gate": (1024, 2816),
    "L1_attn_W_up": (1024, 2816),
    "L1_attn_W_down": (2816, 1024),
    "L1_ffn_expert_weights": (4096, 2),
    "L1_ffn_w1": (1024, 8192),
    "L1_ffn_w2": (8192, 1024),
    "L2_attn_W_qkv": (3072, 1024),
    "L2_attn_b_qkv": (3072,),
    "L2_attn_W_o": (1024, 1024),
    "L2_attn_b_o": (1024,),
    "L2_ffn_gamma_x": (1024,),
    "L2_ffn_gamma_slot": (1024,),
    "L2_ffn_slot_embeds": (8, 512, 1024),
    "L2_ffn_w1": (1024, 4096),
    "L2_ffn_w2": (4096, 1024),
    "L3_attn_w_qkv": (3072, 1024),
    "L3_attn_w_o": (1024, 1024),
    "L3_ffn_router_w": (8, 1024),
    "L3_ffn_w13": (8, 1024, 4096),
    "L3_ffn_w2": (8, 2048, 1024),
}


class Builder:
    def __init__(self, ext_scratch=()):
        self.ext_scratch = set(ext_scratch)
        nc = bass.Bass("TRN2", target_bir_lowering=False, debug=False)
        self.nc = nc
        self.P = Prog(nc)
        self.es = ExitStack()
        self.IN = {}
        for name, shape in INPUT_SHAPES.items():
            self.IN[name] = nc.dram_tensor(
                name, list(shape), F32, kind="ExternalInput").ap()
        self.ident_in = nc.dram_tensor(
            "c_ident", [128, 128], F32, kind="ExternalInput").ap()
        self.y_out = nc.dram_tensor("y", [T, D], F32, kind="ExternalOutput").ap()
        self.SCR = {}
        self.dbg_outputs = []

        self.x_fm = self.scratch("x_fm", (D, T))
        self.xn_fm = self.scratch("xn_fm", (D, T))
        self.s_fm = self.scratch("s_fm", (D, T), BF16)
        self.g_fm = self.scratch("g_fm", (D, T), BF16)

        es = self.es
        self.WA = es.enter_context(nc.sbuf_tensor("WA", [128, 16384], BF16))
        self.WB = es.enter_context(nc.sbuf_tensor("WB", [128, 16384], BF16))
        self.XA = es.enter_context(nc.sbuf_tensor("XA", [128, 16384], BF16))
        self.XB = es.enter_context(nc.sbuf_tensor("XB", [128, 16384], BF16))
        self.FA = es.enter_context(nc.sbuf_tensor("FA", [128, 8192], F32))
        self.FB = es.enter_context(nc.sbuf_tensor("FB", [128, 4096], F32))
        self.FC = es.enter_context(nc.sbuf_tensor("FC", [128, 2048], F32))
        self.ROWS = es.enter_context(nc.sbuf_tensor("ROWS", [128, 512], F32))
        self.BIAS = es.enter_context(nc.sbuf_tensor("BIAS", [128, 64], F32))
        self.COEFR = es.enter_context(nc.sbuf_tensor("COEFR", [1, 4096], F32))
        self.OUTS = es.enter_context(nc.sbuf_tensor("OUTS", [128, 1024], BF16))
        self.CONST = es.enter_context(nc.sbuf_tensor("CONST", [128, 32], F32))
        self.ONESROW = es.enter_context(nc.sbuf_tensor("ONESROW", [1, 128], BF16))
        self.ONESCOLB = es.enter_context(nc.sbuf_tensor("ONESCOLB", [128, 1], BF16))
        self.ONESROWF = es.enter_context(nc.sbuf_tensor("ONESROWF", [1, 128], F32))
        self.IDENT = es.enter_context(nc.sbuf_tensor("IDENT", [128, 128], F32))
        self.IDENTB = es.enter_context(nc.sbuf_tensor("IDENTB", [128, 128], BF16))
        self.PS = [es.enter_context(nc.psum_tensor(f"PS{i}", [128, 512], F32))
                   for i in range(8)]

        self.const_idx = {}
        self._const_next = 0

    def scratch(self, name, shape, dtype=F32):
        kind = "ExternalInput" if name in self.ext_scratch else "Internal"
        self.SCR[name] = self.nc.dram_tensor(
            name, list(shape), dtype, kind=kind).ap()
        return self.SCR[name]

    def const(self, val):
        if val not in self.const_idx:
            idx = self._const_next
            self._const_next += 1
            assert idx < 32
            self.const_idx[val] = idx
            self.P.do("vector",
                      lambda e, idx=idx, val=val:
                      e.memset(self.CONST[:, idx:idx + 1], float(val)),
                      inc="c_init")
        idx = self.const_idx[val]
        return self.CONST[:, idx:idx + 1]

    def dbg(self, scr_name):
        self.dbg_outputs.append(scr_name)

    def emit_init(self):
        P = self.P
        P.do("gpsimd", lambda e: e.dma_start(out=self.IDENT[:, :],
                                             in_=self.ident_in[:, :]),
             inc="c_initd", inc_n=16)
        P.do("gpsimd", lambda e: e.dma_start(out=self.IDENTB[:, :],
                                             in_=self.ident_in[:, :]),
             inc="c_initd", inc_n=16)
        P.do("vector", lambda e: e.memset(self.ONESROW[0:1, :], 1.0),
             inc="c_init")
        P.do("vector", lambda e: e.memset(self.ONESCOLB[:, 0:1], 1.0),
             inc="c_init")
        P.do("vector", lambda e: e.memset(self.ONESROWF[0:1, :], 1.0),
             inc="c_init")
        for v in (1.0, 1e-5, 1e-6, float(np.log(128.0 ** -0.5))):
            self.const(v)
        def _mkfills(e):
            self._fill_reg = e.to_reg(-1e30)
            self._fill0_reg = e.to_reg(0.0)
            return None
        P.do("gpsimd", _mkfills)
        P.barrier()

    def emit_x_in(self):
        P = self.P
        x_in = self.IN["L0_attn_x"]
        for g in range(8):
            t0 = g * 512
            for j in range(4):
                P.do("sync", lambda e, t0=t0, j=j: e.dma_start(
                    out=self.FA[:, j * 1024:(j + 1) * 1024],
                    in_=x_in[t0 + j * 128: t0 + (j + 1) * 128, :]),
                    inc="xin_ld", inc_n=16)
            P.wait("tensor", "xin_ld")
            for dd in range(8):
                bank = self.PS[dd % 2]
                P.wait("tensor", ("xin_ev", max(0, P.cnt("xin_ev") - 1)))
                for j in range(4):
                    P.do("tensor", lambda e, dd=dd, j=j, bank=bank: e.matmul(
                        bank[:, j * 128:(j + 1) * 128],
                        self.FA[:, j * 1024 + dd * 128:
                                j * 1024 + (dd + 1) * 128],
                        self.IDENT[:, :], start=True, stop=True),
                        inc="xin_mm" if j == 3 else None)
                P.do("vector", lambda e, dd=dd, bank=bank: e.tensor_copy(
                    self.FB[:, dd * 512:(dd + 1) * 512], bank[:, :]),
                    waits=("xin_mm",), inc="xin_ev")
                P.do("sync", lambda e, dd=dd, t0=t0: e.dma_start(
                    out=self.x_fm[dd * 128:(dd + 1) * 128, t0:t0 + 512],
                    in_=self.FB[:, dd * 512:(dd + 1) * 512]),
                    waits=("xin_ev",), inc="xin_st", inc_n=16)
            P.wait("sync", "xin_st")
            P.barrier()

    def emit_weight_prep(self):
        P = self.P
        nat_bf = self.scratch("nat_bf", (4352, 2816), BF16)
        self.tw = {}
        jobs = [
            ("l0_inproj0", self.IN["L0_attn_in_proj_w"][0], 4256, 1024),
            ("l0_inproj1", self.IN["L0_attn_in_proj_w"][1], 4256, 1024),
            ("l0_outproj0", self.IN["L0_attn_out_proj_w"][0], 1024, 2048),
            ("l0_outproj1", self.IN["L0_attn_out_proj_w"][1], 1024, 2048),
            ("l0f_gate", self.IN["L0_ffn_w_gate"], 2816, 1024),
            ("l0f_up", self.IN["L0_ffn_w_up"], 2816, 1024),
            ("l0f_down", self.IN["L0_ffn_w_down"], 1024, 2816),
            ("l2_qkv", self.IN["L2_attn_W_qkv"], 3072, 1024),
            ("l2_o", self.IN["L2_attn_W_o"], 1024, 1024),
            ("l3_qkv", self.IN["L3_attn_w_qkv"], 3072, 1024),
            ("l3_o", self.IN["L3_attn_w_o"], 1024, 1024),
        ]
        for name, src, O, I in jobs:
            dst = self.scratch("tw_" + name, (I, O), BF16)
            self.tw[name] = dst
            n_chunks = cdiv(O, 128)
            for ai in range(n_chunks):
                pr = min(128, O - ai * 128)
                h = (ai % 2) * 5632
                P.do("gpsimd", lambda e, src=src, ai=ai, pr=pr, I=I, h=h:
                     e.dma_start(out=self.WA[0:pr, h:h + I],
                                 in_=src[ai * 128: ai * 128 + pr, :]),
                     inc="wp_ld", inc_n=16,
                     waits=(("wp_st", max(0, P.cnt("wp_st") - 16)),))
                P.do("sync", lambda e, ai=ai, pr=pr, I=I, nat_bf=nat_bf, h=h:
                     e.dma_start(out=nat_bf[ai * 128: ai * 128 + pr, 0:I],
                                 in_=self.WA[0:pr, h:h + I]),
                     waits=("wp_ld", "wp_t"), inc="wp_st", inc_n=16)
            FSP = 2048
            Opad = cdiv(O, 128) * 128
            wi = 0
            for i0 in range(0, I, 128):
                for o0 in range(0, O, FSP):
                    ff = min(FSP, Opad - o0)
                    fv = min(FSP, O - o0)
                    h = (wi % 2) * 8192
                    wi += 1
                    P.do("sync", lambda e, i0=i0, o0=o0, ff=ff, nat_bf=nat_bf, h=h:
                         e.dma_start_transpose(self.WB[:, h:h + ff],
                                               nat_bf[o0:o0 + ff,
                                                      i0:i0 + 128]),
                         waits=("wp_st",
                                ("wp_tst", max(0, P.cnt("wp_tst") - 16))),
                         inc="wp_t", inc_n=16)
                    P.do("sync", lambda e, i0=i0, o0=o0, fv=fv, dst=dst, h=h:
                         e.dma_start(out=dst[i0:i0 + 128, o0:o0 + fv],
                                     in_=self.WB[:, h:h + fv]),
                         waits=("wp_t",), inc="wp_tst", inc_n=16)
        P.barrier()

    def emit_module_ln(self):
        P = self.P
        ones_col = self.const(1.0)
        eps_col = self.const(1e-5)
        for tb in range(T // 512):
            t0 = tb * 512
            for k in range(8):
                P.do("sync", lambda e, k=k, t0=t0: e.dma_start(
                    out=self.FA[:, k * 512:(k + 1) * 512],
                    in_=self.x_fm[k * 128:(k + 1) * 128, t0:t0 + 512]),
                    inc="ln_ld", inc_n=16)
            P.wait("tensor", "ln_ld")
            for k in range(8):
                P.do("tensor", lambda e, k=k: e.matmul(
                    self.PS[0][0:1, 0:512], ones_col,
                    self.FA[:, k * 512:(k + 1) * 512],
                    start=(k == 0), stop=(k == 7)),
                    inc="ln_mm1" if k == 7 else None)
            P.wait("scalar", "ln_ld")
            for k in range(8):
                P.do("scalar", lambda e, k=k: e.activation(
                    self.FB[:, k * 512:(k + 1) * 512],
                    self.FA[:, k * 512:(k + 1) * 512], AF.Square),
                    inc="ln_sq" if k == 7 else None)
            P.wait("tensor", "ln_sq")
            for k in range(8):
                P.do("tensor", lambda e, k=k: e.matmul(
                    self.PS[1][0:1, 0:512], ones_col,
                    self.FB[:, k * 512:(k + 1) * 512],
                    start=(k == 0), stop=(k == 7)),
                    inc="ln_mm2" if k == 7 else None)
            S_ = lambda i: self.FC[0:1, i * 512:(i + 1) * 512]
            P.do("vector", lambda e: e.tensor_scalar(
                S_(0), self.PS[0][0:1, 0:512], 1.0 / D, None, ALU.mult),
                waits=("ln_mm1",), inc="ln_v")
            P.do("vector", lambda e: e.tensor_scalar(
                S_(1), self.PS[1][0:1, 0:512], 1.0 / D, None, ALU.mult),
                waits=("ln_mm2", "ln_v"), inc="ln_v")
            P.do("vector", lambda e: e.tensor_tensor(
                S_(2), S_(0), S_(0), ALU.mult),
                waits=("ln_v",), inc="ln_v")
            P.do("vector", lambda e: e.tensor_tensor(
                S_(3), S_(1), S_(2), ALU.subtract),
                waits=("ln_v",), inc="ln_v")
            P.do("scalar", lambda e: e.activation(
                S_(2), S_(3), AF.Ln,
                bias=eps_col[0:1, :]),
                waits=("ln_v",), inc="ln_s")
            P.do("scalar", lambda e: e.activation(
                S_(1), S_(2), AF.Exp, scale=-0.5),
                waits=("ln_s",), inc="ln_s")
            P.do("vector", lambda e: e.scalar_tensor_tensor(
                S_(0), S_(0), -1.0, S_(1),
                ALU.mult, ALU.mult),
                waits=("ln_s",), inc="ln_v")
            P.do("tensor", lambda e: e.matmul(
                self.PS[2][:, 0:512], self.ONESROWF[0:1, :],
                S_(1), start=True, stop=True),
                waits=("ln_s",), inc="ln_bc")
            P.do("tensor", lambda e: e.matmul(
                self.PS[3][:, 0:512], self.ONESROWF[0:1, :],
                S_(0), start=True, stop=True),
                waits=("ln_v",), inc="ln_bc")
            P.wait("vector", "ln_bc")
            for k in range(8):
                P.do("vector", lambda e, k=k: e.tensor_tensor(
                    self.FB[:, k * 512:(k + 1) * 512],
                    self.FA[:, k * 512:(k + 1) * 512],
                    self.PS[2][:, 0:512], ALU.mult),
                    inc="ln_ap1")
            for k in range(8):
                P.do("vector", lambda e, k=k: e.tensor_tensor(
                    self.FB[:, k * 512:(k + 1) * 512],
                    self.FB[:, k * 512:(k + 1) * 512],
                    self.PS[3][:, 0:512], ALU.add),
                    inc="ln_ap2")
            for k in range(8):
                P.do("sync", lambda e, k=k, t0=t0: e.dma_start(
                    out=self.xn_fm[k * 128:(k + 1) * 128, t0:t0 + 512],
                    in_=self.FB[:, k * 512:(k + 1) * 512]),
                    waits=(("ln_ap2", P.cnt("ln_ap2") - 7 + k),),
                    inc="ln_st", inc_n=16)
            P.wait("sync", "ln_st")
            P.barrier()

    def load_bias(self, tag, bias_hbm, Dout, col0=0):
        P = self.P
        for oc in range(Dout // 128):
            P.do("gpsimd", lambda e, oc=oc: e.dma_start(
                out=self.BIAS[:, col0 + oc:col0 + oc + 1],
                in_=bias_hbm[oc * 128:(oc + 1) * 128]),
                inc=tag + "_bld", inc_n=16)
        return col0

    def emit_proj(self, tag, act_hbm, Din, Dout, w_src, dst_hbm,
                  pre_transposed=False, act_is_f32=True, epilogue=None,
                  o_slab=512, t_block=512, act_resident=None,
                  stage_barrier=True, out_f32=False):
        P = self.P
        nK = Din // 128
        if act_resident is None:
            act_resident = (nK * T <= 32768)
        assert Dout % 128 == 0 and o_slab % 128 == 0

        def act_view(k, t0):
            if act_resident:
                base = self.XA if k < 4 else self.XB
                return base[:, (k % 4) * T + t0:(k % 4) * T + t0 + t_block]
            return self.XA[:, k * t_block:(k + 1) * t_block]

        if act_resident:
            for k in range(nK):
                base = self.XA if k < 4 else self.XB
                eng = "gpsimd" if act_is_f32 else "sync"
                P.do(eng, lambda e, k=k, base=base: e.dma_start(
                    out=base[:, (k % 4) * T:(k % 4) * T + T],
                    in_=act_hbm[k * 128:(k + 1) * 128, :]),
                    inc=tag + "_ald", inc_n=16)

        n_slabs = cdiv(Dout, o_slab)
        assert nK * o_slab <= 8192
        mm_base = P.cnt(tag + "_mm")
        slab_groups = []
        for si in range(n_slabs):
            o0 = si * o_slab
            ow = min(o_slab, Dout - o0)
            h = (si % 2) * 8192
            if si >= 2:
                done_upto = mm_base + sum(slab_groups[:si - 1])
                wld_wait = (tag + "_mm", done_upto)
            else:
                wld_wait = (tag + "_mm", 0)
            weng = "sync" if pre_transposed else "gpsimd"
            for k in range(nK):
                P.do(weng, lambda e, k=k, o0=o0, ow=ow, h=h: e.dma_start(
                    out=self.WA[:, h + k * o_slab:h + k * o_slab + ow],
                    in_=w_src[k * 128:(k + 1) * 128, o0:o0 + ow]),
                    inc=tag + "_wld", inc_n=16,
                    waits=(wld_wait,) if k == 0 else ())
            slab_groups.append((ow // 128) * (T // t_block))
            for tb in range(T // t_block):
                t0 = tb * t_block
                if not act_resident:
                    for k in range(nK):
                        eng = "gpsimd" if act_is_f32 else "sync"
                        P.do(eng, lambda e, k=k, t0=t0: e.dma_start(
                            out=self.XA[:, k * t_block:(k + 1) * t_block],
                            in_=act_hbm[k * 128:(k + 1) * 128,
                                        t0:t0 + t_block]),
                            inc=tag + "_ald", inc_n=16,
                            waits=((tag + "_mm", P.cnt(tag + "_mm")),)
                            if k == 0 else ())
                for oc in range(ow // 128):
                    gi = P.cnt(tag + "_mm")
                    bank = self.PS[gi % 4]
                    P.wait("tensor",
                           (tag + "_evd", max(0, P.cnt(tag + "_evd") - 3)),
                           (tag + "_wld", P.cnt(tag + "_wld")),
                           (tag + "_ald", P.cnt(tag + "_ald")))
                    for k in range(nK):
                        P.do("tensor", lambda e, k=k, oc=oc, h=h,
                             bank=bank, t0=t0: e.matmul(
                                 bank[:, 0:t_block],
                                 self.WA[:, h + k * o_slab + oc * 128:
                                         h + k * o_slab + (oc + 1) * 128],
                                 act_view(k, t0),
                                 start=(k == 0), stop=(k == nK - 1)),
                             inc=(tag + "_mm") if k == nK - 1 else None)
                    oc_abs = (o0 + oc * 128) // 128
                    oh = (gi % 2) * 512
                    out_view = (self.FB[:, 3072 + oh:3072 + oh + t_block]
                                if out_f32 else self.OUTS[:, oh:oh + t_block])
                    prewaits = ((tag + "_mm",),
                                (tag + "_st", max(0, P.cnt(tag + "_st") - 16)))
                    prewaits = ((tag + "_mm", P.cnt(tag + "_mm")),
                                (tag + "_st", max(0, P.cnt(tag + "_st") - 16)))
                    if epilogue is None:
                        P.do("scalar", lambda e, bank=bank, out_view=out_view:
                             e.activation(out_view, bank[:, 0:t_block],
                                          AF.Copy),
                             waits=prewaits, inc=tag + "_evd")
                    else:
                        epilogue(self, tag, bank, oc_abs, 128,
                                 (t0, t0 + t_block), out_view, prewaits)
                    P.do("sync", lambda e, oc_abs=oc_abs, t0=t0,
                         out_view=out_view: e.dma_start(
                             out=dst_hbm[oc_abs * 128:(oc_abs + 1) * 128,
                                         t0:t0 + t_block],
                             in_=out_view),
                         waits=(tag + "_evd",), inc=tag + "_st", inc_n=16)
        if stage_barrier:
            P.barrier()

    def ep_gelu_tanh(self):
        def ep(b, tag, bank, oc_abs, ocw, tcols, out_view, prewaits):
            P = b.P
            tw = tcols[1] - tcols[0]
            gi = P.cnt(tag + "_evd")
            r1 = b.FB[:, (gi % 2) * 512:(gi % 2) * 512 + tw]
            r2 = b.FB[:, 1024 + (gi % 2) * 512:1024 + (gi % 2) * 512 + tw]
            fbw = (tag + "_ep7", max(0, P.cnt(tag + "_ep7") - 1))
            P.do("scalar", lambda e: e.activation(r1, bank[:, 0:tw], AF.Copy),
                 waits=prewaits + (fbw,), inc=tag + "_ep1")
            P.do("scalar", lambda e: e.activation(r2, bank[:, 0:tw], AF.Square),
                 waits=((tag + "_ep1", P.cnt(tag + "_ep1")), fbw),
                 inc=tag + "_ep2")
            P.do("vector", lambda e: e.tensor_scalar(
                r2, r2, 0.044715, 1.0, ALU.mult, ALU.add),
                waits=((tag + "_ep2", P.cnt(tag + "_ep2")),),
                inc=tag + "_ep3")
            P.do("vector", lambda e: e.tensor_tensor(r2, r2, r1, ALU.mult),
                waits=((tag + "_ep3", P.cnt(tag + "_ep3")),),
                inc=tag + "_ep4")
            P.do("scalar", lambda e: e.activation(
                r2, r2, AF.Tanh, scale=0.7978845608028654),
                waits=((tag + "_ep4", P.cnt(tag + "_ep4")),),
                inc=tag + "_ep5")
            P.do("vector", lambda e: e.tensor_scalar(
                r2, r2, 0.5, 0.5, ALU.mult, ALU.add),
                waits=((tag + "_ep5", P.cnt(tag + "_ep5")),),
                inc=tag + "_ep6")
            P.do("vector", lambda e: e.tensor_tensor(
                out_view, r1, r2, ALU.mult),
                waits=((tag + "_ep6", P.cnt(tag + "_ep6")),),
                inc=tag + "_ep7")
            P.do("vector", lambda e: e.nop(),
                 waits=((tag + "_ep7", P.cnt(tag + "_ep7")),),
                 inc=tag + "_evd")
        return ep

    def emit_l1_ffn(self):
        h_fm = self.scratch("l1f_h", (8192, T), BF16)
        y_fm = self.scratch("l1f_y", (1024, T), BF16)
        self.emit_proj("l1f1", self.xn_fm, 1024, 8192,
                       self.IN["L1_ffn_w1"], h_fm,
                       pre_transposed=False, act_is_f32=True,
                       epilogue=self.ep_gelu_tanh())
        self.emit_proj("l1f2", h_fm, 8192, 1024,
                       self.IN["L1_ffn_w2"], y_fm,
                       pre_transposed=False, act_is_f32=False,
                       o_slab=128, t_block=256)
        P = self.P
        ew = self.IN["L1_ffn_expert_weights"]
        P.do("sync", lambda e: e.dma_start(
            out=self.FA[0:1, 0:8192],
            in_=ew.rearrange("t k -> (t k)")),
            inc="l1c_ld", inc_n=16)
        P.do("vector", lambda e: e.tensor_tensor(
            self.COEFR[0:1, 0:T],
            self.FA[0:1, 0:8192].rearrange("p (t k) -> p t k", k=2)[:, :, 0],
            self.FA[0:1, 0:8192].rearrange("p (t k) -> p t k", k=2)[:, :, 1],
            ALU.add),
            waits=("l1c_ld",), inc="l1c_v")
        cdbg = self.scratch("l1c_coef", (1, T))
        P.do("sync", lambda e: e.dma_start(out=cdbg[0:1, :], in_=self.COEFR[0:1, 0:T]),
             waits=("l1c_v",), inc="l1c_cd", inc_n=16)
        for tb in range(T // 512):
            t0 = tb * 512
            P.do("tensor", lambda e, t0=t0: e.matmul(
                self.PS[4][:, 0:512], self.ONESROWF[0:1, :],
                self.COEFR[0:1, t0:t0 + 512], start=True, stop=True),
                waits=("l1c_v", ("l1c_mu", P.cnt("l1c_mu"))),
                inc="l1c_bc")
            for k in range(8):
                P.do("gpsimd", lambda e, k=k, t0=t0: e.dma_start(
                    out=self.FA[:, k * 512:(k + 1) * 512],
                    in_=y_fm[k * 128:(k + 1) * 128, t0:t0 + 512]),
                    inc="l1c_yld", inc_n=16,
                    waits=(("l1c_mu", P.cnt("l1c_mu")), ("l1c_v", P.cnt("l1c_v")))
                    if k == 0 else ())
            P.wait("vector", "l1c_yld", "l1c_bc")
            for k in range(8):
                P.do("vector", lambda e, k=k: e.tensor_tensor(
                    self.FB[:, (k % 4) * 512:(k % 4) * 512 + 512],
                    self.FA[:, k * 512:(k + 1) * 512],
                    self.PS[4][:, 0:512], ALU.mult),
                    waits=(("l1c_si", P.cnt("l1c_si")),
                           ("l1c_st", P.cnt("l1c_st")),
                           ("l1c_mu", P.cnt("l1c_mu"))),
                    inc="l1c_mu")
                P.do("scalar", lambda e, k=k: e.activation(
                    self.OUTS[:, (k % 2) * 512:(k % 2) * 512 + 512],
                    self.FB[:, (k % 4) * 512:(k % 4) * 512 + 512], AF.Silu),
                    waits=(("l1c_mu", P.cnt("l1c_mu")),
                           ("l1c_st", P.cnt("l1c_st")),
                           ("l1c_si", P.cnt("l1c_si"))),
                    inc="l1c_si")
                P.do("sync", lambda e, k=k, t0=t0: e.dma_start(
                    out=self.g_fm[k * 128:(k + 1) * 128, t0:t0 + 512],
                    in_=self.OUTS[:, (k % 2) * 512:(k % 2) * 512 + 512]),
                    waits=(("l1c_si", P.cnt("l1c_si")),),
                    inc="l1c_st", inc_n=16)
        P.barrier()

    def ep_bias_scale(self, bias_col0=None, scale=1.0, func=None):
        func = func or (AF.Identity if bias_col0 is not None else AF.Copy)
        def ep(b, tag, bank, oc_abs, ocw, tcols, out_view, prewaits):
            P = b.P
            tw = tcols[1] - tcols[0]
            bias = 0.0 if bias_col0 is None else \
                b.BIAS[:, bias_col0 + oc_abs:bias_col0 + oc_abs + 1]
            P.do("scalar", lambda e: e.activation(
                out_view, bank[:, 0:tw], func, bias=bias, scale=scale),
                waits=prewaits, inc=tag + "_evd")
        return ep

    def emit_attention(self, tag, qkv_tw, qkv_bias, qkv_scale,
                       o_tw, o_bias, o_scale, exp_scale, p_scale, causal,
                       dst_fm):
        P = self.P
        qkv_fm = self.scratch(tag + "_qkv", (3072, T), BF16)
        o_fm = self.scratch(tag + "_o", (1024, T), BF16)
        bias_col0 = None
        if qkv_bias is not None:
            bias_col0 = 0
            self.load_bias(tag, qkv_bias, 3072, 0)
            P.wait("scalar", tag + "_bld")
        self.emit_proj(tag + "p", self.xn_fm, 1024, 3072, qkv_tw, qkv_fm,
                       pre_transposed=True, act_is_f32=True,
                       epilogue=self.ep_bias_scale(bias_col0, qkv_scale))
        NH = 16
        QA = lambda c0, c1: self.XA[0:65, c0:c1]
        KA = lambda c0, c1: self.XA[0:65, 4096 + c0:4096 + c1]
        VTM = 8192
        VST = 10288
        P.do("vector", lambda e: e.memset(self.XA[64:65, 0:8192], 1.0),
             inc=tag + "_ini")

        P.do("vector", lambda e: e.memset(self.XA[:, VTM:VTM + 2080], 1.0),
             inc=tag + "_ini")
        P.barrier()
        for h in range(NH):
            P.do("sync", lambda e, h=h: e.dma_start(
                out=self.XA[0:64, 0:4096],
                in_=qkv_fm[h * 64:(h + 1) * 64, :]),
                inc=tag + "_hld", inc_n=16)
            P.do("sync", lambda e, h=h: e.dma_start(
                out=self.XA[0:64, 4096:8192],
                in_=qkv_fm[1024 + h * 64:1024 + (h + 1) * 64, :]),
                inc=tag + "_hld", inc_n=16)
            P.do("sync", lambda e, h=h: e.dma_start(
                out=self.XA[0:64, VST:VST + 4096],
                in_=qkv_fm[2048 + h * 64:2048 + (h + 1) * 64, :]),
                inc=tag + "_hld", inc_n=16)
            P.wait("tensor", tag + "_hld")
            for g in range(8):
                P.wait("tensor", (tag + "_vev", P.cnt(tag + "_vev")))
                for j in range(4):
                    kt = g * 4 + j
                    P.do("tensor", lambda e, kt=kt, j=j: e.matmul(
                        self.PS[4][:, j * 64:(j + 1) * 64],
                        self.XA[0:64, VST + kt * 128:VST + (kt + 1) * 128],
                        self.IDENTB[0:64, 0:64], start=True, stop=True),
                        inc=(tag + "_vmm") if j == 3 else None)
                vtv = self.XA[:, VTM + g * 260:VTM + g * 260 + 260]
                vtv = vtv.rearrange("p (t c) -> p t c", c=65)[:, :, 0:64]
                P.do("vector", lambda e, vtv=vtv: e.tensor_copy(
                    vtv, self.PS[4][:, 0:256].rearrange(
                        "p (t c) -> p t c", c=64)),
                    waits=(tag + "_vmm",), inc=tag + "_vev")
            P.do("scalar", lambda e: e.activation(
                self.FA[0:64, 0:4096], self.XA[0:64, 0:4096], AF.Square),
                waits=(tag + "_hld",), inc=tag + "_sq")
            P.do("scalar", lambda e: e.activation(
                self.FA[0:64, 4096:8192], self.XA[0:64, 4096:8192], AF.Square),
                waits=(tag + "_sq",), inc=tag + "_sq")
            ones64 = self.const(1.0)[0:64, :]
            for c in range(8):
                P.do("tensor", lambda e, c=c: e.matmul(
                    self.PS[5][0:1, 0:512], ones64,
                    self.FA[0:64, c * 512:(c + 1) * 512],
                    start=True, stop=True),
                    waits=((tag + "_sq", P.cnt(tag + "_sq") - 1),
                           (tag + "_qsv", P.cnt(tag + "_qsv"))),
                    inc=tag + "_qsm")
                P.do("vector", lambda e, c=c: e.tensor_copy(
                    self.FB[0:1, c * 512:(c + 1) * 512],
                    self.PS[5][0:1, 0:512]),
                    waits=((tag + "_qsm", P.cnt(tag + "_qsm")),),
                    inc=tag + "_qsv")
                P.do("tensor", lambda e, c=c: e.matmul(
                    self.PS[5][0:1, 0:512], ones64,
                    self.FA[0:64, 4096 + c * 512:4096 + (c + 1) * 512],
                    start=True, stop=True),
                    waits=((tag + "_sq", P.cnt(tag + "_sq")),
                           (tag + "_qsv", P.cnt(tag + "_qsv"))),
                    inc=tag + "_qsm")
                P.do("vector", lambda e, c=c: e.tensor_copy(
                    self.FB[32:33, c * 512:(c + 1) * 512],
                    self.PS[5][0:1, 0:512]),
                    waits=((tag + "_qsm", P.cnt(tag + "_qsm")),),
                    inc=tag + "_qsv")
            P.do("vector", lambda e: e.reduce_max(
                self.ROWS[0:1, 0:1], self.FB[32:33, 0:4096], axis=AX.X),
                waits=(tag + "_qsv",), inc=tag + "_bnd")
            P.do("scalar", lambda e: e.activation(
                self.FA[0:1, 0:4096], self.FB[0:1, 0:4096], AF.Sqrt,
                scale=self.ROWS[0:1, 0:1]),
                waits=(tag + "_bnd", tag + "_qsv"), inc=tag + "_bnd2")
            P.do("vector", lambda e: e.tensor_scalar(
                self.XA[64:65, 0:4096], self.FA[0:1, 0:4096],
                -1.0, None, ALU.mult),
                waits=(tag + "_bnd2",), inc=tag + "_bnd3")
            P.wait("tensor", tag + "_bnd3", tag + "_vev", tag + "_ini")
            for qb in range(8):
                q0 = qb * 512
                ntiles = 4 * (qb + 1) if causal else 32
                for kt in range(ntiles):
                    diag = causal and (kt >= 4 * qb)
                    bank = self.PS[P.cnt(tag + "_smm") % 2]
                    P.wait("tensor",
                           (tag + "_xp", max(0, P.cnt(tag + "_xp") - 1)))
                    P.do("tensor", lambda e, kt=kt, q0=q0, bank=bank:
                         e.matmul(bank[:, 0:512],
                                  KA(kt * 128, (kt + 1) * 128),
                                  QA(q0, q0 + 512),
                                  start=True, stop=True),
                         inc=tag + "_smm")
                    if not diag:
                        P.do("scalar", lambda e, kt=kt, bank=bank:
                             e.activation(
                                 self.XB[:, kt * 512:(kt + 1) * 512],
                                 bank[:, 0:512], AF.Exp, scale=exp_scale),
                             waits=((tag + "_smm", P.cnt(tag + "_smm")),),
                             inc=tag + "_xp")
                    else:
                        P.do("scalar", lambda e, bank=bank: e.activation(
                            self.FA[:, 4096:4608], bank[:, 0:512], AF.Copy),
                            waits=((tag + "_smm", P.cnt(tag + "_smm")),
                                   (tag + "_msk",
                                    max(0, P.cnt(tag + "_msk") - 1))),
                            inc=tag + "_dcp")
                        P.do("gpsimd", lambda e, kt=kt, q0=q0:
                             e.affine_select(
                                 self.FA[:, 4608:5120], self.FA[:, 4096:4608],
                                 [[1, 512]], ALU.is_ge, self._fill_reg,
                                 base=q0 - kt * 128, channel_multiplier=-1),
                             waits=((tag + "_dcp", P.cnt(tag + "_dcp")),
                                    (tag + "_xp", P.cnt(tag + "_xp"))),
                             inc=tag + "_msk")
                        P.do("scalar", lambda e, kt=kt: e.activation(
                            self.XB[:, kt * 512:(kt + 1) * 512],
                            self.FA[:, 4608:5120], AF.Exp, scale=exp_scale),
                            waits=((tag + "_msk", P.cnt(tag + "_msk")),),
                            inc=tag + "_xp")
                P.wait("tensor", tag + "_xp",
                       (tag + "_oev", P.cnt(tag + "_oev")),
                       (tag + "_rc", P.cnt(tag + "_rc")))
                for kt in range(ntiles):
                    P.do("tensor", lambda e, kt=kt: e.matmul(
                        self.PS[2][0:65, 0:512],
                        self.XA[:, VTM + kt * 65:VTM + (kt + 1) * 65],
                        self.XB[:, kt * 512:(kt + 1) * 512],
                        start=(kt == 0), stop=(kt == ntiles - 1)),
                        inc=(tag + "_avm") if kt == ntiles - 1 else None)
                P.do("vector", lambda e: e.reciprocal(
                    self.FB[0:1, 0:512], self.PS[2][64:65, 0:512]),
                    waits=((tag + "_avm", P.cnt(tag + "_avm")),),
                    inc=tag + "_rc")
                if p_scale != 1.0:
                    P.do("vector", lambda e: e.tensor_scalar(
                        self.FB[0:1, 0:512], self.FB[0:1, 0:512],
                        p_scale, None, ALU.mult),
                        waits=((tag + "_rc", P.cnt(tag + "_rc")),),
                        inc=tag + "_rc")
                P.do("tensor", lambda e: e.matmul(
                    self.PS[3][0:64, 0:512], self.ONESROWF[0:1, 0:64],
                    self.FB[0:1, 0:512], start=True, stop=True),
                    waits=((tag + "_rc", P.cnt(tag + "_rc")),
                           (tag + "_onrm", P.cnt(tag + "_onrm"))),
                    inc=tag + "_rbc")
                P.do("scalar", lambda e: e.activation(
                    self.FA[0:64, 5120:5632], self.PS[2][0:64, 0:512],
                    AF.Copy),
                    waits=((tag + "_avm", P.cnt(tag + "_avm")),),
                    inc=tag + "_oev")
                P.do("vector", lambda e: e.tensor_tensor(
                    self.OUTS[0:64, 0:512], self.FA[0:64, 5120:5632],
                    self.PS[3][0:64, 0:512], ALU.mult),
                    waits=((tag + "_oev", P.cnt(tag + "_oev")),
                           (tag + "_rbc", P.cnt(tag + "_rbc")),
                           (tag + "_ost", P.cnt(tag + "_ost"))),
                    inc=tag + "_onrm")
                P.do("sync", lambda e, h=h, q0=q0: e.dma_start(
                    out=o_fm[h * 64:(h + 1) * 64, q0:q0 + 512],
                    in_=self.OUTS[0:64, 0:512]),
                    waits=((tag + "_onrm", P.cnt(tag + "_onrm")),),
                    inc=tag + "_ost", inc_n=16)
            P.barrier()
        bias2 = None
        if o_bias is not None:
            bias2 = 32
            self.load_bias(tag + "o", o_bias, 1024, 32)
            P.wait("scalar", tag + "o_bld")
        self.emit_proj(tag + "o", o_fm, 1024, 1024, o_tw, dst_fm,
                       pre_transposed=True, act_is_f32=False,
                       epilogue=self.ep_bias_scale(bias2, o_scale))

    def emit_rms_fm(self, tag, src_fm, dst_fm, Drows, gamma_hbm, mode,
                    eps=1e-6, outscale=1.0, src_is_f32=True):
        P = self.P
        nK = Drows // 128
        tbw = min(512, 4096 // nK)
        ones_col = self.const(1.0)
        gcol0 = None
        if gamma_hbm is not None:
            gcol0 = 40
            for oc in range(nK):
                P.do("gpsimd", lambda e, oc=oc: e.dma_start(
                    out=self.BIAS[:, gcol0 + oc:gcol0 + oc + 1],
                    in_=gamma_hbm[oc * 128:(oc + 1) * 128]),
                    inc=tag + "_gld", inc_n=16)
            P.wait("vector", tag + "_gld")
        for tb in range(T // tbw):
            t0 = tb * tbw
            for k in range(nK):
                eng = "gpsimd" if not src_is_f32 else "sync"
                P.do(eng, lambda e, k=k, t0=t0: e.dma_start(
                    out=self.FA[:, k * tbw:(k + 1) * tbw],
                    in_=src_fm[k * 128:(k + 1) * 128, t0:t0 + tbw]),
                    inc=tag + "_ld", inc_n=16,
                    waits=((tag + "_ap1", P.cnt(tag + "_ap1")),))
            P.wait("scalar", tag + "_ld", (tag + "_ap", P.cnt(tag + "_ap")))
            for k in range(nK):
                P.do("scalar", lambda e, k=k: e.activation(
                    self.FB[:, k * tbw:(k + 1) * tbw],
                    self.FA[:, k * tbw:(k + 1) * tbw], AF.Square),
                    inc=(tag + "_sq") if k == nK - 1 else None)
            P.wait("tensor", tag + "_sq",
                   (tag + "_sv", P.cnt(tag + "_sv")))
            for k in range(nK):
                P.do("tensor", lambda e, k=k: e.matmul(
                    self.PS[6][0:1, 0:tbw], ones_col,
                    self.FB[:, k * tbw:(k + 1) * tbw],
                    start=(k == 0), stop=(k == nK - 1)),
                    inc=(tag + "_mm") if k == nK - 1 else None)
            S_ = lambda i: self.FC[0:1, i * tbw:(i + 1) * tbw]
            if mode == "rms":
                P.do("vector", lambda e: e.tensor_scalar(
                    S_(0), self.PS[6][0:1, 0:tbw], 1.0 / Drows, None,
                    ALU.mult), waits=(tag + "_mm",), inc=tag + "_sv")
                P.do("scalar", lambda e: e.activation(
                    S_(1), S_(0), AF.Ln, bias=self.const(eps)[0:1, :]),
                    waits=(tag + "_sv",), inc=tag + "_sc")
                P.do("scalar", lambda e, outscale=outscale: e.activation(
                    S_(0), S_(1), AF.Exp, scale=-0.5),
                    waits=(tag + "_sc",), inc=tag + "_sc")
                if outscale != 1.0:
                    P.do("vector", lambda e: e.tensor_scalar(
                        S_(0), S_(0), outscale, None, ALU.mult),
                        waits=(tag + "_sc",), inc=tag + "_sv")
                    P.wait("tensor", tag + "_sv")
                else:
                    P.wait("tensor", tag + "_sc")
            else:
                P.do("scalar", lambda e: e.activation(
                    S_(0), self.PS[6][0:1, 0:tbw], AF.Sqrt),
                    waits=(tag + "_mm",), inc=tag + "_sc")
                P.do("vector", lambda e: e.tensor_scalar(
                    S_(1), S_(0), eps, None, ALU.max),
                    waits=(tag + "_sc",), inc=tag + "_sv")
                P.do("vector", lambda e: e.reciprocal(S_(0), S_(1)),
                    waits=(tag + "_sv",), inc=tag + "_sv")
                if outscale != 1.0:
                    P.do("vector", lambda e: e.tensor_scalar(
                        S_(0), S_(0), outscale, None, ALU.mult),
                        waits=(tag + "_sv",), inc=tag + "_sv")
                P.wait("tensor", tag + "_sv")
            P.do("tensor", lambda e: e.matmul(
                self.PS[7][:, 0:tbw], self.ONESROWF[0:1, :],
                S_(0), start=True, stop=True),
                waits=((tag + "_ap", P.cnt(tag + "_ap")),),
                inc=tag + "_bc")
            P.wait("vector", tag + "_bc")
            for k in range(nK):
                P.do("vector", lambda e, k=k: e.tensor_tensor(
                    self.FB[:, k * tbw:(k + 1) * tbw],
                    self.FA[:, k * tbw:(k + 1) * tbw],
                    self.PS[7][:, 0:tbw], ALU.mult),
                    inc=tag + "_ap1")
                if gcol0 is not None:
                    P.do("vector", lambda e, k=k: e.tensor_scalar(
                        self.OUTS[:, (k % 2) * 512:(k % 2) * 512 + tbw],
                        self.FB[:, k * tbw:(k + 1) * tbw],
                        self.BIAS[:, gcol0 + k:gcol0 + k + 1], None,
                        ALU.mult),
                        waits=((tag + "_ap1", P.cnt(tag + "_ap1")),
                               (tag + "_st", P.cnt(tag + "_st"))),
                        inc=tag + "_ap")
                else:
                    P.do("vector", lambda e, k=k: e.tensor_copy(
                        self.OUTS[:, (k % 2) * 512:(k % 2) * 512 + tbw],
                        self.FB[:, k * tbw:(k + 1) * tbw]),
                        waits=((tag + "_ap1", P.cnt(tag + "_ap1")),
                               (tag + "_st", P.cnt(tag + "_st"))),
                        inc=tag + "_ap")
                P.do("sync", lambda e, k=k, t0=t0: e.dma_start(
                    out=dst_fm[k * 128:(k + 1) * 128, t0:t0 + tbw],
                    in_=self.OUTS[:, (k % 2) * 512:(k % 2) * 512 + tbw]),
                    waits=((tag + "_ap", P.cnt(tag + "_ap")),),
                    inc=tag + "_st", inc_n=16)
        P.barrier()

    def emit_fm_to_tm(self, tag, src_fm, dst_tm, D_, dtype_f32=False):
        P = self.P
        nK = D_ // 128
        ident = self.IDENT[:, :] if dtype_f32 else self.IDENTB[:, :]
        XS = self.XA if not dtype_f32 else None
        for tb in range(T // 512):
            t0 = tb * 512
            for k in range(nK):
                eng = "sync"
                buf = self.FA if dtype_f32 else self.XA
                P.do(eng, lambda e, k=k, t0=t0, buf=buf: e.dma_start(
                    out=buf[:, k * 512:(k + 1) * 512],
                    in_=src_fm[k * 128:(k + 1) * 128, t0:t0 + 512]),
                    inc=tag + "_ld", inc_n=16,
                    waits=((tag + "_mm", P.cnt(tag + "_mm")),)
                    if k == 0 else ())
            P.wait("tensor", tag + "_ld")
            for j in range(4):
                for half in range(max(1, nK // 4)):
                    bank = self.PS[half % 2]
                    P.wait("tensor", (tag + "_ev", P.cnt(tag + "_ev")))
                    nsub = min(4, nK - half * 4)
                    for kk in range(nsub):
                        k = half * 4 + kk
                        buf = self.FA if dtype_f32 else self.XA
                        P.do("tensor", lambda e, k=k, kk=kk, j=j, bank=bank,
                             buf=buf, ident=ident: e.matmul(
                                 bank[:, kk * 128:(kk + 1) * 128],
                                 buf[:, k * 512 + j * 128:
                                     k * 512 + (j + 1) * 128],
                                 ident, start=True, stop=True),
                             inc=(tag + "_mm") if kk == nsub - 1 else None)
                    out_sb = self.FB if dtype_f32 else self.OUTS
                    P.do("vector", lambda e, half=half, bank=bank, nsub=nsub,
                         out_sb=out_sb: e.tensor_copy(
                             out_sb[:, half * 512:half * 512 + nsub * 128],
                             bank[:, 0:nsub * 128]),
                        waits=((tag + "_mm", P.cnt(tag + "_mm")),),
                        inc=tag + "_ev")
                out_sb = self.FB if dtype_f32 else self.OUTS
                P.do("sync", lambda e, j=j, t0=t0, out_sb=out_sb:
                     e.dma_start(
                         out=dst_tm[t0 + j * 128:t0 + (j + 1) * 128, :],
                         in_=out_sb[:, 0:D_]),
                     waits=((tag + "_ev", P.cnt(tag + "_ev")),),
                     inc=tag + "_stt", inc_n=16)
                P.wait("vector", (tag + "_stt", P.cnt(tag + "_stt")))
        P.barrier()

    def emit_proj_tm(self, tag, act_hbm, Din, Dout, w_src, dst_tm,
                     pre_transposed=False, act_is_f32=False, epilogue=None,
                     out_f32=False):
        P = self.P
        nK = Din // 128
        assert nK * T <= 32768
        for k in range(nK):
            base = self.XA if k < 4 else self.XB
            eng = "gpsimd" if act_is_f32 else "sync"
            P.do(eng, lambda e, k=k, base=base: e.dma_start(
                out=base[:, (k % 4) * T:(k % 4) * T + T],
                in_=act_hbm[k * 128:(k + 1) * 128, :]),
                inc=tag + "_ald", inc_n=16)
        OUTB = self.FB if out_f32 else self.OUTS

        def act_view(k, t0):
            base = self.XA if k < 4 else self.XB
            return base[:, (k % 4) * T + t0:(k % 4) * T + t0 + 128]

        mm_base = P.cnt(tag + "_mm")
        for ob in range(cdiv(Dout, 512)):
            o0 = ob * 512
            ow = min(512, Dout - o0)
            h = (ob % 2) * 8192
            if ob >= 2:
                wld_wait = (tag + "_mm", mm_base + (ob - 1) * (T // 128))
            else:
                wld_wait = (tag + "_mm", 0)
            weng = "sync" if pre_transposed else "gpsimd"
            for k in range(nK):
                P.do(weng, lambda e, k=k, o0=o0, ow=ow, h=h: e.dma_start(
                    out=self.WA[:, h + k * 512:h + k * 512 + ow],
                    in_=w_src[k * 128:(k + 1) * 128, o0:o0 + ow]),
                    inc=tag + "_wld", inc_n=16,
                    waits=(wld_wait,) if k == 0 else ())
            for tt in range(T // 128):
                gi = P.cnt(tag + "_mm")
                bank = self.PS[gi % 4]
                P.wait("tensor",
                       (tag + "_evd", max(0, P.cnt(tag + "_evd") - 3)),
                       (tag + "_wld", P.cnt(tag + "_wld")),
                       (tag + "_ald", P.cnt(tag + "_ald")))
                for k in range(nK):
                    P.do("tensor", lambda e, k=k, h=h, bank=bank, tt=tt,
                         ow=ow: e.matmul(
                             bank[:, 0:ow], act_view(k, tt * 128),
                             self.WA[:, h + k * 512:h + k * 512 + ow],
                             start=(k == 0), stop=(k == nK - 1)),
                         inc=(tag + "_mm") if k == nK - 1 else None)
                oh = (gi % 2) * 512
                out_view = OUTB[:, oh:oh + ow]
                prewaits = ((tag + "_mm", P.cnt(tag + "_mm")),
                            (tag + "_st", max(0, P.cnt(tag + "_st") - 16)))
                if epilogue is None:
                    P.do("scalar", lambda e, bank=bank, out_view=out_view,
                         ow=ow: e.activation(out_view, bank[:, 0:ow],
                                             AF.Copy),
                         waits=prewaits, inc=tag + "_evd")
                else:
                    epilogue(self, tag, bank, ob, ow, (tt * 128,),
                             out_view, prewaits)
                P.do("sync", lambda e, tt=tt, o0=o0, ow=ow,
                     out_view=out_view: e.dma_start(
                         out=dst_tm[tt * 128:(tt + 1) * 128, o0:o0 + ow],
                         in_=out_view),
                     waits=(tag + "_evd",), inc=tag + "_st", inc_n=16)
        P.barrier()

    def emit_fp4_tile(self, tag, x_view, out_view, W_, ab, v, prewaits=()):
        P = self.P
        nb = W_ // 16
        AB = ab
        V = v
        ACC = ab
        S2 = self.ROWS[:, 0:nb]
        R = self.ROWS[:, 256:256 + nb]
        t = tag
        P.do("scalar", lambda e: e.activation(AB, x_view, AF.Abs),
             waits=prewaits + ((t + "_q", P.cnt(t + "_q")),), inc=t + "_qs")
        P.do("vector", lambda e: e.reduce_max(
            S2, AB.rearrange("p (b w) -> p b w", w=16), axis=AX.X),
            waits=((t + "_qs", P.cnt(t + "_qs")),), inc=t + "_q")
        P.do("vector", lambda e: e.tensor_scalar(
            S2, S2, 1.0 / 12.0, 0.5e-12, ALU.mult, ALU.max),
            waits=((t + "_q", P.cnt(t + "_q")),), inc=t + "_q")
        P.do("vector", lambda e: e.reciprocal(R, S2),
             waits=((t + "_q", P.cnt(t + "_q")),), inc=t + "_q")
        P.do("vector", lambda e: e.tensor_tensor(
            V.rearrange("p (b w) -> p b w", w=16),
            AB.rearrange("p (b w) -> p b w", w=16),
            R.broadcast_to([128, nb, 16]), ALU.mult),
            waits=((t + "_q", P.cnt(t + "_q")),), inc=t + "_q")
        thr = [0.5, 1.5, 2.5, 3.5, 5.0, 5.0, 7.0, 7.0,
               10.0, 10.0, 10.0, 10.0]
        P.do("vector", lambda e: e.tensor_single_scalar(
            ACC, V, thr[0], ALU.is_gt),
            waits=((t + "_q", P.cnt(t + "_q")),), inc=t + "_q")
        for m in thr[1:]:
            P.do("vector", lambda e, m=m: e.scalar_tensor_tensor(
                ACC, V, m, ACC, ALU.is_gt, ALU.add),
                waits=((t + "_q", P.cnt(t + "_q")),), inc=t + "_q")
        P.do("scalar", lambda e: e.activation(V, x_view, AF.Sign),
             waits=((t + "_q", P.cnt(t + "_q")),), inc=t + "_qs")
        P.do("vector", lambda e: e.tensor_tensor(ACC, ACC, V, ALU.mult),
             waits=((t + "_qs", P.cnt(t + "_qs")),), inc=t + "_q")
        P.do("vector", lambda e: e.tensor_tensor(
            out_view.rearrange("p (b w) -> p b w", w=16),
            ACC.rearrange("p (b w) -> p b w", w=16),
            S2.broadcast_to([128, nb, 16]), ALU.mult),
            waits=((t + "_q", P.cnt(t + "_q")),), inc=t + "_q")

    def emit_l0_ffn(self):
        P = self.P
        xq_fm = self.scratch("l0f_xq", (1024, T), BF16)
        hq_fm = self.scratch("l0f_hq", (2816, T), BF16)
        gate_tm = self.scratch("l0f_gate", (T, 2816))
        up_tm = self.scratch("l0f_up", (T, 2816))
        natq = self.scratch("l0f_natq", (2816, 2816), BF16)
        twq = {}
        for nm in ("gate", "up", "down"):
            I = 2816 if nm == "down" else 1024
            O = 1024 if nm == "down" else 2816
            twq[nm] = self.scratch("twq_" + nm, (I, O), BF16)
        for tb in range(T // 512):
            t0 = tb * 512
            for k in range(8):
                P.do("sync", lambda e, k=k, t0=t0: e.dma_start(
                    out=self.FA[:, k * 512:(k + 1) * 512],
                    in_=self.xn_fm[k * 128:(k + 1) * 128, t0:t0 + 512]),
                    inc="l0x_ld", inc_n=16,
                    waits=(("l0x_mm2", P.cnt("l0x_mm2")),) if k == 0 else ())
            P.wait("tensor", "l0x_ld")
            for j in range(4):
                for half in range(2):
                    bank = self.PS[half]
                    P.wait("tensor", ("l0x_ev", P.cnt("l0x_ev")))
                    for kk in range(4):
                        k = half * 4 + kk
                        P.do("tensor", lambda e, k=k, kk=kk, j=j, bank=bank:
                             e.matmul(bank[:, kk * 128:(kk + 1) * 128],
                                      self.FA[:, k * 512 + j * 128:
                                              k * 512 + (j + 1) * 128],
                                      self.IDENT[:, :], start=True, stop=True),
                             inc="l0x_mm" if kk == 3 else None)
                    P.do("vector", lambda e, half=half, bank=bank:
                         e.tensor_copy(
                             self.FB[:, half * 512:(half + 1) * 512],
                             bank[:, :]),
                         waits=(("l0x_mm", P.cnt("l0x_mm")),),
                         inc="l0x_ev")
                P.wait("vector", ("l0x_ev", P.cnt("l0x_ev")))
                self.emit_fp4_tile("l0x", self.FB[:, 0:1024],
                                   self.XA[:, 0:1024], 1024,
                                   self.FB[:, 1024:2048],
                                   self.FB[:, 2048:3072],
                                   prewaits=(("l0x_ev", P.cnt("l0x_ev")),))
                for half in range(2):
                    bank = self.PS[2 + half]
                    P.wait("tensor", ("l0x_ev2", P.cnt("l0x_ev2")),
                           ("l0x_q", P.cnt("l0x_q")))
                    for kk in range(4):
                        k = half * 4 + kk
                        P.do("tensor", lambda e, k=k, kk=kk, bank=bank:
                             e.matmul(bank[:, kk * 128:(kk + 1) * 128],
                                      self.XA[:, k * 128:(k + 1) * 128],
                                      self.IDENTB[:, :], start=True, stop=True),
                             inc="l0x_mm2" if kk == 3 else None)
                    P.do("vector", lambda e, half=half, bank=bank:
                         e.tensor_copy(
                             self.OUTS[:, half * 512:(half + 1) * 512],
                             bank[:, :]),
                         waits=(("l0x_mm2", P.cnt("l0x_mm2")),),
                         inc="l0x_ev2")
                    P.do("sync", lambda e, half=half, t0=t0, j=j:
                         e.dma_start(
                             out=xq_fm[half * 512:(half + 1) * 512,
                                       t0 + j * 128:t0 + (j + 1) * 128]
                             .rearrange("(c p) t -> p c t", p=128),
                             in_=self.OUTS[:, half * 512:(half + 1) * 512]
                             .rearrange("p (c t) -> p c t", t=128)),
                         waits=(("l0x_ev2", P.cnt("l0x_ev2")),),
                         inc="l0x_st", inc_n=16)
                P.wait("vector", ("l0x_st", P.cnt("l0x_st")))
        P.barrier()
        for nm, w_ap, O, I in (
                ("gate", self.IN["L0_ffn_w_gate"], 2816, 1024),
                ("up", self.IN["L0_ffn_w_up"], 2816, 1024),
                ("down", self.IN["L0_ffn_w_down"], 1024, 2816)):
            for r in range(O // 128):
                P.do("sync", lambda e, r=r, w_ap=w_ap, I=I: e.dma_start(
                    out=self.FA[:, 0:I], in_=w_ap[r * 128:(r + 1) * 128, :]),
                    inc="l0w_ld", inc_n=16,
                    waits=(("l0w_st", P.cnt("l0w_st")),))
                P.wait("vector", ("l0w_ld", P.cnt("l0w_ld")))
                self.emit_fp4_tile("l0w", self.FA[:, 0:I],
                                   self.XA[:, 0:I], I,
                                   self.FA[:, 2816:2816 + I],
                                   self.FB[:, 0:I],
                                   prewaits=(("l0w_ld", P.cnt("l0w_ld")),))
                P.do("sync", lambda e, r=r, I=I, natq=natq: e.dma_start(
                    out=natq[r * 128:(r + 1) * 128, 0:I],
                    in_=self.XA[:, 0:I]),
                    waits=(("l0w_q", P.cnt("l0w_q")),),
                    inc="l0w_st", inc_n=16)
            for i0 in range(0, I, 128):
                for o0 in range(0, O, 2048):
                    ff = min(2048, O - o0)
                    P.do("sync", lambda e, i0=i0, o0=o0, ff=ff, natq=natq:
                         e.dma_start_transpose(
                             self.WB[:, 0:ff], natq[o0:o0 + ff,
                                                    i0:i0 + 128]),
                         waits=("l0w_st",
                                ("l0w_tst", max(0, P.cnt("l0w_tst") - 16))),
                         inc="l0w_t", inc_n=16)
                    P.do("sync", lambda e, i0=i0, o0=o0, ff=ff, nm=nm:
                         e.dma_start(out=twq[nm][i0:i0 + 128, o0:o0 + ff],
                                     in_=self.WB[:, 0:ff]),
                         waits=("l0w_t",), inc="l0w_tst", inc_n=16)
            P.barrier()
        self.emit_proj_tm("l0g", xq_fm, 1024, 2816, twq["gate"], gate_tm,
                          pre_transposed=True, out_f32=True)
        self.emit_proj_tm("l0u", xq_fm, 1024, 2816, twq["up"], up_tm,
                          pre_transposed=True, out_f32=True)
        for tt in range(T // 128):
            P.do("sync", lambda e, tt=tt: e.dma_start(
                out=self.FA[:, 0:2816], in_=gate_tm[tt * 128:(tt + 1) * 128, :]),
                inc="l0h_ld", inc_n=16,
                waits=(("l0h_mm", P.cnt("l0h_mm")),))
            P.do("sync", lambda e, tt=tt: e.dma_start(
                out=self.FA[:, 4096:4096 + 2816],
                in_=up_tm[tt * 128:(tt + 1) * 128, :]),
                inc="l0h_ld", inc_n=16)
            P.wait("scalar", "l0h_ld")
            P.do("scalar", lambda e: e.activation(
                self.FA[:, 0:2816], self.FA[:, 0:2816], AF.Silu),
                waits=(("l0h_q", P.cnt("l0h_q")),), inc="l0h_si")
            P.do("vector", lambda e: e.tensor_tensor(
                self.FA[:, 0:2816], self.FA[:, 0:2816],
                self.FA[:, 4096:4096 + 2816], ALU.mult),
                waits=(("l0h_si", P.cnt("l0h_si")),), inc="l0h_hm")
            P.wait("vector", ("l0h_hm", P.cnt("l0h_hm")))
            self.emit_fp4_tile("l0h", self.FA[:, 0:2816],
                               self.XA[:, 0:2816], 2816,
                               self.FA[:, 2816:5632],
                               self.FB[:, 0:2816],
                               prewaits=(("l0h_hm", P.cnt("l0h_hm")),))
            for g in range(6):
                bank = self.PS[4 + (g % 2)]
                nsub = min(4, 22 - g * 4)
                P.wait("tensor", ("l0h_ev", P.cnt("l0h_ev")),
                       ("l0h_q", P.cnt("l0h_q")))
                for kk in range(nsub):
                    k = g * 4 + kk
                    P.do("tensor", lambda e, k=k, kk=kk, bank=bank:
                         e.matmul(bank[:, kk * 128:(kk + 1) * 128],
                                  self.XA[:, k * 128:(k + 1) * 128],
                                  self.IDENTB[:, :], start=True, stop=True),
                         inc="l0h_mm" if kk == nsub - 1 else None)
                P.do("vector", lambda e, bank=bank, nsub=nsub, g=g:
                     e.tensor_copy(
                         self.OUTS[:, 0:nsub * 128], bank[:, 0:nsub * 128]),
                     waits=(("l0h_mm", P.cnt("l0h_mm")),
                            ("l0h_st", P.cnt("l0h_st"))),
                     inc="l0h_ev")
                P.do("sync", lambda e, g=g, tt=tt, nsub=nsub: e.dma_start(
                    out=hq_fm[g * 512:g * 512 + nsub * 128,
                              tt * 128:(tt + 1) * 128]
                    .rearrange("(c p) t -> p c t", p=128),
                    in_=self.OUTS[:, 0:nsub * 128]
                    .rearrange("p (c t) -> p c t", t=128)),
                    waits=(("l0h_ev", P.cnt("l0h_ev")),),
                    inc="l0h_st", inc_n=16)
        P.barrier()
        self.emit_proj("l0d", hq_fm, 2816, 1024, twq["down"], self.g_fm,
                       pre_transposed=True, act_is_f32=False,
                       epilogue=self.ep_bias_scale(None, 1.0, AF.Silu),
                       o_slab=256, act_resident=False, t_block=512)

    def emit_l2_ffn(self, phases=(0, 1, 2, 3, 4, 5, 6, 7, 8)):
        P = self.P
        phases = set(phases)
        xn2_fm = self.scratch("l2f_xn2", (1024, T), BF16)
        xn2_tm = self.scratch("l2f_xn2tm", (T, 1024), BF16)
        se_nat = self.scratch("l2f_senat", (4096, 1024), BF16)
        se_fm = self.scratch("l2f_se", (1024, 4096), BF16)
        L_scr = self.scratch("l2f_L", (T, 4096))
        pc_scr = self.scratch("l2f_pc", (T, 4096), BF16)
        disp_scr = self.scratch("l2f_disp", (T, 4096), BF16)
        slots_fm = self.scratch("l2f_slots", (1024, 4096), BF16)
        h2_fm = self.scratch("l2f_h2", (4096, T), BF16)
        proc_fm = self.scratch("l2f_proc", (1024, T), BF16)
        proc_tm = self.scratch("l2f_proctm", (T, 1024), BF16)
        maxrow_scr = self.scratch("l2f_maxrow", (32, 128))

        if 0 in phases:
            self.emit_rms_fm("l2x", self.xn_fm, xn2_fm, 1024,
                         self.IN["L2_ffn_gamma_x"], "maxnorm",
                         eps=1e-12, outscale=32.0)
            self.emit_fm_to_tm("l2xt", xn2_fm, xn2_tm, 1024)

        if 1 not in phases:
            return
        gs = self.IN["L2_ffn_gamma_slot"]
        se_in = self.IN["L2_ffn_slot_embeds"].rearrange("e t d -> (e t) d")
        P.do("sync", lambda e: e.dma_start(
            out=self.COEFR[0:1, 0:1024], in_=gs[:]),
            inc="l2s_gld", inc_n=16)
        for c in range(2):
            P.do("tensor", lambda e, c=c: e.matmul(
                self.PS[6][:, 0:512], self.ONESROWF[0:1, :],
                self.COEFR[0:1, c * 512:(c + 1) * 512],
                start=True, stop=True),
                waits=("l2s_gld", ("l2s_gev", P.cnt("l2s_gev"))),
                inc="l2s_gbc")
            P.do("vector", lambda e, c=c: e.tensor_copy(
                self.FB[:, c * 512:(c + 1) * 512], self.PS[6][:, 0:512]),
                waits=(("l2s_gbc", P.cnt("l2s_gbc")),), inc="l2s_gev")
        for r in range(32):
            P.do("sync", lambda e, r=r: e.dma_start(
                out=self.FA[:, 0:1024], in_=se_in[r * 128:(r + 1) * 128, :]),
                inc="l2s_ld", inc_n=16,
                waits=(("l2s_st", P.cnt("l2s_st")),))
            P.do("vector", lambda e: e.scalar_tensor_tensor(
                self.FA[:, 1024:2048], self.FA[:, 0:1024], 1.0,
                self.FA[:, 0:1024], ALU.mult, ALU.mult,
                accum_out=self.ROWS[:, 0:1]),
                waits=(("l2s_ld", P.cnt("l2s_ld")), "l2s_gev",
                       ("l2s_q", P.cnt("l2s_q"))),
                inc="l2s_v")
            P.do("scalar", lambda e: e.activation(
                self.ROWS[:, 1:2], self.ROWS[:, 0:1], AF.Sqrt),
                waits=(("l2s_v", P.cnt("l2s_v")),), inc="l2s_c")
            P.do("vector", lambda e: e.tensor_scalar(
                self.ROWS[:, 2:3], self.ROWS[:, 1:2], 1e-12, None, ALU.max),
                waits=(("l2s_c", P.cnt("l2s_c")),), inc="l2s_v")
            P.do("vector", lambda e: e.reciprocal(
                self.ROWS[:, 3:4], self.ROWS[:, 2:3]),
                waits=(("l2s_v", P.cnt("l2s_v")),), inc="l2s_v")
            P.do("vector", lambda e: e.tensor_scalar(
                self.ROWS[:, 3:4], self.ROWS[:, 3:4], 32.0, None, ALU.mult),
                waits=(("l2s_v", P.cnt("l2s_v")),), inc="l2s_v")
            P.do("vector", lambda e: e.tensor_tensor(
                self.FA[:, 1024:2048], self.FA[:, 0:1024],
                self.FB[:, 0:1024], ALU.mult),
                waits=(("l2s_v", P.cnt("l2s_v")),), inc="l2s_v")
            P.do("vector", lambda e: e.tensor_scalar(
                self.XA[:, 0:1024], self.FA[:, 1024:2048],
                self.ROWS[:, 3:4], None, ALU.mult),
                waits=(("l2s_v", P.cnt("l2s_v")),), inc="l2s_q")
            P.do("sync", lambda e, r=r: e.dma_start(
                out=se_nat[r * 128:(r + 1) * 128, :],
                in_=self.XA[:, 0:1024]),
                waits=(("l2s_q", P.cnt("l2s_q")),),
                inc="l2s_st", inc_n=16)
        for i0 in range(0, 1024, 128):
            for o0 in range(0, 4096, 2048):
                P.do("sync", lambda e, i0=i0, o0=o0: e.dma_start_transpose(
                    self.WB[:, 0:2048], se_nat[o0:o0 + 2048, i0:i0 + 128]),
                    waits=("l2s_st", ("l2s_t2", P.cnt("l2s_t2"))),
                    inc="l2s_t", inc_n=16)
                P.do("sync", lambda e, i0=i0, o0=o0: e.dma_start(
                    out=se_fm[i0:i0 + 128, o0:o0 + 2048],
                    in_=self.WB[:, 0:2048]),
                    waits=("l2s_t",), inc="l2s_t2", inc_n=16)
        P.barrier()
        if 2 not in phases:
            return
        for k in range(8):
            base = self.XA if k < 4 else self.XB
            P.do("sync", lambda e, k=k, base=base: e.dma_start(
                out=base[:, (k % 4) * T:(k % 4) * T + T],
                in_=xn2_fm[k * 128:(k + 1) * 128, :]),
                inc="l2L_ald", inc_n=16)
            base2 = self.WA if k < 4 else self.WB
            P.do("gpsimd", lambda e, k=k, base2=base2: e.dma_start(
                out=base2[:, (k % 4) * 4096:(k % 4) * 4096 + 4096],
                in_=se_fm[k * 128:(k + 1) * 128, :]),
                inc="l2L_ald", inc_n=16)
        MAXC = self.ROWS[:, 32:64]
        TMPR = self.ROWS[:, 64:68]
        for st_ in range(32):
            s0 = st_ * 128
            for nb in range(8):
                gi = P.cnt("l2L_mm")
                bank = self.PS[gi % 2]
                P.wait("tensor", ("l2L_ev", max(0, P.cnt("l2L_ev") - 1)),
                       ("l2L_ald", P.cnt("l2L_ald")))
                for k in range(8):
                    xbase = self.XA if k < 4 else self.XB
                    wbase = self.WA if k < 4 else self.WB
                    P.do("tensor", lambda e, k=k, nb=nb, s0=s0, bank=bank,
                         xbase=xbase, wbase=wbase: e.matmul(
                             bank[:, 0:512],
                             xbase[:, (k % 4) * T + s0:(k % 4) * T + s0 + 128],
                             wbase[:, (k % 4) * 4096 + nb * 512:
                                   (k % 4) * 4096 + (nb + 1) * 512],
                             start=(k == 0), stop=(k == 7)),
                         inc="l2L_mm" if k == 7 else None)
                P.do("scalar", lambda e, nb=nb, bank=bank: e.activation(
                    self.FA[:, nb * 512:(nb + 1) * 512], bank[:, 0:512],
                    AF.Copy),
                    waits=(("l2L_mm", P.cnt("l2L_mm")),
                           ("l2L_tp", P.cnt("l2L_tp")),
                           ("l2L_st", P.cnt("l2L_st"))),
                    inc="l2L_ev")
            P.do("sync", lambda e, s0=s0: e.dma_start(
                out=L_scr[s0:s0 + 128, :], in_=self.FA[:, 0:4096]),
                waits=("l2L_ev",), inc="l2L_st", inc_n=16)
            for nb in range(8):
                bank = self.PS[2 + (nb % 2)]
                P.wait("tensor", ("l2L_mx", P.cnt("l2L_mx")))
                for j in range(4):
                    P.do("tensor", lambda e, nb=nb, j=j, bank=bank: e.matmul(
                        bank[:, j * 128:(j + 1) * 128],
                        self.FA[:, nb * 512 + j * 128:
                                nb * 512 + (j + 1) * 128],
                        self.IDENT[:, :], start=True, stop=True),
                        waits=(("l2L_ev", P.cnt("l2L_ev")),)
                        if j == 0 else (),
                        inc="l2L_tp" if j == 3 else None)
                if st_ == 0:
                    P.do("vector", lambda e, nb=nb, bank=bank: e.reduce_max(
                        MAXC[:, nb * 4:(nb + 1) * 4],
                        bank.rearrange("p (a b) -> p a b", b=128),
                        axis=AX.X),
                        waits=(("l2L_tp", P.cnt("l2L_tp")),),
                        inc="l2L_mx")
                else:
                    P.do("vector", lambda e, nb=nb, bank=bank: e.reduce_max(
                        TMPR[:, 0:4],
                        bank.rearrange("p (a b) -> p a b", b=128),
                        axis=AX.X),
                        waits=(("l2L_tp", P.cnt("l2L_tp")),),
                        inc="l2L_mx")
                    P.do("vector", lambda e, nb=nb: e.tensor_tensor(
                        MAXC[:, nb * 4:(nb + 1) * 4],
                        MAXC[:, nb * 4:(nb + 1) * 4],
                        TMPR[:, 0:4], ALU.max),
                        waits=(("l2L_mx", P.cnt("l2L_mx")),),
                        inc="l2L_mx")
        P.barrier()
        if 3 not in phases:
            return
        P.do("tensor", lambda e: e.matmul(
            self.PS[4][0:32, 0:128], MAXC, self.IDENT[:, :],
            start=True, stop=True), inc="l2B_t")
        P.do("vector", lambda e: e.tensor_scalar(
            self.FC[0:32, 0:128], self.PS[4][0:32, 0:128], -1.0, None,
            ALU.mult),
            waits=("l2B_t",), inc="l2B_v")
        P.do("sync", lambda e: e.dma_start(
            out=maxrow_scr[:, :], in_=self.FC[0:32, 0:128]),
            waits=("l2B_v",), inc="l2B_st", inc_n=16)
        P.do("sync", lambda e: e.dma_start(
            out=self.COEFR[0:1, 0:4096],
            in_=maxrow_scr.rearrange("a b -> (a b)")),
            waits=("l2B_st",), inc="l2B_ld", inc_n=16)
        for c in range(8):
            P.do("tensor", lambda e, c=c: e.matmul(
                self.PS[4][:, 0:512], self.ONESROWF[0:1, :],
                self.COEFR[0:1, c * 512:(c + 1) * 512],
                start=True, stop=True),
                waits=("l2B_ld", ("l2B_ev", P.cnt("l2B_ev"))),
                inc="l2B_bc")
            P.do("vector", lambda e, c=c: e.tensor_copy(
                self.FB[:, c * 512:(c + 1) * 512], self.PS[4][:, 0:512]),
                waits=(("l2B_bc", P.cnt("l2B_bc")),), inc="l2B_ev")
        P.barrier()
        if 4 not in phases:
            return
        for st_ in range(32):
            s0 = st_ * 128
            P.do("sync", lambda e, s0=s0: e.dma_start(
                out=self.FA[:, 0:4096], in_=L_scr[s0:s0 + 128, :]),
                inc="l2C_ld", inc_n=16,
                waits=(("l2C_dx", P.cnt("l2C_dx")),))
            P.do("vector", lambda e: e.reduce_max(
                self.ROWS[:, 0:1], self.FA[:, 0:4096], axis=AX.X,
                negate=True),
                waits=(("l2C_ld", P.cnt("l2C_ld")),
                       ("l2C_st", P.cnt("l2C_st"))),
                inc="l2C_v")
            P.do("scalar", lambda e: e.activation(
                self.XB[:, 0:4096], self.FA[:, 0:4096], AF.Exp,
                bias=self.ROWS[:, 0:1], accum_out=self.ROWS[:, 1:2]),
                waits=(("l2C_v", P.cnt("l2C_v")),), inc="l2C_x")
            P.do("vector", lambda e: e.reciprocal(
                self.ROWS[:, 2:3], self.ROWS[:, 1:2]),
                waits=(("l2C_x", P.cnt("l2C_x")),), inc="l2C_v")
            P.do("vector", lambda e: e.tensor_scalar(
                self.XB[:, 0:4096], self.XB[:, 0:4096],
                self.ROWS[:, 2:3], None, ALU.mult),
                waits=(("l2C_v", P.cnt("l2C_v")),), inc="l2C_pc")
            P.do("sync", lambda e, s0=s0: e.dma_start(
                out=pc_scr[s0:s0 + 128, :], in_=self.XB[:, 0:4096]),
                waits=(("l2C_pc", P.cnt("l2C_pc")),),
                inc="l2C_st", inc_n=16)
            P.do("vector", lambda e: e.tensor_tensor(
                self.FA[:, 0:4096], self.FA[:, 0:4096],
                self.FB[:, 0:4096], ALU.add),
                waits=(("l2C_x", P.cnt("l2C_x")),), inc="l2C_v")
            P.do("scalar", lambda e: e.activation(
                self.XB[:, 4096:8192], self.FA[:, 0:4096], AF.Exp),
                waits=(("l2C_v", P.cnt("l2C_v")),
                       ("l2C_st", max(0, P.cnt("l2C_st") - 16))),
                inc="l2C_dx")
            P.do("sync", lambda e, s0=s0: e.dma_start(
                out=disp_scr[s0:s0 + 128, :], in_=self.XB[:, 4096:8192]),
                waits=(("l2C_dx", P.cnt("l2C_dx")),),
                inc="l2C_st", inc_n=16)
        P.barrier()
        if 5 not in phases:
            return
        for c in range(32):
            base = self.XB if c < 16 else self.WA
            off = (c % 16) * 1024
            P.do("sync", lambda e, c=c, base=base, off=off: e.dma_start(
                out=base[:, off:off + 1024],
                in_=xn2_tm[c * 128:(c + 1) * 128, :]),
                inc="l2S_tld", inc_n=16)
        ones_col = self.const(1.0)
        for nb in range(8):
            for c in range(32):
                P.do("gpsimd", lambda e, c=c, nb=nb: e.dma_start(
                    out=self.XA[:, c * 512:(c + 1) * 512],
                    in_=disp_scr[c * 128:(c + 1) * 128,
                                 nb * 512:(nb + 1) * 512]),
                    inc="l2S_dld", inc_n=16,
                    waits=(("l2S_mm", P.cnt("l2S_mm")),
                           ("l2S_sm", P.cnt("l2S_sm"))) if c == 0 else ())
            P.wait("tensor", ("l2S_dld", P.cnt("l2S_dld")),
                   ("l2S_tld", P.cnt("l2S_tld")))
            for d in range(8):
                bank = self.PS[d % 2]
                P.wait("tensor", ("l2S_ev", max(0, P.cnt("l2S_ev") - 1)))
                for c in range(32):
                    base = self.XB if c < 16 else self.WA
                    off = (c % 16) * 1024 + d * 128
                    P.do("tensor", lambda e, c=c, base=base, off=off,
                         bank=bank: e.matmul(
                             bank[:, 0:512], base[:, off:off + 128],
                             self.XA[:, c * 512:(c + 1) * 512],
                             start=(c == 0), stop=(c == 31)),
                         inc="l2S_mm" if c == 31 else None)
                P.do("scalar", lambda e, bank=bank: e.activation(
                    self.OUTS[:, 0:512], bank[:, 0:512], AF.Copy),
                    waits=(("l2S_mm", P.cnt("l2S_mm")),
                           ("l2S_sst", P.cnt("l2S_sst"))),
                    inc="l2S_ev")
                P.do("sync", lambda e, d=d, nb=nb: e.dma_start(
                    out=slots_fm[d * 128:(d + 1) * 128,
                                 nb * 512:(nb + 1) * 512],
                    in_=self.OUTS[:, 0:512]),
                    waits=(("l2S_ev", P.cnt("l2S_ev")),),
                    inc="l2S_sst", inc_n=16)
            P.wait("tensor", ("l2S_sm", P.cnt("l2S_sm")))
            for c in range(32):
                P.do("tensor", lambda e, c=c: e.matmul(
                    self.PS[2][0:1, 0:512], self.ONESCOLB[:, 0:1],
                    self.XA[:, c * 512:(c + 1) * 512],
                    start=(c == 0), stop=(c == 31)),
                    inc="l2S_smm" if c == 31 else None)
            P.do("vector", lambda e, nb=nb: e.tensor_copy(
                self.COEFR[0:1, nb * 512:(nb + 1) * 512],
                self.PS[2][0:1, 0:512]),
                waits=(("l2S_smm", P.cnt("l2S_smm")),), inc="l2S_sm")
        P.do("vector", lambda e: e.reciprocal(
            self.COEFR[0:1, 0:4096], self.COEFR[0:1, 0:4096]),
            waits=(("l2S_sm", P.cnt("l2S_sm")),), inc="l2S_rc")
        for c in range(8):
            P.do("tensor", lambda e, c=c: e.matmul(
                self.PS[3][:, 0:512], self.ONESROWF[0:1, :],
                self.COEFR[0:1, c * 512:(c + 1) * 512],
                start=True, stop=True),
                waits=(("l2S_rc", P.cnt("l2S_rc")),
                       ("l2S_rbe", P.cnt("l2S_rbe"))),
                inc="l2S_rb")
            P.do("vector", lambda e, c=c: e.tensor_copy(
                self.FB[:, c * 512:(c + 1) * 512], self.PS[3][:, 0:512]),
                waits=(("l2S_rb", P.cnt("l2S_rb")),), inc="l2S_rbe")
        P.barrier()
        if 6 not in phases:
            return
        def ep_mult_gelu(b, tag, bank, oc_abs, ocw, tcols, out_view,
                         prewaits):
            Pq = b.P
            tw = tcols[1] - tcols[0]
            gi = Pq.cnt(tag + "_evd")
            r = b.FC[:, (gi % 2) * 512:(gi % 2) * 512 + tw]
            Pq.do("vector", lambda e: e.tensor_tensor(
                r, bank[:, 0:tw], b.FB[:, tcols[0]:tcols[1]], ALU.mult),
                waits=prewaits + ((tag + "_epv",
                                   max(0, Pq.cnt(tag + "_epv") - 1)),),
                inc=tag + "_epm")
            Pq.do("scalar", lambda e: e.activation(out_view, r, AF.Gelu),
                  waits=((tag + "_epm", Pq.cnt(tag + "_epm")),),
                  inc=tag + "_epv")
            Pq.do("scalar", lambda e: e.nop(),
                  waits=((tag + "_epv", Pq.cnt(tag + "_epv")),),
                  inc=tag + "_evd")
        self.emit_proj("l2f1", slots_fm, 1024, 4096,
                       self.IN["L2_ffn_w1"], h2_fm,
                       pre_transposed=False, act_is_f32=False,
                       epilogue=ep_mult_gelu)
        self.emit_proj("l2f2", h2_fm, 4096, 1024, self.IN["L2_ffn_w2"],
                       proc_fm, pre_transposed=False, act_is_f32=False,
                       o_slab=256, t_block=512, act_resident=False)
        self.emit_fm_to_tm("l2pt", proc_fm, proc_tm, 1024)
        if 8 not in phases:
            return
        for c in range(32):
            base = self.WA if c < 16 else self.WB
            off = (c % 16) * 1024
            P.do("sync", lambda e, c=c, base=base, off=off: e.dma_start(
                out=base[:, off:off + 1024],
                in_=proc_tm[c * 128:(c + 1) * 128, :]),
                inc="l2O_pld", inc_n=16)
        for sb in range(8):
            s0 = sb * 512
            for j in range(4):
                P.do("gpsimd", lambda e, j=j, s0=s0: e.dma_start(
                    out=self.XA[:, j * 4096:(j + 1) * 4096],
                    in_=pc_scr[s0 + j * 128:s0 + (j + 1) * 128, :]),
                    inc="l2O_cld", inc_n=16,
                    waits=(("l2O_tp", P.cnt("l2O_tp")),) if j == 0 else ())
            P.wait("tensor", ("l2O_cld", P.cnt("l2O_cld")))
            for nc_ in range(32):
                bank = self.PS[4 + (nc_ % 2)]
                P.wait("tensor", ("l2O_te", max(0, P.cnt("l2O_te") - 1)))
                for j in range(4):
                    P.do("tensor", lambda e, nc_=nc_, j=j, bank=bank:
                         e.matmul(
                             bank[:, j * 128:(j + 1) * 128],
                             self.XA[:, j * 4096 + nc_ * 128:
                                     j * 4096 + (nc_ + 1) * 128],
                             self.IDENTB[:, :], start=True, stop=True),
                         inc="l2O_tp" if j == 3 else None)
                P.do("vector", lambda e, nc_=nc_, bank=bank: e.tensor_copy(
                    self.XB[:, nc_ * 512:(nc_ + 1) * 512], bank[:, :]),
                    waits=(("l2O_tp", P.cnt("l2O_tp")),
                           ("l2O_mm", P.cnt("l2O_mm"))),
                    inc="l2O_te")
            P.wait("tensor", ("l2O_te", P.cnt("l2O_te")),
                   ("l2O_pld", P.cnt("l2O_pld")))
            for d in range(8):
                bank = self.PS[d % 2]
                P.wait("tensor", ("l2O_oe", max(0, P.cnt("l2O_oe") - 1)))
                for c in range(32):
                    base = self.WA if c < 16 else self.WB
                    off = (c % 16) * 1024 + d * 128
                    P.do("tensor", lambda e, c=c, base=base, off=off,
                         bank=bank: e.matmul(
                             bank[:, 0:512], base[:, off:off + 128],
                             self.XB[:, c * 512:(c + 1) * 512],
                             start=(c == 0), stop=(c == 31)),
                         inc="l2O_mm" if c == 31 else None)
                P.do("scalar", lambda e, bank=bank: e.activation(
                    self.OUTS[:, 0:512], bank[:, 0:512], AF.Silu),
                    waits=(("l2O_mm", P.cnt("l2O_mm")),
                           ("l2O_gst", P.cnt("l2O_gst"))),
                    inc="l2O_oe")
                P.do("sync", lambda e, d=d, s0=s0: e.dma_start(
                    out=self.g_fm[d * 128:(d + 1) * 128, s0:s0 + 512],
                    in_=self.OUTS[:, 0:512]),
                    waits=(("l2O_oe", P.cnt("l2O_oe")),),
                    inc="l2O_gst", inc_n=16)
        P.barrier()

    def emit_l1_attn(self):
        P = self.P
        h_fm = self.scratch("g1_h", (1024, T), BF16)
        q_fm = self.scratch("g1_q", (512, T), BF16)
        k_fm = self.scratch("g1_k", (512, T), BF16)
        v_fm = self.scratch("g1_v", (1024, T), BF16)
        go_fm = self.scratch("g1_go", (1024, T), BF16)
        gl_fm = self.scratch("g1_gl", (512, T))
        o_fm = self.scratch("g1_o", (1024, T), BF16)
        ao_fm = self.scratch("g1_ao", (1024, T), BF16)
        x2_fm = self.scratch("g1_x2", (1024, T))
        n_fm = self.scratch("g1_n", (1024, T), BF16)
        mg_fm = self.scratch("g1_mg", (2816, T), BF16)
        mu_fm = self.scratch("g1_mu", (2816, T), BF16)
        mid_fm = self.scratch("g1_mid", (2816, T), BF16)

        self.emit_rms_fm("g1h", self.xn_fm, h_fm, 1024,
                         self.IN["L1_attn_attn_norm_w"], "rms", eps=1e-6)
        self.emit_proj("g1q", h_fm, 1024, 512, self.IN["L1_attn_Wq"],
                       q_fm, act_is_f32=False)
        self.emit_proj("g1k", h_fm, 1024, 512, self.IN["L1_attn_Wk"],
                       k_fm, act_is_f32=False)
        self.emit_proj("g1v", h_fm, 1024, 1024, self.IN["L1_attn_Wv"],
                       v_fm, act_is_f32=False)
        self.emit_proj("g1g", h_fm, 1024, 1024, self.IN["L1_attn_Wg"],
                       go_fm, act_is_f32=False)
        for k in range(8):
            P.do("gpsimd", lambda e, k=k: e.dma_start(
                out=self.WA[:, k * 16:(k + 1) * 16],
                in_=self.IN["L1_attn_Wgk1"][k * 128:(k + 1) * 128, :]),
                inc="g1l_w1", inc_n=16)
            base = self.XA if k < 4 else None
            P.do("sync", lambda e, k=k: e.dma_start(
                out=(self.XA[:, (k % 4) * T:(k % 4) * T + T] if k < 4 else
                     self.WB[:, (k % 4) * T:(k % 4) * T + T]),
                in_=h_fm[k * 128:(k + 1) * 128, :]),
                inc="g1l_h", inc_n=16)
        P.do("gpsimd", lambda e: e.dma_start(
            out=self.WA[:, 256:256 + 64].rearrange("p (a b) -> p a b", a=4)
            if False else self.WA[0:16, 256:256 + 512],
            in_=self.IN["L1_attn_Wgk2"][:, :]),
            inc="g1l_w2", inc_n=16)
        for oc in range(4):
            P.do("gpsimd", lambda e, oc=oc: e.dma_start(
                out=self.BIAS[:, 8 + oc:9 + oc],
                in_=self.IN["L1_attn_bgk2"][oc * 128:(oc + 1) * 128]),
                inc="g1l_b", inc_n=16)
        P.wait("tensor", "g1l_w1", "g1l_h")
        for tb in range(8):
            t0 = tb * 512
            for k in range(8):
                src_t = (self.XA if k < 4 else self.WB)
                P.do("tensor", lambda e, k=k, t0=t0, src_t=src_t: e.matmul(
                    self.PS[5][0:16, 0:512],
                    self.WA[:, k * 16:(k + 1) * 16],
                    src_t[:, (k % 4) * T + t0:(k % 4) * T + t0 + 512],
                    start=(k == 0), stop=(k == 7)),
                    waits=(("g1l_u", P.cnt("g1l_u")),) if k == 0 else (),
                    inc="g1l_m1" if k == 7 else None)
            P.do("vector", lambda e, tb=tb: e.tensor_copy(
                self.XB[0:16, tb * 512:(tb + 1) * 512],
                self.PS[5][0:16, 0:512]),
                waits=(("g1l_m1", P.cnt("g1l_m1")),), inc="g1l_u")
        ln1 = self.const(1.0)
        P.wait("tensor", "g1l_u", "g1l_w2")
        for tb in range(8):
            t0 = tb * 512
            for oc in range(4):
                P.do("tensor", lambda e, oc=oc, t0=t0: e.matmul(
                    self.PS[6][:, 0:512],
                    self.WA[0:16, 256 + oc * 128:256 + (oc + 1) * 128],
                    self.XB[0:16, t0:t0 + 512], start=True, stop=True),
                    waits=(("g1l_ev", P.cnt("g1l_ev")),),
                    inc="g1l_m2")
                FAv = self.FA
                P.do("scalar", lambda e, oc=oc: e.activation(
                    FAv[:, 0:512], self.PS[6][:, 0:512], AF.Identity,
                    bias=self.BIAS[:, 8 + oc:9 + oc]),
                    waits=(("g1l_m2", P.cnt("g1l_m2")),
                           ("g1l_gv", P.cnt("g1l_gv")),
                           ("g1l_b", P.cnt("g1l_b"))),
                    inc="g1l_ev")
                P.do("scalar", lambda e: e.activation(
                    FAv[:, 512:1024], FAv[:, 0:512], AF.Abs),
                    waits=(("g1l_ev", P.cnt("g1l_ev")),), inc="g1l_s")
                P.do("scalar", lambda e: e.activation(
                    FAv[:, 512:1024], FAv[:, 512:1024], AF.Exp, scale=-1.0),
                    waits=(("g1l_s", P.cnt("g1l_s")),), inc="g1l_s")
                P.do("scalar", lambda e: e.activation(
                    FAv[:, 512:1024], FAv[:, 512:1024], AF.Ln, bias=ln1),
                    waits=(("g1l_s", P.cnt("g1l_s")),), inc="g1l_s")
                P.do("vector", lambda e: e.tensor_scalar(
                    FAv[:, 1024:1536], FAv[:, 0:512], 0.0, None, ALU.min),
                    waits=(("g1l_s", P.cnt("g1l_s")),), inc="g1l_gv")
                P.do("vector", lambda e: e.tensor_tensor(
                    FAv[:, 1024:1536], FAv[:, 1024:1536], FAv[:, 512:1024],
                    ALU.subtract),
                    waits=(("g1l_gv", P.cnt("g1l_gv")),), inc="g1l_gv")
                P.do("vector", lambda e: e.tensor_scalar(
                    FAv[:, 1024:1536], FAv[:, 1024:1536], 1.0 / 16.0, None,
                    ALU.mult),
                    waits=(("g1l_gv", P.cnt("g1l_gv")),), inc="g1l_gv")
                P.do("sync", lambda e, oc=oc, t0=t0: e.dma_start(
                    out=gl_fm[oc * 128:(oc + 1) * 128, t0:t0 + 512],
                    in_=self.FA[:, 1024:1536]),
                    waits=(("g1l_gv", P.cnt("g1l_gv")),),
                    inc="g1l_st", inc_n=16)
                P.wait("vector", ("g1l_st", P.cnt("g1l_st")))
        P.barrier()

        P.do("sync", lambda e: e.dma_start(
            out=self.COEFR[0:1, 0:256], in_=self.IN["L1_attn_gnorm_w"][:]),
            inc="gc_gn", inc_n=16)
        P.do("tensor", lambda e: e.matmul(
            self.PS[7][:, 0:256], self.ONESROWF[0:1, :],
            self.COEFR[0:1, 0:256], start=True, stop=True),
            waits=("gc_gn",), inc="gc_gnb")
        P.do("vector", lambda e: e.tensor_copy(
            self.FC[:, 256:512], self.PS[7][:, 0:256]),
            waits=(("gc_gnb", P.cnt("gc_gnb")),), inc="gc_gne")

        P.barrier()
        LNC = float(np.log(128.0 ** -0.5))
        lnc_col = self.const(LNC)
        for hd in range(4):
            P.do("sync", lambda e, hd=hd: e.dma_start(
                out=self.XA[:, 0:T], in_=q_fm[hd * 128:(hd + 1) * 128, :]),
                inc="gc_ld", inc_n=16)
            P.do("sync", lambda e, hd=hd: e.dma_start(
                out=self.XA[:, T:2 * T], in_=k_fm[hd * 128:(hd + 1) * 128, :]),
                inc="gc_ld", inc_n=16)
            for half in range(2):
                P.do("sync", lambda e, hd=hd, half=half: e.dma_start(
                    out=self.XB[:, half * T:(half + 1) * T],
                    in_=v_fm[hd * 256 + half * 128:
                             hd * 256 + half * 128 + 128, :]),
                    inc="gc_ld", inc_n=16)
            P.do("sync", lambda e, hd=hd: e.dma_start(
                out=self.FB[:, 0:T], in_=gl_fm[hd * 128:(hd + 1) * 128, :]),
                inc="gc_ld", inc_n=16)
            P.wait("vector", "gc_ld")
            for c in range(64):
                P.do("vector", lambda e, c=c: e.tensor_tensor_scan(
                    self.FA[:, 4096 + c * 64:4096 + (c + 1) * 64],
                    self.FB[:, c * 64:(c + 1) * 64],
                    self.FB[:, c * 64:(c + 1) * 64], 0.0,
                    ALU.add, ALU.bypass),
                    inc="gc_b" if c == 63 else None)
            P.do("vector", lambda e: e.memset(self.FC[:, 0:256], 0.0),
                 inc="gc_s0")
            P.wait("scalar", "gc_b", "gc_ld")
            for c in range(64):
                t0 = c * 64
                B_ = self.FA[:, 4096 + t0:4096 + t0 + 64]
                P.do("scalar", lambda e, B_=B_: e.activation(
                    self.FA[:, 0:64], B_, AF.Exp, bias=lnc_col),
                    waits=(("gc_i5", P.cnt("gc_i5")),), inc="gc_e1")
                P.do("vector", lambda e, t0=t0: e.tensor_tensor(
                    self.FA[:, 64:128], self.XA[:, t0:t0 + 64],
                    self.FA[:, 0:64], ALU.mult),
                    waits=(("gc_e1", P.cnt("gc_e1")),
                           ("gc_m2", P.cnt("gc_m2"))), inc="gc_i1")
                P.do("vector", lambda e: e.tensor_copy(
                    self.XA[:, 2 * T:2 * T + 64], self.FA[:, 64:128]),
                    waits=(("gc_i1", P.cnt("gc_i1")),
                           ("gc_m1", P.cnt("gc_m1"))), inc="gc_i2")
                P.do("scalar", lambda e, B_=B_: e.activation(
                    self.FA[:, 128:192], B_, AF.Exp, scale=-1.0),
                    waits=(("gc_i3", P.cnt("gc_i3")),), inc="gc_e2")
                P.do("vector", lambda e, t0=t0: e.tensor_tensor(
                    self.XA[:, 2 * T + 64:2 * T + 128],
                    self.XA[:, T + t0:T + t0 + 64],
                    self.FA[:, 128:192], ALU.mult),
                    waits=(("gc_e2", P.cnt("gc_e2")),
                           ("gc_m1", P.cnt("gc_m1"))), inc="gc_i3")
                P.do("vector", lambda e, B_=B_: e.tensor_scalar(
                    self.FA[:, 192:256], B_, -1.0, B_[:, 63:64],
                    ALU.mult, ALU.add),
                    waits=(("gc_e3", P.cnt("gc_e3")),), inc="gc_i4")
                P.do("scalar", lambda e: e.activation(
                    self.FA[:, 192:256], self.FA[:, 192:256], AF.Exp),
                    waits=(("gc_i4", P.cnt("gc_i4")),), inc="gc_e3")
                P.do("vector", lambda e, t0=t0: e.tensor_tensor(
                    self.XA[:, 2 * T + 128:2 * T + 192],
                    self.XA[:, T + t0:T + t0 + 64],
                    self.FA[:, 192:256], ALU.mult),
                    waits=(("gc_e3", P.cnt("gc_e3")),
                           ("gc_kt", P.cnt("gc_kt"))), inc="gc_i5")
                P.do("tensor", lambda e: e.matmul(
                    self.PS[0][0:64, 0:64], self.XA[:, 2 * T + 64:2 * T + 128],
                    self.XA[:, 2 * T:2 * T + 64], start=True, stop=True),
                    waits=(("gc_i5", P.cnt("gc_i5")),
                           ("gc_i2", P.cnt("gc_i2")),
                           ("gc_at", P.cnt("gc_at"))),
                    inc="gc_m1")
                P.do("scalar", lambda e: e.activation(
                    self.FA[0:64, 256:320], self.PS[0][0:64, 0:64],
                    AF.Copy),
                    waits=(("gc_m1", P.cnt("gc_m1")),
                           ("gc_msk", P.cnt("gc_msk"))), inc="gc_at")
                P.do("gpsimd", lambda e: e.affine_select(
                    self.XA[0:64, 2 * T + 192:2 * T + 256],
                    self.FA[0:64, 256:320],
                    [[1, 64]], ALU.is_ge, self._fill0_reg,
                    base=0, channel_multiplier=-1),
                    waits=(("gc_at", P.cnt("gc_at")),
                           ("gc_m2", P.cnt("gc_m2"))),
                    inc="gc_msk")
                for half in range(2):
                    P.do("tensor", lambda e, half=half, t0=t0: e.matmul(
                        self.PS[1][0:64, half * 128:(half + 1) * 128],
                        self.XB[:, half * T + t0:half * T + t0 + 64],
                        self.IDENTB[:, :], start=True, stop=True),
                        waits=(("gc_vte", P.cnt("gc_vte")),)
                        if half == 0 else (),
                        inc="gc_vt" if half == 1 else None)
                P.do("vector", lambda e: e.tensor_copy(
                    self.XB[0:64, 2 * T:2 * T + 256], self.PS[1][0:64, 0:256]),
                    waits=(("gc_vt", P.cnt("gc_vt")),
                           ("gc_m2", P.cnt("gc_m2")),
                           ("gc_m3", P.cnt("gc_m3"))), inc="gc_vte")
                P.do("tensor", lambda e: e.matmul(
                    self.PS[2][0:64, 0:128], self.XA[:, 2 * T + 128:2 * T + 192],
                    self.IDENTB[:, :], start=True, stop=True),
                    waits=(("gc_kte", P.cnt("gc_kte")),), inc="gc_kt")
                P.do("vector", lambda e: e.tensor_copy(
                    self.XB[0:64, 2 * T + 256:2 * T + 384],
                    self.PS[2][0:64, 0:128]),
                    waits=(("gc_kt", P.cnt("gc_kt")),
                           ("gc_m3", P.cnt("gc_m3"))), inc="gc_kte")
                P.do("tensor", lambda e: e.matmul(
                    self.PS[3][0:64, 0:256], self.XA[0:64, 2 * T + 192:2 * T + 256],
                    self.XB[0:64, 2 * T:2 * T + 256], start=True, stop=False),
                    waits=(("gc_msk", P.cnt("gc_msk")),
                           ("gc_vte", P.cnt("gc_vte")),
                           ("gc_oev", P.cnt("gc_oev"))),
                    inc=None)
                P.do("tensor", lambda e: e.matmul(
                    self.PS[3][0:64, 0:256], self.FA[:, 64:128],
                    self.FC[:, 0:256], start=False, stop=True),
                    waits=(("gc_sup", P.cnt("gc_sup")),
                           ("gc_s0", P.cnt("gc_s0"))),
                    inc="gc_m2")
                P.do("tensor", lambda e: e.matmul(
                    self.PS[4][:, 0:256], self.XB[0:64, 2 * T + 256:2 * T + 384],
                    self.XB[0:64, 2 * T:2 * T + 256], start=True, stop=True),
                    waits=(("gc_kte", P.cnt("gc_kte")),
                           ("gc_sup", P.cnt("gc_sup"))),
                    inc="gc_m3")
                P.do("scalar", lambda e, B_=B_: e.activation(
                    self.FA[:, 320:321], B_[:, 63:64], AF.Exp),
                    waits=(("gc_sup", P.cnt("gc_sup")),), inc="gc_e4")
                P.do("vector", lambda e: e.tensor_scalar(
                    self.FC[:, 0:256], self.FC[:, 0:256],
                    self.FA[:, 320:321], None, ALU.mult),
                    waits=(("gc_e4", P.cnt("gc_e4")),
                           ("gc_m2", P.cnt("gc_m2"))),
                    inc="gc_sc")
                P.do("vector", lambda e: e.tensor_tensor(
                    self.FC[:, 0:256], self.FC[:, 0:256],
                    self.PS[4][:, 0:256], ALU.add),
                    waits=(("gc_sc", P.cnt("gc_sc")),
                           ("gc_m3", P.cnt("gc_m3"))),
                    inc="gc_sup")
                P.do("scalar", lambda e: e.activation(
                    self.FA[0:64, 512:768], self.PS[3][0:64, 0:256], AF.Copy),
                    waits=(("gc_m2", P.cnt("gc_m2")),
                           ("gc_og", P.cnt("gc_og"))),
                    inc="gc_oev")
                P.do("vector", lambda e: e.scalar_tensor_tensor(
                    self.FA[0:64, 768:1024], self.FA[0:64, 512:768], 1.0,
                    self.FA[0:64, 512:768], ALU.mult, ALU.mult,
                    accum_out=self.ROWS[0:64, 0:1]),
                    waits=(("gc_oev", P.cnt("gc_oev")),), inc="gc_ov")
                P.do("scalar", lambda e: e.activation(
                    self.ROWS[0:64, 1:2], self.ROWS[0:64, 0:1], AF.Ln,
                    bias=self.const(1e-6)[0:64, :], scale=1.0 / 256.0),
                    waits=(("gc_ov", P.cnt("gc_ov")),), inc="gc_oc")
                P.do("scalar", lambda e: e.activation(
                    self.ROWS[0:64, 2:3], self.ROWS[0:64, 1:2], AF.Exp,
                    scale=-0.5),
                    waits=(("gc_oc", P.cnt("gc_oc")),), inc="gc_oc")
                P.do("vector", lambda e: e.tensor_scalar(
                    self.FA[0:64, 512:768], self.FA[0:64, 512:768],
                    self.ROWS[0:64, 2:3], None, ALU.mult),
                    waits=(("gc_oc", P.cnt("gc_oc")),), inc="gc_ov")
                P.do("vector", lambda e: e.tensor_tensor(
                    self.XB[0:64, 2 * T + 384:2 * T + 640],
                    self.FA[0:64, 512:768], self.FC[0:64, 256:512],
                    ALU.mult),
                    waits=(("gc_ov", P.cnt("gc_ov")),
                           ("gc_ot", P.cnt("gc_ot"))), inc="gc_og")
                for half in range(2):
                    P.do("tensor", lambda e, half=half: e.matmul(
                        self.PS[5][:, half * 64:(half + 1) * 64],
                        self.XB[0:64, 2 * T + 384 + half * 128:
                                2 * T + 384 + (half + 1) * 128],
                        self.IDENTB[0:64, 0:64], start=True, stop=True),
                        waits=(("gc_ote", P.cnt("gc_ote")),
                               ("gc_og", P.cnt("gc_og")))
                        if half == 0 else (),
                        inc="gc_ot" if half == 1 else None)
                P.do("vector", lambda e: e.tensor_copy(
                    self.OUTS[:, 0:128], self.PS[5][:, 0:128]),
                    waits=(("gc_ot", P.cnt("gc_ot")),
                           ("gc_ost", P.cnt("gc_ost"))),
                    inc="gc_ote")
                P.do("sync", lambda e, hd=hd, t0=t0: e.dma_start(
                    out=o_fm[hd * 256:hd * 256 + 256, t0:t0 + 64]
                    .rearrange("(a p) t -> p a t", p=128),
                    in_=self.OUTS[:, 0:128]
                    .rearrange("p (a t) -> p a t", t=64)),
                    waits=(("gc_ote", P.cnt("gc_ote")),),
                    inc="gc_ost", inc_n=16)
                if hd == 0 and c == 0:
                    dbgb = self.scratch("gla_dbg", (128, 192 + 256))
                    P.do("sync", lambda e: e.dma_start(
                        out=dbgb[:, 0:64], in_=self.FA[:, 4096:4160]),
                        waits=(("gc_b", P.cnt("gc_b")),),
                        inc="gc_dbg", inc_n=16)
                    P.do("gpsimd", lambda e: e.dma_start(
                        out=dbgb[0:64, 64:128],
                        in_=self.XA[0:64, 2 * T + 192:2 * T + 256]),
                        waits=(("gc_msk", P.cnt("gc_msk")),),
                        inc="gc_dbg", inc_n=16)
                    P.do("sync", lambda e: e.dma_start(
                        out=dbgb[0:64, 192:448],
                        in_=self.FA[0:64, 512:768]),
                        waits=(("gc_oev", P.cnt("gc_oev")),),
                        inc="gc_dbg", inc_n=16)
                    P.do("gpsimd", lambda e: e.dma_start(
                        out=dbgb[64:128, 64:192],
                        in_=self.XB[0:64, 2 * T:2 * T + 128]),
                        waits=(("gc_vte", P.cnt("gc_vte")),),
                        inc="gc_dbg", inc_n=16)
                    dbg2 = self.scratch("gla_dbg2", (128, 512))
                    P.do("gpsimd", lambda e: e.dma_start(
                        out=dbg2[0:64, 0:256],
                        in_=self.XB[0:64, 2 * T + 384:2 * T + 640]),
                        waits=(("gc_og", P.cnt("gc_og")),),
                        inc="gc_dbg", inc_n=16)
                    P.do("sync", lambda e: e.dma_start(
                        out=dbg2[0:64, 256:512],
                        in_=self.FC[0:64, 256:512]),
                        waits=(("gc_og", P.cnt("gc_og")),),
                        inc="gc_dbg", inc_n=16)
                    P.wait("gpsimd", ("gc_dbg", P.cnt("gc_dbg")))
                    P.wait("sync", ("gc_dbg", P.cnt("gc_dbg")))
            P.barrier()
        P.barrier()

        o_pre = self.scratch("g1_opre", (1024, T), BF16)
        P.do("sync", lambda e: e.dma_start(out=o_pre[:, :], in_=o_fm[:, :]),
             inc="gp_cp", inc_n=16)
        P.wait("sync", ("gp_cp", P.cnt("gp_cp")))
        P.barrier()
        for tb in range(8):
            t0 = tb * 512
            for k in range(8):
                P.do("sync", lambda e, k=k, t0=t0: e.dma_start(
                    out=self.XA[:, k * 512:(k + 1) * 512],
                    in_=o_fm[k * 128:(k + 1) * 128, t0:t0 + 512]),
                    inc="gg_lda", inc_n=16,
                    waits=(("gg_mu", P.cnt("gg_mu")),) if k == 0 else ())
                P.do("gpsimd", lambda e, k=k, t0=t0: e.dma_start(
                    out=self.FA[:, k * 512:(k + 1) * 512],
                    in_=go_fm[k * 128:(k + 1) * 128, t0:t0 + 512]),
                    inc="gg_ldb", inc_n=16,
                    waits=(("gg_si", P.cnt("gg_si")),) if k == 0 else ())
            P.wait("scalar", "gg_lda", "gg_ldb")
            for k in range(8):
                P.do("scalar", lambda e, k=k: e.activation(
                    self.FB[:, (k % 4) * 512:(k % 4) * 512 + 512],
                    self.FA[:, k * 512:(k + 1) * 512], AF.Silu),
                    waits=(("gg_mu", max(0, P.cnt("gg_mu") - 3)),),
                    inc="gg_si")
                P.do("vector", lambda e, k=k: e.tensor_tensor(
                    self.OUTS[:, (k % 2) * 512:(k % 2) * 512 + 512],
                    self.XA[:, k * 512:(k + 1) * 512],
                    self.FB[:, (k % 4) * 512:(k % 4) * 512 + 512], ALU.mult),
                    waits=(("gg_si", P.cnt("gg_si")),
                           ("gg_st", max(0, P.cnt("gg_st") - 16))),
                    inc="gg_mu")
                P.do("sync", lambda e, k=k, t0=t0: e.dma_start(
                    out=o_fm[k * 128:(k + 1) * 128, t0:t0 + 512],
                    in_=self.OUTS[:, (k % 2) * 512:(k % 2) * 512 + 512]),
                    waits=(("gg_mu", P.cnt("gg_mu")),),
                    inc="gg_st", inc_n=16)
        P.barrier()
        self.emit_proj("g1o", o_fm, 1024, 1024, self.IN["L1_attn_Wo"],
                       ao_fm, act_is_f32=False)
        for tb in range(8):
            t0 = tb * 512
            for k in range(8):
                P.do("sync", lambda e, k=k, t0=t0: e.dma_start(
                    out=self.FA[:, k * 512:(k + 1) * 512],
                    in_=self.xn_fm[k * 128:(k + 1) * 128, t0:t0 + 512]),
                    inc="gx_lda", inc_n=16,
                    waits=(("gx_ad", P.cnt("gx_ad")),) if k == 0 else ())
                P.do("gpsimd", lambda e, k=k, t0=t0: e.dma_start(
                    out=self.FB[:, k * 512:(k + 1) * 512],
                    in_=ao_fm[k * 128:(k + 1) * 128, t0:t0 + 512]),
                    inc="gx_ldb", inc_n=16,
                    waits=(("gx_ad", P.cnt("gx_ad")),) if k == 0 else ())
            P.wait("vector", "gx_lda", "gx_ldb")
            for k in range(8):
                P.do("vector", lambda e, k=k: e.tensor_tensor(
                    self.FA[:, k * 512:(k + 1) * 512],
                    self.FA[:, k * 512:(k + 1) * 512],
                    self.FB[:, k * 512:(k + 1) * 512], ALU.add),
                    inc="gx_ad")
                P.do("sync", lambda e, k=k, t0=t0: e.dma_start(
                    out=x2_fm[k * 128:(k + 1) * 128, t0:t0 + 512],
                    in_=self.FA[:, k * 512:(k + 1) * 512]),
                    waits=(("gx_ad", P.cnt("gx_ad")),),
                    inc="gx_st", inc_n=16)
            P.wait("sync", "gx_st")
        P.barrier()
        self.emit_rms_fm("g1n", x2_fm, n_fm, 1024,
                         self.IN["L1_attn_mlp_norm_w"], "rms", eps=1e-6)
        self.emit_proj("g1mg", n_fm, 1024, 2816, self.IN["L1_attn_W_gate"],
                       mg_fm, act_is_f32=False,
                       epilogue=self.ep_bias_scale(None, 1.0, AF.Silu))
        self.emit_proj("g1mu", n_fm, 1024, 2816, self.IN["L1_attn_W_up"],
                       mu_fm, act_is_f32=False)
        for tb in range(8):
            t0 = tb * 512
            for k in range(22):
                P.do("sync", lambda e, k=k, t0=t0: e.dma_start(
                    out=self.XA[:, (k % 11) * 512:(k % 11) * 512 + 512],
                    in_=mg_fm[k * 128:(k + 1) * 128, t0:t0 + 512]),
                    inc="gm_lda", inc_n=16,
                    waits=(("gm_mu", max(0, P.cnt("gm_mu") - 10)),))
                P.do("gpsimd", lambda e, k=k, t0=t0: e.dma_start(
                    out=self.XB[:, (k % 11) * 512:(k % 11) * 512 + 512],
                    in_=mu_fm[k * 128:(k + 1) * 128, t0:t0 + 512]),
                    inc="gm_ldb", inc_n=16,
                    waits=(("gm_mu", max(0, P.cnt("gm_mu") - 10)),))
                P.wait("vector", ("gm_lda", P.cnt("gm_lda")),
                       ("gm_ldb", P.cnt("gm_ldb")))
                P.do("vector", lambda e, k=k: e.tensor_tensor(
                    self.OUTS[:, (k % 2) * 512:(k % 2) * 512 + 512],
                    self.XA[:, (k % 11) * 512:(k % 11) * 512 + 512],
                    self.XB[:, (k % 11) * 512:(k % 11) * 512 + 512],
                    ALU.mult),
                    waits=(("gm_st", max(0, P.cnt("gm_st") - 16)),),
                    inc="gm_mu")
                P.do("sync", lambda e, k=k, t0=t0: e.dma_start(
                    out=mid_fm[k * 128:(k + 1) * 128, t0:t0 + 512],
                    in_=self.OUTS[:, (k % 2) * 512:(k % 2) * 512 + 512]),
                    waits=(("gm_mu", P.cnt("gm_mu")),),
                    inc="gm_st", inc_n=16)
        P.barrier()
        def ep_addx2(b, tag, bank, oc_abs, ocw, tcols, out_view, prewaits):
            Pq = b.P
            tw = tcols[1] - tcols[0]
            gi = Pq.cnt(tag + "_evd")
            r = b.FC[:, (gi % 2) * 512:(gi % 2) * 512 + tw]
            Pq.do("gpsimd", lambda e: e.dma_start(
                out=r, in_=x2_fm[oc_abs * 128:(oc_abs + 1) * 128,
                                 tcols[0]:tcols[1]]),
                waits=((tag + "_epv", max(0, Pq.cnt(tag + "_epv") - 1)),),
                inc=tag + "_epl", inc_n=16)
            Pq.do("vector", lambda e: e.tensor_tensor(
                out_view, bank[:, 0:tw], r, ALU.add),
                waits=prewaits + ((tag + "_epl", Pq.cnt(tag + "_epl")),),
                inc=tag + "_epv")
            Pq.do("vector", lambda e: e.nop(),
                  waits=((tag + "_epv", Pq.cnt(tag + "_epv")),),
                  inc=tag + "_evd")
        self.emit_proj("g1md", mid_fm, 2816, 1024,
                       self.IN["L1_attn_W_down"], self.s_fm,
                       act_is_f32=False, o_slab=256, t_block=512,
                       act_resident=False, epilogue=ep_addx2)

    def emit_l0_attn(self):
        P = self.P
        xin = self.xn_fm
        for blk in range(2):
            out_f32 = (blk == 0)
            dst = self.scratch(f"m{blk}_out", (1024, T),
                               F32 if out_f32 else BF16) if blk == 0 \
                else self.s_fm
            self._emit_mamba_block(blk, xin, dst, out_f32)
            xin = dst

    def _emit_mamba_block(self, blk, xin_fm, dst_fm, dst_f32):
        P = self.P
        t_ = f"m{blk}"
        h_fm = self.scratch(t_ + "_h", (1024, T), BF16)
        zx_fm = self.scratch(t_ + "_zx", (4224, T), BF16)
        dt_fm = self.scratch(t_ + "_dt", (32, T))
        xc_fm = self.scratch(t_ + "_xc", (2176, T), BF16)
        dacs_scr = self.scratch(t_ + "_dacs", (32, T))
        y_fm = self.scratch(t_ + "_y", (2048, T), BF16)
        yg_fm = self.scratch(t_ + "_yg", (2048, T), BF16)
        ipw = self.tw[f"l0_inproj{blk}"]
        opw = self.tw[f"l0_outproj{blk}"]

        self.emit_rms_fm(t_ + "h", xin_fm, h_fm, 1024,
                         self.IN["L0_attn_norm_w"][blk], "rms", eps=1e-5,
                         src_is_f32=True)
        self.emit_proj(t_ + "p", h_fm, 1024, 4224, ipw, zx_fm,
                       pre_transposed=True, act_is_f32=False)
        ln1 = self.const(1.0)
        P.do("gpsimd", lambda e: e.dma_start(
            out=self.BIAS[0:32, 16:17],
            in_=self.IN["L0_attn_dt_bias"][blk][:]),
            inc=t_ + "d_bl", inc_n=16)
        for k in range(8):
            P.do("sync", lambda e, k=k: e.dma_start(
                out=self.WA[:, k * 32:(k + 1) * 32],
                in_=ipw[k * 128:(k + 1) * 128, 4224:4256]),
                inc=t_ + "d_wl", inc_n=16)
            base = self.XA if k < 4 else self.XB
            P.do("gpsimd", lambda e, k=k, base=base: e.dma_start(
                out=base[:, (k % 4) * T:(k % 4) * T + T],
                in_=h_fm[k * 128:(k + 1) * 128, :]),
                inc=t_ + "d_hl", inc_n=16)
        P.wait("tensor", t_ + "d_wl", t_ + "d_hl")
        for tb in range(8):
            t0 = tb * 512
            for k in range(8):
                base = self.XA if k < 4 else self.XB
                P.do("tensor", lambda e, k=k, t0=t0, base=base: e.matmul(
                    self.PS[5][0:32, 0:512], self.WA[:, k * 32:(k + 1) * 32],
                    base[:, (k % 4) * T + t0:(k % 4) * T + t0 + 512],
                    start=(k == 0), stop=(k == 7)),
                    waits=((t_ + "d_sv", P.cnt(t_ + "d_sv")),)
                    if k == 0 else (),
                    inc=t_ + "d_mm" if k == 7 else None)
            P.do("scalar", lambda e: e.activation(
                self.FA[0:32, 0:512], self.PS[5][0:32, 0:512], AF.Identity,
                bias=self.BIAS[0:32, 16:17]),
                waits=((t_ + "d_mm", P.cnt(t_ + "d_mm")),
                       (t_ + "d_bl", P.cnt(t_ + "d_bl")),
                       (t_ + "d_sv", P.cnt(t_ + "d_sv"))),
                inc=t_ + "d_sv")
            P.do("scalar", lambda e: e.activation(
                self.FA[0:32, 512:1024], self.FA[0:32, 0:512], AF.Abs),
                waits=((t_ + "d_sv", P.cnt(t_ + "d_sv")),), inc=t_ + "d_sv")
            P.do("scalar", lambda e: e.activation(
                self.FA[0:32, 512:1024], self.FA[0:32, 512:1024], AF.Exp,
                scale=-1.0),
                waits=((t_ + "d_sv", P.cnt(t_ + "d_sv")),), inc=t_ + "d_sv")
            P.do("scalar", lambda e: e.activation(
                self.FA[0:32, 512:1024], self.FA[0:32, 512:1024], AF.Ln,
                bias=ln1[0:32, :]),
                waits=((t_ + "d_sv", P.cnt(t_ + "d_sv")),), inc=t_ + "d_sv")
            P.do("vector", lambda e: e.tensor_scalar(
                self.FA[0:32, 0:512], self.FA[0:32, 0:512], 0.0, None,
                ALU.max),
                waits=((t_ + "d_sv", P.cnt(t_ + "d_sv")),), inc=t_ + "d_vv")
            P.do("vector", lambda e: e.tensor_tensor(
                self.FA[0:32, 0:512], self.FA[0:32, 0:512],
                self.FA[0:32, 512:1024], ALU.add),
                waits=((t_ + "d_vv", P.cnt(t_ + "d_vv")),), inc=t_ + "d_vv")
            P.do("sync", lambda e, t0=t0: e.dma_start(
                out=dt_fm[:, t0:t0 + 512], in_=self.FA[0:32, 0:512]),
                waits=((t_ + "d_vv", P.cnt(t_ + "d_vv")),),
                inc=t_ + "d_st", inc_n=16)
            P.wait("scalar", (t_ + "d_st", P.cnt(t_ + "d_st")))
        P.barrier()
        for c in range(17):
            P.do("gpsimd", lambda e, c=c: e.dma_start(
                out=self.BIAS[:, 20:24],
                in_=self.IN["L0_attn_conv_w"][blk][c * 128:(c + 1) * 128, :]),
                inc=t_ + "c_wl", inc_n=16,
                waits=((t_ + "c_ac", P.cnt(t_ + "c_ac")),))
            P.do("gpsimd", lambda e, c=c: e.dma_start(
                out=self.BIAS[:, 24:25],
                in_=self.IN["L0_attn_conv_b"][blk][c * 128:(c + 1) * 128]),
                inc=t_ + "c_wl", inc_n=16)
            for tb in range(8):
                t0 = tb * 512
                if tb == 0:
                    P.do("vector", lambda e: e.memset(
                        self.XA[:, 0:3], 0.0), inc=t_ + "c_ms",
                        waits=((t_ + "c_ac", P.cnt(t_ + "c_ac")),))
                    P.do("sync", lambda e, c=c: e.dma_start(
                        out=self.XA[:, 3:515],
                        in_=zx_fm[2048 + c * 128:2048 + (c + 1) * 128,
                                  0:512]),
                        inc=t_ + "c_ld", inc_n=16)
                else:
                    P.do("sync", lambda e, c=c, t0=t0: e.dma_start(
                        out=self.XA[:, 0:515],
                        in_=zx_fm[2048 + c * 128:2048 + (c + 1) * 128,
                                  t0 - 3:t0 + 512]),
                        inc=t_ + "c_ld", inc_n=16,
                        waits=((t_ + "c_ac", P.cnt(t_ + "c_ac")),))
                P.wait("vector", (t_ + "c_ld", P.cnt(t_ + "c_ld")),
                       (t_ + "c_wl", P.cnt(t_ + "c_wl")),
                       (t_ + "c_ms", P.cnt(t_ + "c_ms")))
                P.do("vector", lambda e: e.tensor_scalar(
                    self.FA[:, 0:512], self.XA[:, 3:515],
                    self.BIAS[:, 23:24], self.BIAS[:, 24:25],
                    ALU.mult, ALU.add), inc=t_ + "c_ac")
                for i in range(3):
                    P.do("vector", lambda e, i=i: e.scalar_tensor_tensor(
                        self.FA[:, 0:512], self.XA[:, i:512 + i],
                        self.BIAS[:, 20 + i:21 + i], self.FA[:, 0:512],
                        ALU.mult, ALU.add),
                        waits=((t_ + "c_ac", P.cnt(t_ + "c_ac")),),
                        inc=t_ + "c_ac")
                P.do("scalar", lambda e: e.activation(
                    self.OUTS[:, 0:512], self.FA[:, 0:512], AF.Silu),
                    waits=((t_ + "c_ac", P.cnt(t_ + "c_ac")),
                           (t_ + "c_st", P.cnt(t_ + "c_st"))),
                    inc=t_ + "c_si")
                P.do("sync", lambda e, c=c, t0=t0: e.dma_start(
                    out=xc_fm[c * 128:(c + 1) * 128, t0:t0 + 512],
                    in_=self.OUTS[:, 0:512]),
                    waits=((t_ + "c_si", P.cnt(t_ + "c_si")),),
                    inc=t_ + "c_st", inc_n=16)
        P.barrier()
        P.do("gpsimd", lambda e: e.dma_start(
            out=self.BIAS[0:32, 17:18], in_=self.IN["L0_attn_A_log"][blk][:]),
            inc=t_ + "s_al", inc_n=16)
        P.do("scalar", lambda e: e.activation(
            self.BIAS[0:32, 18:19], self.BIAS[0:32, 17:18], AF.Exp),
            waits=((t_ + "s_al", P.cnt(t_ + "s_al")),), inc=t_ + "s_ac")
        P.do("vector", lambda e: e.tensor_scalar(
            self.BIAS[0:32, 18:19], self.BIAS[0:32, 18:19], -1.0, None,
            ALU.mult),
            waits=((t_ + "s_ac", P.cnt(t_ + "s_ac")),), inc=t_ + "s_av")
        P.do("sync", lambda e: e.dma_start(
            out=self.COEFR[0:1, 0:32], in_=self.IN["L0_attn_D"][blk][:]),
            inc=t_ + "s_dl", inc_n=16)
        P.do("tensor", lambda e: e.matmul(
            self.PS[0][:, 0:32], self.ONESROWF[0:1, :],
            self.COEFR[0:1, 0:32], start=True, stop=True),
            waits=(t_ + "s_dl",), inc=t_ + "s_db")
        P.do("vector", lambda e: e.tensor_copy(
            self.FB[:, 384:416], self.PS[0][:, 0:32]),
            waits=((t_ + "s_db", P.cnt(t_ + "s_db")),), inc=t_ + "s_dbe")
        P.do("vector", lambda e: e.memset(self.FC[0:64, 0:2048], 0.0),
             inc=t_ + "s_p0")
        P.barrier()
        for z in range(16):
            tz = z * 256
            for c in range(16):
                P.do("sync", lambda e, c=c, tz=tz: e.dma_start(
                    out=self.XA[:, c * 256:(c + 1) * 256],
                    in_=xc_fm[c * 128:(c + 1) * 128, tz:tz + 256]),
                    inc=t_ + "z_ld", inc_n=16,
                    waits=((t_ + "z_tp", P.cnt(t_ + "z_tp")),)
                    if c == 0 else ())
            P.do("sync", lambda e, tz=tz: e.dma_start(
                out=self.XA[0:64, 4096:4352],
                in_=xc_fm[2048:2112, tz:tz + 256]),
                inc=t_ + "z_ld", inc_n=16)
            P.do("sync", lambda e, tz=tz: e.dma_start(
                out=self.XA[0:64, 4352:4608],
                in_=xc_fm[2112:2176, tz:tz + 256]),
                inc=t_ + "z_ld", inc_n=16)
            P.do("gpsimd", lambda e, tz=tz: e.dma_start(
                out=self.FA[0:64, 1024:1280],
                in_=xc_fm[2112:2176, tz:tz + 256]),
                inc=t_ + "z_lc", inc_n=16,
                waits=((t_ + "z_om", P.cnt(t_ + "z_om")),))
            P.do("sync", lambda e, tz=tz: e.dma_start(
                out=self.FA[0:32, 0:256], in_=dt_fm[:, tz:tz + 256]),
                inc=t_ + "z_ld", inc_n=16)
            P.wait("vector", (t_ + "z_ld", P.cnt(t_ + "z_ld")),
                   (t_ + "s_av", P.cnt(t_ + "s_av")))
            P.do("vector", lambda e: e.tensor_scalar(
                self.FA[0:32, 256:512], self.FA[0:32, 0:256],
                self.BIAS[0:32, 18:19], None, ALU.mult),
                waits=((t_ + "z_sc", P.cnt(t_ + "z_sc")),),
                inc=t_ + "z_da")
            P.do("vector", lambda e: e.tensor_tensor_scan(
                self.FA[0:32, 256:512], self.FA[0:32, 256:512],
                self.FA[0:32, 256:512], 0.0, ALU.add, ALU.bypass),
                waits=((t_ + "z_da", P.cnt(t_ + "z_da")),),
                inc=t_ + "z_sc")
            P.do("sync", lambda e, tz=tz: e.dma_start(
                out=dacs_scr[:, tz:tz + 256], in_=self.FA[0:32, 256:512]),
                waits=((t_ + "z_sc", P.cnt(t_ + "z_sc")),),
                inc=t_ + "z_dst", inc_n=16)
            P.wait("tensor", (t_ + "z_dst", P.cnt(t_ + "z_dst")))
            for sub in range(2):
                P.do("tensor", lambda e, sub=sub: e.matmul(
                    self.PS[7][:, sub * 64:sub * 64 + 32],
                    self.FA[0:32, 256 + sub * 128:256 + (sub + 1) * 128],
                    self.IDENT[0:32, 0:32], start=True, stop=True),
                    waits=((t_ + "z_tte", P.cnt(t_ + "z_tte")),)
                    if sub == 0 else (),
                    inc=None)
                P.do("tensor", lambda e, sub=sub: e.matmul(
                    self.PS[7][:, 128 + sub * 64:128 + sub * 64 + 32],
                    self.FA[0:32, sub * 128:(sub + 1) * 128],
                    self.IDENT[0:32, 0:32], start=True, stop=True),
                    inc=(t_ + "z_tt") if sub == 1 else None)
            P.do("vector", lambda e: e.tensor_copy(
                self.FB[:, 0:32], self.PS[7][:, 0:32]),
                waits=((t_ + "z_tt", P.cnt(t_ + "z_tt")),), inc=t_ + "z_tte")
            P.do("vector", lambda e: e.tensor_copy(
                self.FB[:, 32:64], self.PS[7][:, 64:96]),
                inc=t_ + "z_tte")
            P.do("vector", lambda e: e.tensor_copy(
                self.FB[:, 128:160], self.PS[7][:, 128:160]),
                inc=t_ + "z_tte")
            P.do("vector", lambda e: e.tensor_copy(
                self.FB[:, 160:192], self.PS[7][:, 192:224]),
                inc=t_ + "z_tte")
            P.do("vector", lambda e: e.tensor_scalar(
                self.FB[:, 64:128], self.FB[:, 0:64], -1.0, None, ALU.mult),
                waits=((t_ + "z_tte", P.cnt(t_ + "z_tte")),),
                inc=t_ + "z_pre")
            P.do("scalar", lambda e: e.activation(
                self.FB[:, 192:256], self.FB[:, 0:64], AF.Exp),
                waits=((t_ + "z_tte", P.cnt(t_ + "z_tte")),),
                inc=t_ + "z_pre")
            def load_dalast(e, tz=tz):
                with self.nc.allow_non_contiguous_dma("tiny row"):
                    return e.dma_start(
                        out=self.COEFR[0:1, 4064:4096],
                        in_=dacs_scr[:, tz + 255])
            P.do("sync", load_dalast, waits=((t_ + "z_dst", P.cnt(t_ + "z_dst")),),
                 inc=t_ + "z_dll", inc_n=16)
            P.do("tensor", lambda e: e.matmul(
                self.PS[0][:, 0:32], self.ONESROWF[0:1, :],
                self.COEFR[0:1, 4064:4096], start=True, stop=True),
                waits=((t_ + "z_dll", P.cnt(t_ + "z_dll")),
                       (t_ + "z_dle", P.cnt(t_ + "z_dle")),
                       (t_ + "z_ye2", P.cnt(t_ + "z_ye2"))),
                inc=t_ + "z_dlb")
            P.do("vector", lambda e: e.tensor_copy(
                self.FB[:, 256:288], self.PS[0][:, 0:32]),
                waits=((t_ + "z_dlb", P.cnt(t_ + "z_dlb")),),
                inc=t_ + "z_dle")
            P.barrier()
            for c in range(16):
                for sub in range(2):
                    bank = self.PS[2 + ((c * 2 + sub) % 2)]
                    P.wait("tensor",
                           (t_ + "z_xte", P.cnt(t_ + "z_xte")))
                    P.do("tensor", lambda e, c=c, sub=sub, bank=bank:
                         e.matmul(
                             bank[:, 0:128],
                             self.XA[:, c * 256 + sub * 128:
                                     c * 256 + (sub + 1) * 128],
                             self.IDENTB[:, :], start=True, stop=True),
                         inc=t_ + "z_xt")
                    P.do("vector", lambda e, c=c, sub=sub, bank=bank:
                         e.tensor_copy(
                             self.XB[:, sub * 2048 + c * 128:
                                     sub * 2048 + (c + 1) * 128],
                             bank[:, 0:128]),
                         waits=((t_ + "z_xt", P.cnt(t_ + "z_xt")),),
                         inc=t_ + "z_xte")
            for sub in range(2):
                P.do("tensor", lambda e, sub=sub: e.matmul(
                    self.PS[4][:, sub * 64:(sub + 1) * 64],
                    self.XA[0:64, 4096 + sub * 128:4096 + (sub + 1) * 128],
                    self.IDENTB[0:64, 0:64], start=True, stop=True),
                    waits=((t_ + "z_bte", P.cnt(t_ + "z_bte")),)
                    if sub == 0 else (),
                    inc=(t_ + "z_bt") if sub == 1 else None)
            P.do("vector", lambda e: e.tensor_copy(
                self.XB[:, 4096:4224].rearrange("p (s c) -> p s c", c=64),
                self.PS[4][:, 0:128].rearrange("p (s c) -> p s c", c=64)),
                waits=((t_ + "z_bt", P.cnt(t_ + "z_bt")),), inc=t_ + "z_bte")
            for sub in range(2):
                bank = self.PS[2 + sub]
                P.do("tensor", lambda e, sub=sub, bank=bank: e.matmul(
                    bank[:, 0:256],
                    self.XA[0:64, 4096 + sub * 128:4096 + (sub + 1) * 128],
                    self.XA[0:64, 4352:4608], start=True, stop=True),
                    waits=((t_ + "z_xte", P.cnt(t_ + "z_xte")),
                           (t_ + "z_gte", P.cnt(t_ + "z_gte")))
                    if sub == 0 else (),
                    inc=(t_ + "z_gt") if sub == 1 else None)
            for sub in range(2):
                P.do("vector", lambda e, sub=sub: e.tensor_copy(
                    self.XA[:, 4608 + sub * 256:4608 + (sub + 1) * 256],
                    self.PS[2 + sub][:, 0:256]),
                    waits=((t_ + "z_gt", P.cnt(t_ + "z_gt")),),
                    inc=t_ + "z_gte")
            P.do("sync", lambda e, tz=tz: e.dma_start(
                out=self.COEFR[0:1, 0:4096]
                .rearrange("p (h t) -> p h t", t=256)[:, 0:16, :],
                in_=dacs_scr[0:16, tz:tz + 256]),
                waits=((t_ + "z_rb", P.cnt(t_ + "z_rb")),
                       (t_ + "z_dst", P.cnt(t_ + "z_dst"))),
                inc=t_ + "z_rl", inc_n=16)
            P.barrier(("tensor", "vector", "scalar", "sync", "gpsimd"))
            for hd in range(32):
                if hd % 16 == 0:
                    g = hd // 16
                    if g == 1:
                        P.do("sync", lambda e, tz=tz: e.dma_start(
                            out=self.COEFR[0:1, 0:4096]
                            .rearrange("p (h t) -> p h t", t=256)[:, 0:16, :],
                            in_=dacs_scr[16:32, tz:tz + 256]),
                            waits=((t_ + "z_rb", P.cnt(t_ + "z_rb")),),
                            inc=t_ + "z_rl", inc_n=16)
                    P.wait("tensor", (t_ + "z_rl", P.cnt(t_ + "z_rl")))
                hh = hd % 16
                dacs_row = self.COEFR[0:1, hh * 256:(hh + 1) * 256]
                nA0 = self.FB[:, 64 + hd:65 + hd] if False else None
                sub_col = lambda base, sub, hd=hd: \
                    self.FB[:, base + sub * 32 + hd % 32:
                            base + sub * 32 + hd % 32 + 1]
                dacscol = lambda sub: self.FB[:, sub * 32 + hd:sub * 32 + hd + 1]
                negcol = lambda sub: self.FB[:, 64 + sub * 32 + hd:
                                             64 + sub * 32 + hd + 1]
                dtcol = lambda sub: self.FB[:, 128 + sub * 32 + hd:
                                            128 + sub * 32 + hd + 1]
                eacol = lambda sub: self.FB[:, 192 + sub * 32 + hd:
                                            192 + sub * 32 + hd + 1]
                dlcol = self.FB[:, 256 + hd:257 + hd]
                dcol = self.FB[:, 288:289]
                ccol = self.FB[:, 289:290]
                P.do("vector", lambda e, hd=hd: e.tensor_copy(
                    self.BIAS[:, 25:27],
                    self.FB[:, 64 + hd:97 + hd:32]),
                    waits=((t_ + "z_pre", P.cnt(t_ + "z_pre")),
                           (t_ + "z_dx", P.cnt(t_ + "z_dx")),
                           (t_ + "z_oe", P.cnt(t_ + "z_oe"))),
                    inc=t_ + "z_cp")
                P.do("vector", lambda e, hd=hd: e.tensor_copy(
                    self.BIAS[:, 27:29],
                    self.FB[:, 192 + hd:225 + hd:32]),
                    inc=t_ + "z_cp")
                P.do("tensor", lambda e, dacs_row=dacs_row: e.matmul(
                    self.PS[1][:, 0:256], self.ONESROWF[0:1, :],
                    dacs_row, start=True, stop=True),
                    waits=((t_ + "z_dx", P.cnt(t_ + "z_dx")),),
                    inc=t_ + "z_rb")
                for sub in range(2):
                    P.do("scalar", lambda e, sub=sub: e.activation(
                        self.FA[:, 512:768], self.PS[1][:, 0:256], AF.Exp,
                        bias=self.BIAS[:, 25 + sub:26 + sub]),
                        waits=((t_ + "z_rb", P.cnt(t_ + "z_rb")),
                               (t_ + "z_mk", P.cnt(t_ + "z_mk")),
                               (t_ + "z_cp", P.cnt(t_ + "z_cp"))),
                        inc=t_ + "z_dx")
                    P.do("gpsimd", lambda e, sub=sub: e.affine_select(
                        self.XA[:, 5120:5376], self.FA[:, 512:768],
                        [[1, 256]], ALU.is_ge, self._fill0_reg,
                        base=-sub * 128, channel_multiplier=-1),
                        waits=((t_ + "z_dx", P.cnt(t_ + "z_dx")),
                               (t_ + "z_mp", P.cnt(t_ + "z_mp"))),
                        inc=t_ + "z_mk")
                    P.do("vector", lambda e, sub=sub: e.scalar_tensor_tensor(
                        self.XA[:, 5376 + sub * 256:5376 + (sub + 1) * 256],
                        self.XA[:, 4608 + sub * 256:4608 + (sub + 1) * 256],
                        dtcol(sub), self.XA[:, 5120:5376],
                        ALU.mult, ALU.mult),
                        waits=((t_ + "z_mk", P.cnt(t_ + "z_mk")),
                               (t_ + "z_ym", P.cnt(t_ + "z_ym")),
                               (t_ + "z_gte", P.cnt(t_ + "z_gte"))),
                        inc=t_ + "z_mp")
                    if blk == 0 and z == 0 and hd == 0:
                        mdbg = self.SCR.get("m_dbg") or self.scratch(
                            "m_dbg", (128, 1024))
                        P.do("sync", lambda e, sub=sub: e.dma_start(
                            out=mdbg[:, sub * 256:(sub + 1) * 256],
                            in_=self.FA[:, 512:768]),
                            waits=((t_ + "z_dx", P.cnt(t_ + "z_dx")),),
                            inc=t_ + "z_dbg", inc_n=16)
                        P.do("gpsimd", lambda e, sub=sub: e.dma_start(
                            out=mdbg[:, 512 + sub * 256:512 + (sub + 1) * 256],
                            in_=self.XA[:, 5376 + sub * 256:
                                        5376 + (sub + 1) * 256]),
                            waits=((t_ + "z_mp", P.cnt(t_ + "z_mp")),),
                            inc=t_ + "z_dbg", inc_n=16)
                        P.do("sync", lambda e: e.dma_start(
                            out=mdbg[:, 768:832], in_=self.FB[:, 0:64]),
                            inc=t_ + "z_dbg", inc_n=16)
                        P.do("sync", lambda e: e.dma_start(
                            out=mdbg[0:1, 832:1024].rearrange("p c -> p c"),
                            in_=self.COEFR[0:1, 0:192]),
                            inc=t_ + "z_dbg", inc_n=16)
                        P.wait("sync", (t_ + "z_dbg", P.cnt(t_ + "z_dbg")))
                        P.wait("gpsimd", (t_ + "z_dbg", P.cnt(t_ + "z_dbg")))
                        P.barrier()
                for sl in range(2):
                    ybank = self.PS[2 + sl]
                    obank = self.PS[4 + sl]
                    P.wait("tensor", (t_ + "z_yv", P.cnt(t_ + "z_yv")),
                        (t_ + "z_mp", P.cnt(t_ + "z_mp")))
                    for ss in range(2):
                        P.do("tensor", lambda e, sl=sl, ss=ss, hd=hd,
                             ybank=ybank: e.matmul(
                                 ybank[:, 0:64],
                                 self.XA[:, 5376 + ss * 256 + sl * 128:
                                         5376 + ss * 256 + (sl + 1) * 128],
                                 self.XB[:, ss * 2048 + hd * 64:
                                         ss * 2048 + (hd + 1) * 64],
                                 start=(ss == 0), stop=(ss == 1)),
                             inc=(t_ + "z_ym") if ss == 1 else None)
                    P.do("tensor", lambda e, sl=sl, hd=hd, obank=obank:
                         e.matmul(
                             obank[:, 0:64],
                             self.FA[0:64, 1024 + sl * 128:
                                     1024 + (sl + 1) * 128],
                             self.FC[0:64, hd * 64:(hd + 1) * 64],
                             start=True, stop=True),
                         waits=((t_ + "z_pue", P.cnt(t_ + "z_pue")),
                                (t_ + "z_oe", P.cnt(t_ + "z_oe")),
                                (t_ + "z_lc", P.cnt(t_ + "z_lc"))),
                         inc=t_ + "z_om")
                for sl in range(2):
                    ybank = self.PS[2 + sl]
                    obank = self.PS[4 + sl]
                    P.do("scalar", lambda e, sl=sl: e.activation(
                        self.FA[:, 768 + sl * 64:768 + (sl + 1) * 64],
                        obank[:, 0:64], AF.Identity,
                        scale=self.BIAS[:, 27 + sl:28 + sl]),
                        waits=((t_ + "z_om", P.cnt(t_ + "z_om")),
                               (t_ + "z_yf", P.cnt(t_ + "z_yf")),
                               (t_ + "z_cp", P.cnt(t_ + "z_cp"))),
                        inc=t_ + "z_oe")
                    P.do("vector", lambda e, sl=sl, hd=hd:
                         e.scalar_tensor_tensor(
                             self.FA[:, 896 + sl * 64:896 + (sl + 1) * 64],
                             self.XB[:, sl * 2048 + hd * 64:
                                     sl * 2048 + (hd + 1) * 64],
                             self.FB[:, 384 + hd:385 + hd],
                             ybank[:, 0:64], ALU.mult, ALU.add),
                         waits=((t_ + "z_ym", P.cnt(t_ + "z_ym")),
                                (t_ + "z_oe", P.cnt(t_ + "z_oe")),
                                (t_ + "z_yf", P.cnt(t_ + "z_yf"))),
                         inc=t_ + "z_yv")
                    P.do("vector", lambda e, sl=sl: e.tensor_tensor(
                        self.XB[:, 4224 + sl * 64:4224 + (sl + 1) * 64],
                        self.FA[:, 768 + sl * 64:768 + (sl + 1) * 64],
                        self.FA[:, 896 + sl * 64:896 + (sl + 1) * 64],
                        ALU.add),
                        waits=((t_ + "z_yv", P.cnt(t_ + "z_yv")),
                               (t_ + "z_ybt", P.cnt(t_ + "z_ybt"))),
                        inc=t_ + "z_yf")
                for sub in range(2):
                    P.do("vector", lambda e, sub=sub: e.tensor_scalar(
                        dcol, negcol(sub), 1.0, dlcol, ALU.mult, ALU.add),
                        waits=((t_ + "z_sme", P.cnt(t_ + "z_sme")),
                               (t_ + "z_dle", P.cnt(t_ + "z_dle"))),
                        inc=t_ + "z_sv")
                    P.do("scalar", lambda e: e.activation(
                        dcol, dcol, AF.Exp),
                        waits=((t_ + "z_sv", P.cnt(t_ + "z_sv")),),
                        inc=t_ + "z_se")
                    P.do("vector", lambda e, sub=sub: e.tensor_tensor(
                        ccol, dcol, dtcol(sub), ALU.mult),
                        waits=((t_ + "z_se", P.cnt(t_ + "z_se")),),
                        inc=t_ + "z_sv")
                    P.do("vector", lambda e, sub=sub: e.tensor_scalar(
                        self.XB[:, 4352 + sub * 64:4352 + (sub + 1) * 64],
                        self.XB[:, 4096 + sub * 64:4096 + (sub + 1) * 64],
                        ccol, None, ALU.mult),
                        waits=((t_ + "z_sv", P.cnt(t_ + "z_sv")),
                               (t_ + "z_sm", P.cnt(t_ + "z_sm")),
                               (t_ + "z_bte", P.cnt(t_ + "z_bte"))),
                        inc=t_ + "z_bd")
                for sub in range(2):
                    P.do("tensor", lambda e, sub=sub, hd=hd: e.matmul(
                        self.PS[6][0:64, 0:64],
                        self.XB[:, 4352 + sub * 64:4352 + (sub + 1) * 64],
                        self.XB[:, sub * 2048 + hd * 64:
                                sub * 2048 + (hd + 1) * 64],
                        start=(sub == 0), stop=(sub == 1)),
                        waits=((t_ + "z_bd", P.cnt(t_ + "z_bd")),
                               (t_ + "z_pue", P.cnt(t_ + "z_pue")))
                        if sub == 0 else (),
                        inc=(t_ + "z_sm") if sub == 1 else None)
                P.do("scalar", lambda e: e.activation(
                    self.FB[:, 290:291], dlcol, AF.Exp),
                    waits=((t_ + "z_pue", P.cnt(t_ + "z_pue")),),
                    inc=t_ + "z_px")
                P.do("vector", lambda e, hd=hd: e.tensor_scalar(
                    self.FC[0:64, hd * 64:(hd + 1) * 64],
                    self.FC[0:64, hd * 64:(hd + 1) * 64],
                    self.FB[0:64, 290:291], None, ALU.mult),
                    waits=((t_ + "z_px", P.cnt(t_ + "z_px")),
                           (t_ + "z_om", P.cnt(t_ + "z_om"))),
                    inc=t_ + "z_pu")
                P.do("vector", lambda e, hd=hd: e.tensor_tensor(
                    self.FC[0:64, hd * 64:(hd + 1) * 64],
                    self.FC[0:64, hd * 64:(hd + 1) * 64],
                    self.PS[6][0:64, 0:64], ALU.add),
                    waits=((t_ + "z_pu", P.cnt(t_ + "z_pu")),
                           (t_ + "z_sm", P.cnt(t_ + "z_sm"))),
                    inc=t_ + "z_pue")
                for sl in range(2):
                    P.do("tensor", lambda e, sl=sl: e.matmul(
                        self.PS[0][0:64, sl * 128:(sl + 1) * 128],
                        self.XB[:, 4224 + sl * 64:4224 + (sl + 1) * 64],
                        self.IDENTB[:, :], start=True, stop=True),
                        waits=((t_ + "z_yf", P.cnt(t_ + "z_yf")),
                               (t_ + "z_ye2", P.cnt(t_ + "z_ye2")),
                               (t_ + "z_dle", P.cnt(t_ + "z_dle")))
                        if sl == 0 else (),
                        inc=(t_ + "z_ybt") if sl == 1 else None)
                P.do("vector", lambda e: e.tensor_copy(
                    self.OUTS[0:64, 0:256], self.PS[0][0:64, 0:256]),
                    waits=((t_ + "z_ybt", P.cnt(t_ + "z_ybt")),
                           (t_ + "z_yst", P.cnt(t_ + "z_yst"))),
                    inc=t_ + "z_ye2")
                P.do("sync", lambda e, hd=hd, tz=tz: e.dma_start(
                    out=y_fm[hd * 64:(hd + 1) * 64, tz:tz + 256],
                    in_=self.OUTS[0:64, 0:256]),
                    waits=((t_ + "z_ye2", P.cnt(t_ + "z_ye2")),),
                    inc=t_ + "z_yst", inc_n=16)
                P.barrier()
            P.barrier()
        for tb in range(8):
            t0 = tb * 512
            for k in range(16):
                P.do("sync", lambda e, k=k, t0=t0: e.dma_start(
                    out=self.XA[:, k * 512:(k + 1) * 512],
                    in_=y_fm[k * 128:(k + 1) * 128, t0:t0 + 512]),
                    inc=t_ + "g_la", inc_n=16,
                    waits=((t_ + "g_mu", P.cnt(t_ + "g_mu")),)
                    if k == 0 else ())
                P.do("gpsimd", lambda e, k=k, t0=t0: e.dma_start(
                    out=self.XB[:, k * 512:(k + 1) * 512],
                    in_=zx_fm[k * 128:(k + 1) * 128, t0:t0 + 512]),
                    inc=t_ + "g_lb", inc_n=16,
                    waits=((t_ + "g_si", P.cnt(t_ + "g_si")),)
                    if k == 0 else ())
            P.wait("scalar", (t_ + "g_la", P.cnt(t_ + "g_la")),
                   (t_ + "g_lb", P.cnt(t_ + "g_lb")))
            for k in range(16):
                P.do("scalar", lambda e, k=k: e.activation(
                    self.FA[:, (k % 8) * 512:(k % 8) * 512 + 512],
                    self.XB[:, k * 512:(k + 1) * 512], AF.Silu),
                    waits=((t_ + "g_mu", max(0, P.cnt(t_ + "g_mu") - 7)),),
                    inc=t_ + "g_si")
                P.do("vector", lambda e, k=k: e.tensor_tensor(
                    self.OUTS[:, (k % 2) * 512:(k % 2) * 512 + 512],
                    self.XA[:, k * 512:(k + 1) * 512],
                    self.FA[:, (k % 8) * 512:(k % 8) * 512 + 512], ALU.mult),
                    waits=((t_ + "g_si", P.cnt(t_ + "g_si")),
                           (t_ + "g_st", P.cnt(t_ + "g_st"))),
                    inc=t_ + "g_mu")
                P.do("sync", lambda e, k=k, t0=t0: e.dma_start(
                    out=yg_fm[k * 128:(k + 1) * 128, t0:t0 + 512],
                    in_=self.OUTS[:, (k % 2) * 512:(k % 2) * 512 + 512]),
                    waits=((t_ + "g_mu", P.cnt(t_ + "g_mu")),),
                    inc=t_ + "g_st", inc_n=16)
        P.barrier()
        yn_fm = self.scratch(t_ + "_yn", (2048, T), BF16)
        self.emit_rms_fm(t_ + "gn", yg_fm, yn_fm, 2048,
                         self.IN["L0_attn_gnorm_w"][blk], "rms", eps=1e-5,
                         src_is_f32=False)
        def ep_addx(b, tag, bank, oc_abs, ocw, tcols, out_view, prewaits):
            Pq = b.P
            tw2 = tcols[1] - tcols[0]
            gi = Pq.cnt(tag + "_evd")
            r = b.FC[:, 1024 + (gi % 2) * 512:1024 + (gi % 2) * 512 + tw2]
            Pq.do("gpsimd", lambda e: e.dma_start(
                out=r, in_=xin_fm[oc_abs * 128:(oc_abs + 1) * 128,
                                  tcols[0]:tcols[1]]),
                waits=((tag + "_epv", max(0, Pq.cnt(tag + "_epv") - 1)),),
                inc=tag + "_epl", inc_n=16)
            Pq.do("vector", lambda e: e.tensor_tensor(
                out_view, bank[:, 0:tw2], r, ALU.add),
                waits=prewaits + ((tag + "_epl", Pq.cnt(tag + "_epl")),),
                inc=tag + "_epv")
            Pq.do("vector", lambda e: e.nop(),
                  waits=((tag + "_epv", Pq.cnt(tag + "_epv")),),
                  inc=tag + "_evd")
        self.emit_proj(t_ + "o", yn_fm, 2048, 1024, opw, dst_fm,
                       pre_transposed=True, act_is_f32=False,
                       epilogue=ep_addx, out_f32=dst_f32)

    def emit_l3_ffn(self):
        P = self.P
        ce8 = self.scratch("l3_ce8", (8, T))
        h_scr = self.scratch("l3_h", (2048, T), BF16)
        g_acc = self.scratch("l3_acc", (1024, T))
        P.do("sync", lambda e: e.dma_start(
            out=self.FA[0:8, 0:1024], in_=self.IN["L3_ffn_router_w"][:, :]),
            inc="l3r_ld", inc_n=16)
        P.wait("tensor", "l3r_ld")
        for k in range(8):
            P.do("tensor", lambda e, k=k: e.matmul(
                self.PS[0][:, k * 8:(k + 1) * 8],
                self.FA[0:8, k * 128:(k + 1) * 128],
                self.IDENT[0:8, 0:8], start=True, stop=True),
                inc="l3r_mm" if k == 7 else None)
        P.do("vector", lambda e: e.tensor_copy(
            self.FB[:, 0:64], self.PS[0][:, 0:64]),
            waits=("l3r_mm",), inc="l3r_ev")
        P.do("vector", lambda e: e.tensor_copy(
            self.WB[:, 0:64], self.FB[:, 0:64]),
            waits=(("l3r_ev", P.cnt("l3r_ev")),), inc="l3r_ev")
        for k in range(8):
            base = self.XA if k < 4 else self.XB
            P.do("gpsimd", lambda e, k=k, base=base: e.dma_start(
                out=base[:, (k % 4) * T:(k % 4) * T + T],
                in_=self.xn_fm[k * 128:(k + 1) * 128, :]),
                inc="l3r_ald", inc_n=16)
        P.wait("tensor", "l3r_ev", "l3r_ald")
        for tt in range(32):
            P.wait("tensor", ("l3r_x", P.cnt("l3r_x")))
            for k in range(8):
                base = self.XA if k < 4 else self.XB
                P.do("tensor", lambda e, k=k, tt=tt, base=base: e.matmul(
                    self.PS[1][:, 0:8],
                    base[:, (k % 4) * T + tt * 128:
                         (k % 4) * T + (tt + 1) * 128],
                    self.WB[:, k * 8:(k + 1) * 8],
                    start=(k == 0), stop=(k == 7)),
                    inc="l3r_lm" if k == 7 else None)
            P.do("vector", lambda e: e.reduce_max(
                self.ROWS[:, 0:1], self.PS[1][:, 0:8], axis=AX.X,
                negate=True),
                waits=(("l3r_lm", P.cnt("l3r_lm")),), inc="l3r_v")
            P.do("scalar", lambda e: e.activation(
                self.FA[:, 0:8], self.PS[1][:, 0:8], AF.Exp,
                bias=self.ROWS[:, 0:1], accum_out=self.ROWS[:, 1:2]),
                waits=(("l3r_v", P.cnt("l3r_v")),), inc="l3r_x")
            P.do("vector", lambda e: e.reciprocal(
                self.ROWS[:, 2:3], self.ROWS[:, 1:2]),
                waits=(("l3r_x", P.cnt("l3r_x")),), inc="l3r_v")
            P.do("vector", lambda e: e.tensor_scalar(
                self.FA[:, 0:8], self.FA[:, 0:8], self.ROWS[:, 2:3], None,
                ALU.mult),
                waits=(("l3r_v", P.cnt("l3r_v")),), inc="l3r_v")
            P.do("vector", lambda e: e.max(
                self.FA[:, 8:16], self.FA[:, 0:8]),
                waits=(("l3r_v", P.cnt("l3r_v")),), inc="l3r_v")
            P.do("vector", lambda e: e.tensor_tensor(
                self.ROWS[:, 3:4], self.FA[:, 8:9], self.FA[:, 9:10],
                ALU.add),
                waits=(("l3r_v", P.cnt("l3r_v")),), inc="l3r_v")
            P.do("vector", lambda e: e.tensor_scalar(
                self.ROWS[:, 3:4], self.ROWS[:, 3:4], 1e-6, None, ALU.add),
                waits=(("l3r_v", P.cnt("l3r_v")),), inc="l3r_v")
            P.do("vector", lambda e: e.reciprocal(
                self.ROWS[:, 4:5], self.ROWS[:, 3:4]),
                waits=(("l3r_v", P.cnt("l3r_v")),), inc="l3r_v")
            P.do("vector", lambda e: e.tensor_scalar(
                self.ROWS[:, 5:6], self.FA[:, 8:9], self.ROWS[:, 4:5], None,
                ALU.mult),
                waits=(("l3r_v", P.cnt("l3r_v")),), inc="l3r_v")
            P.do("vector", lambda e: e.tensor_scalar(
                self.ROWS[:, 6:7], self.FA[:, 9:10], self.ROWS[:, 4:5], None,
                ALU.mult),
                waits=(("l3r_v", P.cnt("l3r_v")),), inc="l3r_v")
            P.do("vector", lambda e: e.tensor_scalar(
                self.FA[:, 16:24], self.FA[:, 0:8], self.FA[:, 8:9],
                self.ROWS[:, 5:6], ALU.is_equal, ALU.mult),
                waits=(("l3r_v", P.cnt("l3r_v")),), inc="l3r_v")
            P.do("vector", lambda e: e.tensor_scalar(
                self.FA[:, 24:32], self.FA[:, 0:8], self.FA[:, 9:10],
                self.ROWS[:, 6:7], ALU.is_equal, ALU.mult),
                waits=(("l3r_v", P.cnt("l3r_v")),), inc="l3r_v")
            P.do("vector", lambda e: e.tensor_tensor(
                self.FA[:, 16:24], self.FA[:, 16:24], self.FA[:, 24:32],
                ALU.add),
                waits=(("l3r_v", P.cnt("l3r_v")),), inc="l3r_v")
            P.do("tensor", lambda e: e.matmul(
                self.PS[2][0:8, 0:128], self.FA[:, 16:24],
                self.IDENT[:, :], start=True, stop=True),
                waits=(("l3r_v", P.cnt("l3r_v")),
                       ("l3r_ce", P.cnt("l3r_ce"))),
                inc="l3r_ct")
            P.do("vector", lambda e: e.tensor_copy(
                self.FB[0:8, 64:192], self.PS[2][0:8, 0:128]),
                waits=(("l3r_ct", P.cnt("l3r_ct")),
                       ("l3r_cst", P.cnt("l3r_cst"))),
                inc="l3r_ce")
            P.do("sync", lambda e, tt=tt: e.dma_start(
                out=ce8[:, tt * 128:(tt + 1) * 128],
                in_=self.FB[0:8, 64:192]),
                waits=(("l3r_ce", P.cnt("l3r_ce")),),
                inc="l3r_cst", inc_n=16)
        P.barrier()
        for ex in range(8):
            w13 = self.IN["L3_ffn_w13"][ex]
            w2 = self.IN["L3_ffn_w2"][ex]
            for k in range(8):
                base = self.WA if k < 4 else self.WB
                P.do("gpsimd", lambda e, k=k, base=base, w13=w13:
                     e.dma_start(
                         out=base[:, (k % 4) * 4096:(k % 4) * 4096 + 4096],
                         in_=w13[k * 128:(k + 1) * 128, :]),
                     inc="l3w_ld", inc_n=16,
                     waits=(("l3h_mm", P.cnt("l3h_mm")),) if k == 0 else ())
            P.wait("tensor", ("l3w_ld", P.cnt("l3w_ld")))
            for tb in range(8):
                t0 = tb * 512
                for ic in range(16):
                    bg = self.PS[ic % 2]
                    bu = self.PS[2 + ic % 2]
                    P.wait("tensor",
                           ("l3h_si", max(0, P.cnt("l3h_si") - 1)),
                           ("l3h_hm", max(0, P.cnt("l3h_hm") - 1)))
                    for k in range(8):
                        xb = self.XA if k < 4 else self.XB
                        wb = self.WA if k < 4 else self.WB
                        P.do("tensor", lambda e, k=k, ic=ic, t0=t0, xb=xb,
                             wb=wb, bg=bg: e.matmul(
                                 bg[:, 0:512],
                                 wb[:, (k % 4) * 4096 + ic * 128:
                                    (k % 4) * 4096 + (ic + 1) * 128],
                                 xb[:, (k % 4) * T + t0:
                                    (k % 4) * T + t0 + 512],
                                 start=(k == 0), stop=(k == 7)),
                             inc=None)
                    for k in range(8):
                        xb = self.XA if k < 4 else self.XB
                        wb = self.WA if k < 4 else self.WB
                        P.do("tensor", lambda e, k=k, ic=ic, t0=t0, xb=xb,
                             wb=wb, bu=bu: e.matmul(
                                 bu[:, 0:512],
                                 wb[:, (k % 4) * 4096 + 2048 + ic * 128:
                                    (k % 4) * 4096 + 2048 + (ic + 1) * 128],
                                 xb[:, (k % 4) * T + t0:
                                    (k % 4) * T + t0 + 512],
                                 start=(k == 0), stop=(k == 7)),
                             inc="l3h_mm" if k == 7 else None)
                    P.do("scalar", lambda e, bg=bg: e.activation(
                        self.FC[:, 0:512], bg[:, 0:512], AF.Silu),
                        waits=(("l3h_mm", P.cnt("l3h_mm")),
                               ("l3h_hm", P.cnt("l3h_hm"))),
                        inc="l3h_si")
                    P.do("vector", lambda e, bu=bu: e.tensor_tensor(
                        self.OUTS[:, 0:512], self.FC[:, 0:512],
                        bu[:, 0:512], ALU.mult),
                        waits=(("l3h_si", P.cnt("l3h_si")),
                               ("l3h_st", P.cnt("l3h_st"))),
                        inc="l3h_hm")
                    P.do("sync", lambda e, ic=ic, t0=t0: e.dma_start(
                        out=h_scr[ic * 128:(ic + 1) * 128, t0:t0 + 512],
                        in_=self.OUTS[:, 0:512]),
                        waits=(("l3h_hm", P.cnt("l3h_hm")),),
                        inc="l3h_st", inc_n=16)
            P.barrier()
            for k in range(16):
                P.do("gpsimd", lambda e, k=k, w2=w2: e.dma_start(
                    out=self.WB[:, k * 1024:(k + 1) * 1024],
                    in_=w2[k * 128:(k + 1) * 128, :]),
                    inc="l3y_wl", inc_n=16)
            for tb in range(8):
                t0 = tb * 512
                for k in range(16):
                    P.do("sync", lambda e, k=k, t0=t0: e.dma_start(
                        out=self.WA[:, 8192 + k * 512:8192 + (k + 1) * 512],
                        in_=h_scr[k * 128:(k + 1) * 128, t0:t0 + 512]),
                        inc="l3y_hl", inc_n=16,
                        waits=(("l3y_mm", P.cnt("l3y_mm")),)
                        if k == 0 else ())
                P.do("sync", lambda e, t0=t0, ex=ex: e.dma_start(
                    out=self.COEFR[0:1, 0:512], in_=ce8[ex, t0:t0 + 512]),
                    inc="l3y_cl", inc_n=16,
                    waits=(("l3y_cb", P.cnt("l3y_cb")),))
                P.do("tensor", lambda e: e.matmul(
                    self.PS[6][:, 0:512], self.ONESROWF[0:1, :],
                    self.COEFR[0:1, 0:512], start=True, stop=True),
                    waits=(("l3y_cl", P.cnt("l3y_cl")),
                           ("l3y_cbe", P.cnt("l3y_cbe"))),
                    inc="l3y_cb")
                P.do("vector", lambda e: e.tensor_copy(
                    self.FB[:, 512:1024], self.PS[6][:, 0:512]),
                    waits=(("l3y_cb", P.cnt("l3y_cb")),
                           ("l3y_mu", P.cnt("l3y_mu"))),
                    inc="l3y_cbe")
                P.wait("tensor", ("l3y_hl", P.cnt("l3y_hl")),
                       ("l3y_wl", P.cnt("l3y_wl")))
                for oc in range(8):
                    bank = self.PS[4 + (oc % 2)]
                    P.wait("tensor",
                           ("l3y_mu", max(0, P.cnt("l3y_mu") - 1)))
                    for k in range(16):
                        P.do("tensor", lambda e, k=k, oc=oc, bank=bank:
                             e.matmul(
                                 bank[:, 0:512],
                                 self.WB[:, k * 1024 + oc * 128:
                                         k * 1024 + (oc + 1) * 128],
                                 self.WA[:, 8192 + k * 512:
                                         8192 + (k + 1) * 512],
                                 start=(k == 0), stop=(k == 15)),
                             inc="l3y_mm" if k == 15 else None)
                    P.do("vector", lambda e, bank=bank: e.tensor_tensor(
                        self.FC[:, 512:1024], bank[:, 0:512],
                        self.FB[:, 512:1024], ALU.mult),
                        waits=(("l3y_mm", P.cnt("l3y_mm")),
                               ("l3y_ast", P.cnt("l3y_ast")),
                               ("l3y_cbe", P.cnt("l3y_cbe"))),
                        inc="l3y_mu")
                    if ex > 0:
                        P.do("gpsimd", lambda e, oc=oc, t0=t0: e.dma_start(
                            out=self.FC[:, 1024:1536],
                            in_=g_acc[oc * 128:(oc + 1) * 128, t0:t0 + 512]),
                            inc="l3y_al", inc_n=16,
                            waits=(("l3y_mu", P.cnt("l3y_mu")),))
                        P.do("vector", lambda e: e.tensor_tensor(
                            self.FC[:, 512:1024], self.FC[:, 512:1024],
                            self.FC[:, 1024:1536], ALU.add),
                            waits=(("l3y_al", P.cnt("l3y_al")),),
                            inc="l3y_mu")
                    P.do("sync", lambda e, oc=oc, t0=t0: e.dma_start(
                        out=g_acc[oc * 128:(oc + 1) * 128, t0:t0 + 512],
                        in_=self.FC[:, 512:1024]),
                        waits=(("l3y_mu", P.cnt("l3y_mu")),),
                        inc="l3y_ast", inc_n=16)
                    P.wait("vector", ("l3y_ast", P.cnt("l3y_ast")))
            P.barrier()
        for tb in range(8):
            t0 = tb * 512
            for k in range(8):
                P.do("sync", lambda e, k=k, t0=t0: e.dma_start(
                    out=self.FA[:, k * 512:(k + 1) * 512],
                    in_=g_acc[k * 128:(k + 1) * 128, t0:t0 + 512]),
                    inc="l3g_ld", inc_n=16,
                    waits=(("l3g_si", P.cnt("l3g_si")),) if k == 0 else ())
            P.wait("scalar", ("l3g_ld", P.cnt("l3g_ld")))
            for k in range(8):
                P.do("scalar", lambda e, k=k: e.activation(
                    self.OUTS[:, (k % 2) * 512:(k % 2) * 512 + 512],
                    self.FA[:, k * 512:(k + 1) * 512], AF.Silu),
                    waits=(("l3g_st", max(0, P.cnt("l3g_st") - 16)),),
                    inc="l3g_si")
                P.do("sync", lambda e, k=k, t0=t0: e.dma_start(
                    out=self.g_fm[k * 128:(k + 1) * 128, t0:t0 + 512],
                    in_=self.OUTS[:, (k % 2) * 512:(k % 2) * 512 + 512]),
                    waits=(("l3g_si", P.cnt("l3g_si")),),
                    inc="l3g_st", inc_n=16)
        P.barrier()

    def emit_residual(self):
        P = self.P
        for tb in range(T // 512):
            t0 = tb * 512
            for k in range(8):
                P.do("sync", lambda e, k=k, t0=t0: e.dma_start(
                    out=self.FA[:, k * 512:(k + 1) * 512],
                    in_=self.x_fm[k * 128:(k + 1) * 128, t0:t0 + 512]),
                    inc="rs_ld", inc_n=16)
                P.do("gpsimd", lambda e, k=k, t0=t0: e.dma_start(
                    out=self.XA[:, k * 512:(k + 1) * 512],
                    in_=self.s_fm[k * 128:(k + 1) * 128, t0:t0 + 512]),
                    inc="rs_ld", inc_n=16)
                P.do("gpsimd", lambda e, k=k, t0=t0: e.dma_start(
                    out=self.XB[:, k * 512:(k + 1) * 512],
                    in_=self.g_fm[k * 128:(k + 1) * 128, t0:t0 + 512]),
                    inc="rs_ld", inc_n=16)
            P.wait("vector", "rs_ld")
            for k in range(8):
                P.do("vector", lambda e, k=k: e.tensor_tensor(
                    self.FB[:, k * 512:(k + 1) * 512],
                    self.XA[:, k * 512:(k + 1) * 512],
                    self.XB[:, k * 512:(k + 1) * 512], ALU.mult),
                    inc="rs_v1")
            for k in range(8):
                P.do("vector", lambda e, k=k: e.tensor_tensor(
                    self.FA[:, k * 512:(k + 1) * 512],
                    self.FA[:, k * 512:(k + 1) * 512],
                    self.FB[:, k * 512:(k + 1) * 512], ALU.add),
                    waits=(("rs_v1", P.cnt("rs_v1") - 7 + k),),
                    inc="rs_v2")
            for k in range(8):
                P.do("sync", lambda e, k=k, t0=t0: e.dma_start(
                    out=self.x_fm[k * 128:(k + 1) * 128, t0:t0 + 512],
                    in_=self.FA[:, k * 512:(k + 1) * 512]),
                    waits=(("rs_v2", P.cnt("rs_v2") - 7 + k),),
                    inc="rs_st", inc_n=16)
            P.wait("sync", "rs_st")
            P.barrier()

    def emit_output(self):
        P = self.P
        for g in range(8):
            t0 = g * 512
            for k in range(8):
                P.do("sync", lambda e, k=k, t0=t0: e.dma_start(
                    out=self.FA[:, k * 512:(k + 1) * 512],
                    in_=self.x_fm[k * 128:(k + 1) * 128, t0:t0 + 512]),
                    inc="o_ld", inc_n=16,
                    waits=(("o_mm", max(0, P.cnt("o_mm") - 1)),))
            P.wait("tensor", "o_ld")
            for j in range(4):
                for half in range(2):
                    bank = self.PS[half]
                    P.wait("tensor", ("o_ev", max(0, P.cnt("o_ev") - 1)))
                    for k in range(4):
                        kk = half * 4 + k
                        P.do("tensor", lambda e, j=j, kk=kk, k=k, bank=bank:
                             e.matmul(
                                 bank[:, k * 128:(k + 1) * 128],
                                 self.FA[:, kk * 512 + j * 128:
                                         kk * 512 + (j + 1) * 128],
                                 self.IDENT[:, :], start=True, stop=True),
                             inc="o_mm" if k == 3 else None)
                    P.do("vector", lambda e, half=half, bank=bank: e.tensor_copy(
                        self.FB[:, half * 512:(half + 1) * 512], bank[:, :]),
                        waits=("o_mm",), inc="o_ev")
                P.do("sync", lambda e, j=j, t0=t0: e.dma_start(
                    out=self.y_out[t0 + j * 128: t0 + (j + 1) * 128, :],
                    in_=self.FB[:, 0:1024]),
                    waits=("o_ev",), inc="o_st", inc_n=16)
                P.wait("vector", ("o_st", P.cnt("o_st")))
        P.barrier()

    def finalize(self):
        P = self.P
        for scr_name in self.dbg_outputs:
            ap = self.SCR[scr_name]
            shape = list(ap.tensor.shape)
            o = self.nc.dram_tensor("dbg_" + scr_name, shape,
                                    ap.tensor.dtype, kind="ExternalOutput").ap()
            P.do("sync", lambda e, o=o, ap=ap: e.dma_start(out=o[:, :], in_=ap[:, :]),
                 inc="dbg_d", inc_n=16)
        P.wait("sync", "dbg_d") if self.dbg_outputs else None
        P.wait("sync", "o_st") if "o_st" in P.log_cnt else None
        P.finish(self.es)
        return self.nc


STAGES_FULL = ("init", "x_in", "weight_prep",
               "ln0", "l0_attn", "l0_ffn", "residual",
               "ln0", "l1_attn", "l1_ffn", "residual",
               "ln0", "l2_attn", "l2_ffn", "residual",
               "ln0", "l3_attn", "l3_ffn", "residual",
               "output")


def build_program(stages=STAGES_FULL, dbg=(), ext_scratch=()):
    b = Builder(ext_scratch=ext_scratch)
    for scr in dbg:
        pass
    emitted_dbg = set()
    for st in stages:
        if st == "init":
            b.emit_init()
        elif st == "x_in":
            b.emit_x_in()
        elif st == "weight_prep":
            b.emit_weight_prep()
        elif st == "ln0":
            b.emit_module_ln()
        elif st == "l1_ffn":
            b.emit_l1_ffn()
        elif st == "l0_attn":
            b.emit_l0_attn()
        elif st == "l0_ffn":
            b.emit_l0_ffn()
        elif st == "l1_attn":
            b.emit_l1_attn()
        elif st == "l3_ffn":
            b.emit_l3_ffn()
        elif st == "l2_ffn":
            b.emit_l2_ffn()
        elif st.startswith("l2_ffn:"):
            b.emit_l2_ffn(phases=tuple(range(int(st.split(":")[1]) + 1)))
        elif st == "l2_attn":
            b.emit_attention("l2a", b.tw["l2_qkv"], b.IN["L2_attn_b_qkv"],
                             1.0, b.tw["l2_o"], b.IN["L2_attn_b_o"], 1.0,
                             0.125, 1.0, True, b.s_fm)
        elif st == "l3_attn":
            import math as _m
            b.emit_attention("l3a", b.tw["l3_qkv"], None, 1.0 / 32.0,
                             b.tw["l3_o"], None, 1.0 / 32.0, 0.125,
                             4096.0 / _m.sqrt(4095.0) / 64.0, False, b.s_fm)
        elif st == "residual":
            b.emit_residual()
        elif st == "output":
            b.emit_output()
        else:
            raise ValueError(st)
    for scr in dbg:
        b.dbg(scr)
    nc = b.finalize()
    return nc



_cache = {}


def make_in_maps(inputs):
    ident = np.eye(128, dtype=np.float32)
    x_full = np.ascontiguousarray(inputs["L0_attn_x"])
    zero_ew = np.zeros_like(inputs["L1_ffn_expert_weights"])
    in_maps = []
    for c in range(NCORES):
        m = {}
        for k, v in inputs.items():
            if k == "L0_attn_x":
                m[k] = np.ascontiguousarray(x_full[c])
            elif k == "L1_ffn_expert_weights":
                m[k] = inputs[k] if c == 0 else zero_ew
            elif k in ("L1_ffn_scores", "L1_ffn_top_experts"):
                continue
            else:
                m[k] = v
        m["c_ident"] = ident
        in_maps.append(m)
    return in_maps


def kernel(**inputs):
    key = "full"
    if key not in _cache:
        _cache[key] = build_program()
    nc = _cache[key]
    in_maps = make_in_maps(inputs)
    res = run_bass_kernel_spmd(nc, in_maps, core_ids=list(range(NCORES)))
    y = np.stack([res.results[c]["y"] for c in range(NCORES)], axis=0)
    return y


# version 03 (final)
# speedup vs baseline: 1.0204x; 1.0204x over previous
import numpy as np
from contextlib import ExitStack

import concourse.bass as bass
import concourse.mybir as mybir
from concourse.bass_utils import run_bass_kernel_spmd
import ml_dtypes

F32 = mybir.dt.float32
BF16 = mybir.dt.bfloat16
AF = mybir.ActivationFunctionType
ALU = mybir.AluOpType
AX = mybir.AxisListType

ENGINES = ("tensor", "vector", "scalar", "gpsimd", "sync")
T = 4096
D = 1024
NCORES = 8


def cdiv(a, b):
    return (a + b - 1) // b



class Prog:

    NPOOL = 13

    def __init__(self, nc):
        self.nc = nc
        self.ops = {e: [] for e in ENGINES}
        self.log_cnt = {}
        self.binding = {}
        self.phys_cnt = {}
        self.pool_rr = {}
        self.sem_handle = {}
        self.sem_names = []

    def sem(self, name):
        if name not in self.log_cnt:
            self.log_cnt[name] = 0
        return name

    def cnt(self, name):
        return self.log_cnt[self.sem(name)]

    def _phys_new(self, key):
        i = self.pool_rr.get(key, 0)
        self.pool_rr[key] = i + 1
        pname = f"px_{key[0]}_{key[1]}_{i % self.NPOOL}"
        if pname not in self.phys_cnt:
            self.phys_cnt[pname] = 0
            self.sem_names.append(pname)
        return pname

    def _bind(self, name, eng, is_dma):
        if name not in self.binding:
            self.binding[name] = self._phys_new((eng, "d" if is_dma else "c"))
        return self.binding[name]

    def _resolve(self, waits):
        wl = []
        for wspec in waits:
            if isinstance(wspec, tuple):
                s, v = wspec
            else:
                s, v = wspec, self.cnt(wspec)
            self.sem(s)
            if v <= 0 or s not in self.binding:
                continue
            phys = self.binding[s]
            tgt = self.phys_cnt[phys] - (self.log_cnt[s] - v)
            if tgt > 0:
                wl.append((phys, tgt))
        out = {}
        for p, v in wl:
            out[p] = max(out.get(p, 0), v)
        return list(out.items())

    def do(self, eng, build_fn, waits=(), inc=None, inc_n=1):
        wl = self._resolve(waits)
        if inc is not None:
            self.sem(inc)
            phys = self._bind(inc, eng, inc_n >= 16)
            self.log_cnt[inc] += inc_n
            self.phys_cnt[phys] += inc_n
        handles = self.sem_handle

        def thunk(e, wl=wl, build_fn=build_fn, inc=inc, inc_n=inc_n,
                  phys=(self.binding[inc] if inc is not None else None)):
            for s, v in wl:
                e.wait_ge(handles[s], v)
            instr = build_fn(e)
            if inc is not None:
                instr.then_inc(handles[phys], inc_n)
        self.ops[eng].append(thunk)

    def wait(self, eng, *waits):
        wl = self._resolve(waits)
        handles = self.sem_handle

        def thunk(e, wl=wl):
            for s, v in wl:
                e.wait_ge(handles[s], v)
        self.ops[eng].append(thunk)

    def barrier(self, engines=ENGINES):
        marks = []
        for e in engines:
            pname = "bar_" + e
            if pname not in self.phys_cnt:
                self.phys_cnt[pname] = 0
                self.sem_names.append(pname)
            self.phys_cnt[pname] += 1
            marks.append((e, pname, self.phys_cnt[pname]))
        handles = self.sem_handle
        for e, s, v in marks:
            def thunk(eng, s=s):
                eng.nop().then_inc(handles[s], 1)
            self.ops[e].append(thunk)
        allw = tuple((s, v) for (_, s, v) in marks)
        for e in engines:
            def thunk(eng, allw=allw):
                for s, v in allw:
                    eng.wait_ge(handles[s], v)
            self.ops[e].append(thunk)

    def finish(self, es):
        for name in self.sem_names:
            self.sem_handle[name] = es.enter_context(self.nc.semaphore(name))
        with self.nc.Block() as block:
            for ename in ENGINES:
                getattr(block, ename)(
                    lambda eng, stream=self.ops[ename]:
                        [t(eng) for t in stream])


class NS:
    pass



INPUT_SHAPES = {
    "L0_attn_x": (T, D),
    "L0_attn_in_proj_w": (2, 4256, 1024),
    "L0_attn_conv_w": (2, 2176, 4),
    "L0_attn_conv_b": (2, 2176),
    "L0_attn_dt_bias": (2, 32),
    "L0_attn_A_log": (2, 32),
    "L0_attn_D": (2, 32),
    "L0_attn_gnorm_w": (2, 2048),
    "L0_attn_out_proj_w": (2, 1024, 2048),
    "L0_attn_norm_w": (2, 1024),
    "L0_ffn_w_gate": (2816, 1024),
    "L0_ffn_w_up": (2816, 1024),
    "L0_ffn_w_down": (1024, 2816),
    "L1_attn_attn_norm_w": (1024,),
    "L1_attn_Wq": (1024, 512),
    "L1_attn_Wk": (1024, 512),
    "L1_attn_Wv": (1024, 1024),
    "L1_attn_Wg": (1024, 1024),
    "L1_attn_Wgk1": (1024, 16),
    "L1_attn_Wgk2": (16, 512),
    "L1_attn_bgk2": (512,),
    "L1_attn_gnorm_w": (256,),
    "L1_attn_Wo": (1024, 1024),
    "L1_attn_mlp_norm_w": (1024,),
    "L1_attn_W_gate": (1024, 2816),
    "L1_attn_W_up": (1024, 2816),
    "L1_attn_W_down": (2816, 1024),
    "L1_ffn_expert_weights": (4096, 2),
    "L1_ffn_w1": (1024, 8192),
    "L1_ffn_w2": (8192, 1024),
    "L2_attn_W_qkv": (3072, 1024),
    "L2_attn_b_qkv": (3072,),
    "L2_attn_W_o": (1024, 1024),
    "L2_attn_b_o": (1024,),
    "L2_ffn_gamma_x": (1024,),
    "L2_ffn_gamma_slot": (1024,),
    "L2_ffn_slot_embeds": (8, 512, 1024),
    "L2_ffn_w1": (1024, 4096),
    "L2_ffn_w2": (4096, 1024),
    "L3_attn_w_qkv": (3072, 1024),
    "L3_attn_w_o": (1024, 1024),
    "L3_ffn_router_w": (8, 1024),
    "L3_ffn_w13": (8, 1024, 4096),
    "L3_ffn_w2": (8, 2048, 1024),
}


class Builder:
    def __init__(self, ext_scratch=()):
        self.ext_scratch = set(ext_scratch)
        nc = bass.Bass("TRN2", target_bir_lowering=False, debug=False)
        self.nc = nc
        self.P = Prog(nc)
        self.es = ExitStack()
        self.IN = {}
        for name, shape in INPUT_SHAPES.items():
            self.IN[name] = nc.dram_tensor(
                name, list(shape), F32, kind="ExternalInput").ap()
        self.ident_in = nc.dram_tensor(
            "c_ident", [128, 128], F32, kind="ExternalInput").ap()
        self.y_out = nc.dram_tensor("y", [T, D], F32, kind="ExternalOutput").ap()
        self.SCR = {}
        self.dbg_outputs = []

        self.x_fm = self.scratch("x_fm", (D, T))
        self.xn_fm = self.scratch("xn_fm", (D, T))
        self.s_fm = self.scratch("s_fm", (D, T), BF16)
        self.g_fm = self.scratch("g_fm", (D, T), BF16)

        es = self.es
        self.WA = es.enter_context(nc.sbuf_tensor("WA", [128, 16384], BF16))
        self.WB = es.enter_context(nc.sbuf_tensor("WB", [128, 16384], BF16))
        self.XA = es.enter_context(nc.sbuf_tensor("XA", [128, 16384], BF16))
        self.XB = es.enter_context(nc.sbuf_tensor("XB", [128, 16384], BF16))
        self.FA = es.enter_context(nc.sbuf_tensor("FA", [128, 8192], F32))
        self.FB = es.enter_context(nc.sbuf_tensor("FB", [128, 4096], F32))
        self.FC = es.enter_context(nc.sbuf_tensor("FC", [128, 2048], F32))
        self.ROWS = es.enter_context(nc.sbuf_tensor("ROWS", [128, 512], F32))
        self.BIAS = es.enter_context(nc.sbuf_tensor("BIAS", [128, 64], F32))
        self.COEFR = es.enter_context(nc.sbuf_tensor("COEFR", [1, 4096], F32))
        self.OUTS = es.enter_context(nc.sbuf_tensor("OUTS", [128, 1024], BF16))
        self.CONST = es.enter_context(nc.sbuf_tensor("CONST", [128, 32], F32))
        self.ONESROW = es.enter_context(nc.sbuf_tensor("ONESROW", [1, 128], BF16))
        self.ONESCOLB = es.enter_context(nc.sbuf_tensor("ONESCOLB", [128, 1], BF16))
        self.ONESROWF = es.enter_context(nc.sbuf_tensor("ONESROWF", [1, 128], F32))
        self.IDENT = es.enter_context(nc.sbuf_tensor("IDENT", [128, 128], F32))
        self.IDENTB = es.enter_context(nc.sbuf_tensor("IDENTB", [128, 128], BF16))
        self.PS = [es.enter_context(nc.psum_tensor(f"PS{i}", [128, 512], F32))
                   for i in range(8)]

        self.const_idx = {}
        self._const_next = 0

    def scratch(self, name, shape, dtype=F32):
        kind = "ExternalInput" if name in self.ext_scratch else "Internal"
        self.SCR[name] = self.nc.dram_tensor(
            name, list(shape), dtype, kind=kind).ap()
        return self.SCR[name]

    def const(self, val):
        if val not in self.const_idx:
            idx = self._const_next
            self._const_next += 1
            assert idx < 32
            self.const_idx[val] = idx
            self.P.do("vector",
                      lambda e, idx=idx, val=val:
                      e.memset(self.CONST[:, idx:idx + 1], float(val)),
                      inc="c_init")
        idx = self.const_idx[val]
        return self.CONST[:, idx:idx + 1]

    def dbg(self, scr_name):
        self.dbg_outputs.append(scr_name)

    def emit_init(self):
        P = self.P
        P.do("gpsimd", lambda e: e.dma_start(out=self.IDENT[:, :],
                                             in_=self.ident_in[:, :]),
             inc="c_initd", inc_n=16)
        P.do("gpsimd", lambda e: e.dma_start(out=self.IDENTB[:, :],
                                             in_=self.ident_in[:, :]),
             inc="c_initd", inc_n=16)
        P.do("vector", lambda e: e.memset(self.ONESROW[0:1, :], 1.0),
             inc="c_init")
        P.do("vector", lambda e: e.memset(self.ONESCOLB[:, 0:1], 1.0),
             inc="c_init")
        P.do("vector", lambda e: e.memset(self.ONESROWF[0:1, :], 1.0),
             inc="c_init")
        for v in (1.0, 1e-5, 1e-6, float(np.log(128.0 ** -0.5))):
            self.const(v)
        def _mkfills(e):
            self._fill_reg = e.to_reg(-1e30)
            self._fill0_reg = e.to_reg(0.0)
            return None
        P.do("gpsimd", _mkfills)
        P.barrier()

    def emit_x_in(self):
        P = self.P
        x_in = self.IN["L0_attn_x"]
        for g in range(8):
            t0 = g * 512
            for j in range(4):
                P.do("sync", lambda e, t0=t0, j=j: e.dma_start(
                    out=self.FA[:, j * 1024:(j + 1) * 1024],
                    in_=x_in[t0 + j * 128: t0 + (j + 1) * 128, :]),
                    inc="xin_ld", inc_n=16)
            P.wait("tensor", "xin_ld")
            for dd in range(8):
                bank = self.PS[dd % 2]
                P.wait("tensor", ("xin_ev", max(0, P.cnt("xin_ev") - 1)))
                for j in range(4):
                    P.do("tensor", lambda e, dd=dd, j=j, bank=bank: e.matmul(
                        bank[:, j * 128:(j + 1) * 128],
                        self.FA[:, j * 1024 + dd * 128:
                                j * 1024 + (dd + 1) * 128],
                        self.IDENT[:, :], start=True, stop=True),
                        inc="xin_mm" if j == 3 else None)
                P.do("vector", lambda e, dd=dd, bank=bank: e.tensor_copy(
                    self.FB[:, dd * 512:(dd + 1) * 512], bank[:, :]),
                    waits=("xin_mm",), inc="xin_ev")
                P.do("sync", lambda e, dd=dd, t0=t0: e.dma_start(
                    out=self.x_fm[dd * 128:(dd + 1) * 128, t0:t0 + 512],
                    in_=self.FB[:, dd * 512:(dd + 1) * 512]),
                    waits=("xin_ev",), inc="xin_st", inc_n=16)
            P.wait("sync", "xin_st")
            P.barrier()

    def emit_weight_prep(self):
        P = self.P
        nat_bf = self.scratch("nat_bf", (4352, 2816), BF16)
        self.tw = {}
        jobs = [
            ("l0_inproj0", self.IN["L0_attn_in_proj_w"][0], 4256, 1024),
            ("l0_inproj1", self.IN["L0_attn_in_proj_w"][1], 4256, 1024),
            ("l0_outproj0", self.IN["L0_attn_out_proj_w"][0], 1024, 2048),
            ("l0_outproj1", self.IN["L0_attn_out_proj_w"][1], 1024, 2048),
            ("l0f_gate", self.IN["L0_ffn_w_gate"], 2816, 1024),
            ("l0f_up", self.IN["L0_ffn_w_up"], 2816, 1024),
            ("l0f_down", self.IN["L0_ffn_w_down"], 1024, 2816),
            ("l2_qkv", self.IN["L2_attn_W_qkv"], 3072, 1024),
            ("l2_o", self.IN["L2_attn_W_o"], 1024, 1024),
            ("l3_qkv", self.IN["L3_attn_w_qkv"], 3072, 1024),
            ("l3_o", self.IN["L3_attn_w_o"], 1024, 1024),
        ]
        for name, src, O, I in jobs:
            dst = self.scratch("tw_" + name, (I, O), BF16)
            self.tw[name] = dst
            n_chunks = cdiv(O, 128)
            for ai in range(n_chunks):
                pr = min(128, O - ai * 128)
                h = (ai % 2) * 5632
                P.do("gpsimd", lambda e, src=src, ai=ai, pr=pr, I=I, h=h:
                     e.dma_start(out=self.WA[0:pr, h:h + I],
                                 in_=src[ai * 128: ai * 128 + pr, :]),
                     inc="wp_ld", inc_n=16,
                     waits=(("wp_st", max(0, P.cnt("wp_st") - 16)),))
                P.do("sync", lambda e, ai=ai, pr=pr, I=I, nat_bf=nat_bf, h=h:
                     e.dma_start(out=nat_bf[ai * 128: ai * 128 + pr, 0:I],
                                 in_=self.WA[0:pr, h:h + I]),
                     waits=("wp_ld", "wp_t"), inc="wp_st", inc_n=16)
            FSP = 2048
            Opad = cdiv(O, 128) * 128
            wi = 0
            for i0 in range(0, I, 128):
                for o0 in range(0, O, FSP):
                    ff = min(FSP, Opad - o0)
                    fv = min(FSP, O - o0)
                    h = (wi % 2) * 8192
                    wi += 1
                    P.do("sync", lambda e, i0=i0, o0=o0, ff=ff, nat_bf=nat_bf, h=h:
                         e.dma_start_transpose(self.WB[:, h:h + ff],
                                               nat_bf[o0:o0 + ff,
                                                      i0:i0 + 128]),
                         waits=("wp_st",
                                ("wp_tst", max(0, P.cnt("wp_tst") - 16))),
                         inc="wp_t", inc_n=16)
                    P.do("sync", lambda e, i0=i0, o0=o0, fv=fv, dst=dst, h=h:
                         e.dma_start(out=dst[i0:i0 + 128, o0:o0 + fv],
                                     in_=self.WB[:, h:h + fv]),
                         waits=("wp_t",), inc="wp_tst", inc_n=16)
        P.barrier()

    def emit_module_ln(self):
        P = self.P
        ones_col = self.const(1.0)
        eps_col = self.const(1e-5)
        for tb in range(T // 512):
            t0 = tb * 512
            for k in range(8):
                P.do("sync", lambda e, k=k, t0=t0: e.dma_start(
                    out=self.FA[:, k * 512:(k + 1) * 512],
                    in_=self.x_fm[k * 128:(k + 1) * 128, t0:t0 + 512]),
                    inc="ln_ld", inc_n=16)
            P.wait("tensor", "ln_ld")
            for k in range(8):
                P.do("tensor", lambda e, k=k: e.matmul(
                    self.PS[0][0:1, 0:512], ones_col,
                    self.FA[:, k * 512:(k + 1) * 512],
                    start=(k == 0), stop=(k == 7)),
                    inc="ln_mm1" if k == 7 else None)
            P.wait("scalar", "ln_ld")
            for k in range(8):
                P.do("scalar", lambda e, k=k: e.activation(
                    self.FB[:, k * 512:(k + 1) * 512],
                    self.FA[:, k * 512:(k + 1) * 512], AF.Square),
                    inc="ln_sq" if k == 7 else None)
            P.wait("tensor", "ln_sq")
            for k in range(8):
                P.do("tensor", lambda e, k=k: e.matmul(
                    self.PS[1][0:1, 0:512], ones_col,
                    self.FB[:, k * 512:(k + 1) * 512],
                    start=(k == 0), stop=(k == 7)),
                    inc="ln_mm2" if k == 7 else None)
            S_ = lambda i: self.FC[0:1, i * 512:(i + 1) * 512]
            P.do("vector", lambda e: e.tensor_scalar(
                S_(0), self.PS[0][0:1, 0:512], 1.0 / D, None, ALU.mult),
                waits=("ln_mm1",), inc="ln_v")
            P.do("vector", lambda e: e.tensor_scalar(
                S_(1), self.PS[1][0:1, 0:512], 1.0 / D, None, ALU.mult),
                waits=("ln_mm2", "ln_v"), inc="ln_v")
            P.do("vector", lambda e: e.tensor_tensor(
                S_(2), S_(0), S_(0), ALU.mult),
                waits=("ln_v",), inc="ln_v")
            P.do("vector", lambda e: e.tensor_tensor(
                S_(3), S_(1), S_(2), ALU.subtract),
                waits=("ln_v",), inc="ln_v")
            P.do("scalar", lambda e: e.activation(
                S_(2), S_(3), AF.Ln,
                bias=eps_col[0:1, :]),
                waits=("ln_v",), inc="ln_s")
            P.do("scalar", lambda e: e.activation(
                S_(1), S_(2), AF.Exp, scale=-0.5),
                waits=("ln_s",), inc="ln_s")
            P.do("vector", lambda e: e.scalar_tensor_tensor(
                S_(0), S_(0), -1.0, S_(1),
                ALU.mult, ALU.mult),
                waits=("ln_s",), inc="ln_v")
            P.do("tensor", lambda e: e.matmul(
                self.PS[2][:, 0:512], self.ONESROWF[0:1, :],
                S_(1), start=True, stop=True),
                waits=("ln_s",), inc="ln_bc")
            P.do("tensor", lambda e: e.matmul(
                self.PS[3][:, 0:512], self.ONESROWF[0:1, :],
                S_(0), start=True, stop=True),
                waits=("ln_v",), inc="ln_bc")
            P.wait("vector", "ln_bc")
            for k in range(8):
                P.do("vector", lambda e, k=k: e.tensor_tensor(
                    self.FB[:, k * 512:(k + 1) * 512],
                    self.FA[:, k * 512:(k + 1) * 512],
                    self.PS[2][:, 0:512], ALU.mult),
                    inc="ln_ap1")
            for k in range(8):
                P.do("vector", lambda e, k=k: e.tensor_tensor(
                    self.FB[:, k * 512:(k + 1) * 512],
                    self.FB[:, k * 512:(k + 1) * 512],
                    self.PS[3][:, 0:512], ALU.add),
                    inc="ln_ap2")
            for k in range(8):
                P.do("sync", lambda e, k=k, t0=t0: e.dma_start(
                    out=self.xn_fm[k * 128:(k + 1) * 128, t0:t0 + 512],
                    in_=self.FB[:, k * 512:(k + 1) * 512]),
                    waits=(("ln_ap2", P.cnt("ln_ap2") - 7 + k),),
                    inc="ln_st", inc_n=16)
            P.wait("sync", "ln_st")
            P.barrier()

    def load_bias(self, tag, bias_hbm, Dout, col0=0):
        P = self.P
        for oc in range(Dout // 128):
            P.do("gpsimd", lambda e, oc=oc: e.dma_start(
                out=self.BIAS[:, col0 + oc:col0 + oc + 1],
                in_=bias_hbm[oc * 128:(oc + 1) * 128]),
                inc=tag + "_bld", inc_n=16)
        return col0

    def emit_proj(self, tag, act_hbm, Din, Dout, w_src, dst_hbm,
                  pre_transposed=False, act_is_f32=True, epilogue=None,
                  o_slab=512, t_block=512, act_resident=None,
                  stage_barrier=True, out_f32=False):
        P = self.P
        nK = Din // 128
        if act_resident is None:
            act_resident = (nK * T <= 32768)
        assert Dout % 128 == 0 and o_slab % 128 == 0

        def act_view(k, t0):
            if act_resident:
                base = self.XA if k < 4 else self.XB
                return base[:, (k % 4) * T + t0:(k % 4) * T + t0 + t_block]
            return self.XA[:, k * t_block:(k + 1) * t_block]

        if act_resident:
            for k in range(nK):
                base = self.XA if k < 4 else self.XB
                eng = "gpsimd" if act_is_f32 else "sync"
                P.do(eng, lambda e, k=k, base=base: e.dma_start(
                    out=base[:, (k % 4) * T:(k % 4) * T + T],
                    in_=act_hbm[k * 128:(k + 1) * 128, :]),
                    inc=tag + "_ald", inc_n=16)

        n_slabs = cdiv(Dout, o_slab)
        assert nK * o_slab <= 8192
        mm_base = P.cnt(tag + "_mm")
        slab_groups = []
        for si in range(n_slabs):
            o0 = si * o_slab
            ow = min(o_slab, Dout - o0)
            h = (si % 2) * 8192
            if si >= 2:
                done_upto = mm_base + sum(slab_groups[:si - 1])
                wld_wait = (tag + "_mm", done_upto)
            else:
                wld_wait = (tag + "_mm", 0)
            weng = "sync" if pre_transposed else "gpsimd"
            for k in range(nK):
                P.do(weng, lambda e, k=k, o0=o0, ow=ow, h=h: e.dma_start(
                    out=self.WA[:, h + k * o_slab:h + k * o_slab + ow],
                    in_=w_src[k * 128:(k + 1) * 128, o0:o0 + ow]),
                    inc=tag + "_wld", inc_n=16,
                    waits=(wld_wait,) if k == 0 else ())
            slab_groups.append((ow // 128) * (T // t_block))
            for tb in range(T // t_block):
                t0 = tb * t_block
                if not act_resident:
                    for k in range(nK):
                        eng = "gpsimd" if act_is_f32 else "sync"
                        P.do(eng, lambda e, k=k, t0=t0: e.dma_start(
                            out=self.XA[:, k * t_block:(k + 1) * t_block],
                            in_=act_hbm[k * 128:(k + 1) * 128,
                                        t0:t0 + t_block]),
                            inc=tag + "_ald", inc_n=16,
                            waits=((tag + "_mm", P.cnt(tag + "_mm")),)
                            if k == 0 else ())
                for oc in range(ow // 128):
                    gi = P.cnt(tag + "_mm")
                    bank = self.PS[gi % 4]
                    P.wait("tensor",
                           (tag + "_evd", max(0, P.cnt(tag + "_evd") - 3)),
                           (tag + "_wld", P.cnt(tag + "_wld")),
                           (tag + "_ald", P.cnt(tag + "_ald")))
                    for k in range(nK):
                        P.do("tensor", lambda e, k=k, oc=oc, h=h,
                             bank=bank, t0=t0: e.matmul(
                                 bank[:, 0:t_block],
                                 self.WA[:, h + k * o_slab + oc * 128:
                                         h + k * o_slab + (oc + 1) * 128],
                                 act_view(k, t0),
                                 start=(k == 0), stop=(k == nK - 1)),
                             inc=(tag + "_mm") if k == nK - 1 else None)
                    oc_abs = (o0 + oc * 128) // 128
                    oh = (gi % 2) * 512
                    out_view = (self.FB[:, 3072 + oh:3072 + oh + t_block]
                                if out_f32 else self.OUTS[:, oh:oh + t_block])
                    prewaits = ((tag + "_mm",),
                                (tag + "_st", max(0, P.cnt(tag + "_st") - 16)))
                    prewaits = ((tag + "_mm", P.cnt(tag + "_mm")),
                                (tag + "_st", max(0, P.cnt(tag + "_st") - 16)))
                    if epilogue is None:
                        P.do("scalar", lambda e, bank=bank, out_view=out_view:
                             e.activation(out_view, bank[:, 0:t_block],
                                          AF.Copy),
                             waits=prewaits, inc=tag + "_evd")
                    else:
                        epilogue(self, tag, bank, oc_abs, 128,
                                 (t0, t0 + t_block), out_view, prewaits)
                    P.do("sync", lambda e, oc_abs=oc_abs, t0=t0,
                         out_view=out_view: e.dma_start(
                             out=dst_hbm[oc_abs * 128:(oc_abs + 1) * 128,
                                         t0:t0 + t_block],
                             in_=out_view),
                         waits=(tag + "_evd",), inc=tag + "_st", inc_n=16)
        if stage_barrier:
            P.barrier()

    def ep_gelu_tanh(self):
        def ep(b, tag, bank, oc_abs, ocw, tcols, out_view, prewaits):
            P = b.P
            tw = tcols[1] - tcols[0]
            gi = P.cnt(tag + "_evd")
            r1 = b.FB[:, (gi % 2) * 512:(gi % 2) * 512 + tw]
            r2 = b.FB[:, 1024 + (gi % 2) * 512:1024 + (gi % 2) * 512 + tw]
            fbw = (tag + "_ep7", max(0, P.cnt(tag + "_ep7") - 1))
            P.do("scalar", lambda e: e.activation(r1, bank[:, 0:tw], AF.Copy),
                 waits=prewaits + (fbw,), inc=tag + "_ep1")
            P.do("scalar", lambda e: e.activation(r2, bank[:, 0:tw], AF.Square),
                 waits=((tag + "_ep1", P.cnt(tag + "_ep1")), fbw),
                 inc=tag + "_ep2")
            P.do("vector", lambda e: e.tensor_scalar(
                r2, r2, 0.044715, 1.0, ALU.mult, ALU.add),
                waits=((tag + "_ep2", P.cnt(tag + "_ep2")),),
                inc=tag + "_ep3")
            P.do("vector", lambda e: e.tensor_tensor(r2, r2, r1, ALU.mult),
                waits=((tag + "_ep3", P.cnt(tag + "_ep3")),),
                inc=tag + "_ep4")
            P.do("scalar", lambda e: e.activation(
                r2, r2, AF.Tanh, scale=0.7978845608028654),
                waits=((tag + "_ep4", P.cnt(tag + "_ep4")),),
                inc=tag + "_ep5")
            P.do("vector", lambda e: e.tensor_scalar(
                r2, r2, 0.5, 0.5, ALU.mult, ALU.add),
                waits=((tag + "_ep5", P.cnt(tag + "_ep5")),),
                inc=tag + "_ep6")
            P.do("vector", lambda e: e.tensor_tensor(
                out_view, r1, r2, ALU.mult),
                waits=((tag + "_ep6", P.cnt(tag + "_ep6")),),
                inc=tag + "_ep7")
            P.do("vector", lambda e: e.nop(),
                 waits=((tag + "_ep7", P.cnt(tag + "_ep7")),),
                 inc=tag + "_evd")
        return ep

    def emit_l1_ffn(self):
        h_fm = self.scratch("l1f_h", (8192, T), BF16)
        y_fm = self.scratch("l1f_y", (1024, T), BF16)
        self.emit_proj("l1f1", self.xn_fm, 1024, 8192,
                       self.IN["L1_ffn_w1"], h_fm,
                       pre_transposed=False, act_is_f32=True,
                       epilogue=self.ep_gelu_tanh())
        self.emit_proj("l1f2", h_fm, 8192, 1024,
                       self.IN["L1_ffn_w2"], y_fm,
                       pre_transposed=False, act_is_f32=False,
                       o_slab=128, t_block=256)
        P = self.P
        ew = self.IN["L1_ffn_expert_weights"]
        P.do("sync", lambda e: e.dma_start(
            out=self.FA[0:1, 0:8192],
            in_=ew.rearrange("t k -> (t k)")),
            inc="l1c_ld", inc_n=16)
        P.do("vector", lambda e: e.tensor_tensor(
            self.COEFR[0:1, 0:T],
            self.FA[0:1, 0:8192].rearrange("p (t k) -> p t k", k=2)[:, :, 0],
            self.FA[0:1, 0:8192].rearrange("p (t k) -> p t k", k=2)[:, :, 1],
            ALU.add),
            waits=("l1c_ld",), inc="l1c_v")
        cdbg = self.scratch("l1c_coef", (1, T))
        P.do("sync", lambda e: e.dma_start(out=cdbg[0:1, :], in_=self.COEFR[0:1, 0:T]),
             waits=("l1c_v",), inc="l1c_cd", inc_n=16)
        for tb in range(T // 512):
            t0 = tb * 512
            P.do("tensor", lambda e, t0=t0: e.matmul(
                self.PS[4][:, 0:512], self.ONESROWF[0:1, :],
                self.COEFR[0:1, t0:t0 + 512], start=True, stop=True),
                waits=("l1c_v", ("l1c_mu", P.cnt("l1c_mu"))),
                inc="l1c_bc")
            for k in range(8):
                P.do("gpsimd", lambda e, k=k, t0=t0: e.dma_start(
                    out=self.FA[:, k * 512:(k + 1) * 512],
                    in_=y_fm[k * 128:(k + 1) * 128, t0:t0 + 512]),
                    inc="l1c_yld", inc_n=16,
                    waits=(("l1c_mu", P.cnt("l1c_mu")), ("l1c_v", P.cnt("l1c_v")))
                    if k == 0 else ())
            P.wait("vector", "l1c_yld", "l1c_bc")
            for k in range(8):
                P.do("vector", lambda e, k=k: e.tensor_tensor(
                    self.FB[:, (k % 4) * 512:(k % 4) * 512 + 512],
                    self.FA[:, k * 512:(k + 1) * 512],
                    self.PS[4][:, 0:512], ALU.mult),
                    waits=(("l1c_si", P.cnt("l1c_si")),
                           ("l1c_st", P.cnt("l1c_st")),
                           ("l1c_mu", P.cnt("l1c_mu"))),
                    inc="l1c_mu")
                P.do("scalar", lambda e, k=k: e.activation(
                    self.OUTS[:, (k % 2) * 512:(k % 2) * 512 + 512],
                    self.FB[:, (k % 4) * 512:(k % 4) * 512 + 512], AF.Silu),
                    waits=(("l1c_mu", P.cnt("l1c_mu")),
                           ("l1c_st", P.cnt("l1c_st")),
                           ("l1c_si", P.cnt("l1c_si"))),
                    inc="l1c_si")
                P.do("sync", lambda e, k=k, t0=t0: e.dma_start(
                    out=self.g_fm[k * 128:(k + 1) * 128, t0:t0 + 512],
                    in_=self.OUTS[:, (k % 2) * 512:(k % 2) * 512 + 512]),
                    waits=(("l1c_si", P.cnt("l1c_si")),),
                    inc="l1c_st", inc_n=16)
        P.barrier()

    def ep_bias_scale(self, bias_col0=None, scale=1.0, func=None):
        func = func or (AF.Identity if bias_col0 is not None else AF.Copy)
        def ep(b, tag, bank, oc_abs, ocw, tcols, out_view, prewaits):
            P = b.P
            tw = tcols[1] - tcols[0]
            bias = 0.0 if bias_col0 is None else \
                b.BIAS[:, bias_col0 + oc_abs:bias_col0 + oc_abs + 1]
            P.do("scalar", lambda e: e.activation(
                out_view, bank[:, 0:tw], func, bias=bias, scale=scale),
                waits=prewaits, inc=tag + "_evd")
        return ep

    def emit_attention(self, tag, qkv_tw, qkv_bias, qkv_scale,
                       o_tw, o_bias, o_scale, exp_scale, p_scale, causal,
                       dst_fm):
        P = self.P
        qkv_fm = self.scratch(tag + "_qkv", (3072, T), BF16)
        o_fm = self.scratch(tag + "_o", (1024, T), BF16)
        bias_col0 = None
        if qkv_bias is not None:
            bias_col0 = 0
            self.load_bias(tag, qkv_bias, 3072, 0)
            P.wait("scalar", tag + "_bld")
        self.emit_proj(tag + "p", self.xn_fm, 1024, 3072, qkv_tw, qkv_fm,
                       pre_transposed=True, act_is_f32=True,
                       epilogue=self.ep_bias_scale(bias_col0, qkv_scale))
        NH = 16
        QA = lambda c0, c1: self.XA[0:65, c0:c1]
        KA = lambda c0, c1: self.XA[0:65, 4096 + c0:4096 + c1]
        VTM = 8192
        VST = 10288
        P.do("vector", lambda e: e.memset(self.XA[64:65, 0:8192], 1.0),
             inc=tag + "_ini")

        P.do("vector", lambda e: e.memset(self.XA[:, VTM:VTM + 2080], 1.0),
             inc=tag + "_ini")
        P.barrier()
        for h in range(NH):
            P.do("sync", lambda e, h=h: e.dma_start(
                out=self.XA[0:64, 0:4096],
                in_=qkv_fm[h * 64:(h + 1) * 64, :]),
                inc=tag + "_hld", inc_n=16)
            P.do("sync", lambda e, h=h: e.dma_start(
                out=self.XA[0:64, 4096:8192],
                in_=qkv_fm[1024 + h * 64:1024 + (h + 1) * 64, :]),
                inc=tag + "_hld", inc_n=16)
            P.do("sync", lambda e, h=h: e.dma_start(
                out=self.XA[0:64, VST:VST + 4096],
                in_=qkv_fm[2048 + h * 64:2048 + (h + 1) * 64, :]),
                inc=tag + "_hld", inc_n=16)
            P.wait("tensor", tag + "_hld")
            for g in range(8):
                P.wait("tensor", (tag + "_vev", P.cnt(tag + "_vev")))
                for j in range(4):
                    kt = g * 4 + j
                    P.do("tensor", lambda e, kt=kt, j=j: e.matmul(
                        self.PS[4][:, j * 64:(j + 1) * 64],
                        self.XA[0:64, VST + kt * 128:VST + (kt + 1) * 128],
                        self.IDENTB[0:64, 0:64], start=True, stop=True),
                        inc=(tag + "_vmm") if j == 3 else None)
                vtv = self.XA[:, VTM + g * 260:VTM + g * 260 + 260]
                vtv = vtv.rearrange("p (t c) -> p t c", c=65)[:, :, 0:64]
                P.do("vector", lambda e, vtv=vtv: e.tensor_copy(
                    vtv, self.PS[4][:, 0:256].rearrange(
                        "p (t c) -> p t c", c=64)),
                    waits=(tag + "_vmm",), inc=tag + "_vev")
            P.do("scalar", lambda e: e.activation(
                self.FA[0:64, 0:4096], self.XA[0:64, 0:4096], AF.Square),
                waits=(tag + "_hld",), inc=tag + "_sq")
            P.do("scalar", lambda e: e.activation(
                self.FA[0:64, 4096:8192], self.XA[0:64, 4096:8192], AF.Square),
                waits=(tag + "_sq",), inc=tag + "_sq")
            ones64 = self.const(1.0)[0:64, :]
            for c in range(8):
                P.do("tensor", lambda e, c=c: e.matmul(
                    self.PS[5][0:1, 0:512], ones64,
                    self.FA[0:64, c * 512:(c + 1) * 512],
                    start=True, stop=True),
                    waits=((tag + "_sq", P.cnt(tag + "_sq") - 1),
                           (tag + "_qsv", P.cnt(tag + "_qsv"))),
                    inc=tag + "_qsm")
                P.do("vector", lambda e, c=c: e.tensor_copy(
                    self.FB[0:1, c * 512:(c + 1) * 512],
                    self.PS[5][0:1, 0:512]),
                    waits=((tag + "_qsm", P.cnt(tag + "_qsm")),),
                    inc=tag + "_qsv")
                P.do("tensor", lambda e, c=c: e.matmul(
                    self.PS[5][0:1, 0:512], ones64,
                    self.FA[0:64, 4096 + c * 512:4096 + (c + 1) * 512],
                    start=True, stop=True),
                    waits=((tag + "_sq", P.cnt(tag + "_sq")),
                           (tag + "_qsv", P.cnt(tag + "_qsv"))),
                    inc=tag + "_qsm")
                P.do("vector", lambda e, c=c: e.tensor_copy(
                    self.FB[32:33, c * 512:(c + 1) * 512],
                    self.PS[5][0:1, 0:512]),
                    waits=((tag + "_qsm", P.cnt(tag + "_qsm")),),
                    inc=tag + "_qsv")
            P.do("vector", lambda e: e.reduce_max(
                self.ROWS[0:1, 0:1], self.FB[32:33, 0:4096], axis=AX.X),
                waits=(tag + "_qsv",), inc=tag + "_bnd")
            P.do("scalar", lambda e: e.activation(
                self.FA[0:1, 0:4096], self.FB[0:1, 0:4096], AF.Sqrt,
                scale=self.ROWS[0:1, 0:1]),
                waits=(tag + "_bnd", tag + "_qsv"), inc=tag + "_bnd2")
            P.do("vector", lambda e: e.tensor_scalar(
                self.XA[64:65, 0:4096], self.FA[0:1, 0:4096],
                -1.0, None, ALU.mult),
                waits=(tag + "_bnd2",), inc=tag + "_bnd3")
            P.wait("tensor", tag + "_bnd3", tag + "_vev", tag + "_ini")
            for qb in range(8):
                q0 = qb * 512
                ntiles = 4 * (qb + 1) if causal else 32
                for kt in range(ntiles):
                    diag = causal and (kt >= 4 * qb)
                    bank = self.PS[P.cnt(tag + "_smm") % 2]
                    P.wait("tensor",
                           (tag + "_xp", max(0, P.cnt(tag + "_xp") - 1)))
                    P.do("tensor", lambda e, kt=kt, q0=q0, bank=bank:
                         e.matmul(bank[:, 0:512],
                                  KA(kt * 128, (kt + 1) * 128),
                                  QA(q0, q0 + 512),
                                  start=True, stop=True),
                         inc=tag + "_smm")
                    if not diag:
                        P.do("scalar", lambda e, kt=kt, bank=bank:
                             e.activation(
                                 self.XB[:, kt * 512:(kt + 1) * 512],
                                 bank[:, 0:512], AF.Exp, scale=exp_scale),
                             waits=((tag + "_smm", P.cnt(tag + "_smm")),),
                             inc=tag + "_xp")
                    else:
                        P.do("scalar", lambda e, bank=bank: e.activation(
                            self.FA[:, 4096:4608], bank[:, 0:512], AF.Copy),
                            waits=((tag + "_smm", P.cnt(tag + "_smm")),
                                   (tag + "_msk",
                                    max(0, P.cnt(tag + "_msk") - 1))),
                            inc=tag + "_dcp")
                        P.do("gpsimd", lambda e, kt=kt, q0=q0:
                             e.affine_select(
                                 self.FA[:, 4608:5120], self.FA[:, 4096:4608],
                                 [[1, 512]], ALU.is_ge, self._fill_reg,
                                 base=q0 - kt * 128, channel_multiplier=-1),
                             waits=((tag + "_dcp", P.cnt(tag + "_dcp")),
                                    (tag + "_xp", P.cnt(tag + "_xp"))),
                             inc=tag + "_msk")
                        P.do("scalar", lambda e, kt=kt: e.activation(
                            self.XB[:, kt * 512:(kt + 1) * 512],
                            self.FA[:, 4608:5120], AF.Exp, scale=exp_scale),
                            waits=((tag + "_msk", P.cnt(tag + "_msk")),),
                            inc=tag + "_xp")
                P.wait("tensor", tag + "_xp",
                       (tag + "_oev", P.cnt(tag + "_oev")),
                       (tag + "_rc", P.cnt(tag + "_rc")))
                for kt in range(ntiles):
                    P.do("tensor", lambda e, kt=kt: e.matmul(
                        self.PS[2][0:65, 0:512],
                        self.XA[:, VTM + kt * 65:VTM + (kt + 1) * 65],
                        self.XB[:, kt * 512:(kt + 1) * 512],
                        start=(kt == 0), stop=(kt == ntiles - 1)),
                        inc=(tag + "_avm") if kt == ntiles - 1 else None)
                P.do("vector", lambda e: e.reciprocal(
                    self.FB[0:1, 0:512], self.PS[2][64:65, 0:512]),
                    waits=((tag + "_avm", P.cnt(tag + "_avm")),),
                    inc=tag + "_rc")
                if p_scale != 1.0:
                    P.do("vector", lambda e: e.tensor_scalar(
                        self.FB[0:1, 0:512], self.FB[0:1, 0:512],
                        p_scale, None, ALU.mult),
                        waits=((tag + "_rc", P.cnt(tag + "_rc")),),
                        inc=tag + "_rc")
                P.do("tensor", lambda e: e.matmul(
                    self.PS[3][0:64, 0:512], self.ONESROWF[0:1, 0:64],
                    self.FB[0:1, 0:512], start=True, stop=True),
                    waits=((tag + "_rc", P.cnt(tag + "_rc")),
                           (tag + "_onrm", P.cnt(tag + "_onrm"))),
                    inc=tag + "_rbc")
                P.do("scalar", lambda e: e.activation(
                    self.FA[0:64, 5120:5632], self.PS[2][0:64, 0:512],
                    AF.Copy),
                    waits=((tag + "_avm", P.cnt(tag + "_avm")),),
                    inc=tag + "_oev")
                P.do("vector", lambda e: e.tensor_tensor(
                    self.OUTS[0:64, 0:512], self.FA[0:64, 5120:5632],
                    self.PS[3][0:64, 0:512], ALU.mult),
                    waits=((tag + "_oev", P.cnt(tag + "_oev")),
                           (tag + "_rbc", P.cnt(tag + "_rbc")),
                           (tag + "_ost", P.cnt(tag + "_ost"))),
                    inc=tag + "_onrm")
                P.do("sync", lambda e, h=h, q0=q0: e.dma_start(
                    out=o_fm[h * 64:(h + 1) * 64, q0:q0 + 512],
                    in_=self.OUTS[0:64, 0:512]),
                    waits=((tag + "_onrm", P.cnt(tag + "_onrm")),),
                    inc=tag + "_ost", inc_n=16)
            P.barrier()
        bias2 = None
        if o_bias is not None:
            bias2 = 32
            self.load_bias(tag + "o", o_bias, 1024, 32)
            P.wait("scalar", tag + "o_bld")
        self.emit_proj(tag + "o", o_fm, 1024, 1024, o_tw, dst_fm,
                       pre_transposed=True, act_is_f32=False,
                       epilogue=self.ep_bias_scale(bias2, o_scale))

    def emit_rms_fm(self, tag, src_fm, dst_fm, Drows, gamma_hbm, mode,
                    eps=1e-6, outscale=1.0, src_is_f32=True):
        P = self.P
        nK = Drows // 128
        tbw = min(512, 4096 // nK)
        ones_col = self.const(1.0)
        gcol0 = None
        if gamma_hbm is not None:
            gcol0 = 40
            for oc in range(nK):
                P.do("gpsimd", lambda e, oc=oc: e.dma_start(
                    out=self.BIAS[:, gcol0 + oc:gcol0 + oc + 1],
                    in_=gamma_hbm[oc * 128:(oc + 1) * 128]),
                    inc=tag + "_gld", inc_n=16)
            P.wait("vector", tag + "_gld")
        for tb in range(T // tbw):
            t0 = tb * tbw
            for k in range(nK):
                eng = "gpsimd" if not src_is_f32 else "sync"
                P.do(eng, lambda e, k=k, t0=t0: e.dma_start(
                    out=self.FA[:, k * tbw:(k + 1) * tbw],
                    in_=src_fm[k * 128:(k + 1) * 128, t0:t0 + tbw]),
                    inc=tag + "_ld", inc_n=16,
                    waits=((tag + "_ap1", P.cnt(tag + "_ap1")),))
            P.wait("scalar", tag + "_ld", (tag + "_ap", P.cnt(tag + "_ap")))
            for k in range(nK):
                P.do("scalar", lambda e, k=k: e.activation(
                    self.FB[:, k * tbw:(k + 1) * tbw],
                    self.FA[:, k * tbw:(k + 1) * tbw], AF.Square),
                    inc=(tag + "_sq") if k == nK - 1 else None)
            P.wait("tensor", tag + "_sq",
                   (tag + "_sv", P.cnt(tag + "_sv")))
            for k in range(nK):
                P.do("tensor", lambda e, k=k: e.matmul(
                    self.PS[6][0:1, 0:tbw], ones_col,
                    self.FB[:, k * tbw:(k + 1) * tbw],
                    start=(k == 0), stop=(k == nK - 1)),
                    inc=(tag + "_mm") if k == nK - 1 else None)
            S_ = lambda i: self.FC[0:1, i * tbw:(i + 1) * tbw]
            if mode == "rms":
                P.do("vector", lambda e: e.tensor_scalar(
                    S_(0), self.PS[6][0:1, 0:tbw], 1.0 / Drows, None,
                    ALU.mult), waits=(tag + "_mm",), inc=tag + "_sv")
                P.do("scalar", lambda e: e.activation(
                    S_(1), S_(0), AF.Ln, bias=self.const(eps)[0:1, :]),
                    waits=(tag + "_sv",), inc=tag + "_sc")
                P.do("scalar", lambda e, outscale=outscale: e.activation(
                    S_(0), S_(1), AF.Exp, scale=-0.5),
                    waits=(tag + "_sc",), inc=tag + "_sc")
                if outscale != 1.0:
                    P.do("vector", lambda e: e.tensor_scalar(
                        S_(0), S_(0), outscale, None, ALU.mult),
                        waits=(tag + "_sc",), inc=tag + "_sv")
                    P.wait("tensor", tag + "_sv")
                else:
                    P.wait("tensor", tag + "_sc")
            else:
                P.do("scalar", lambda e: e.activation(
                    S_(0), self.PS[6][0:1, 0:tbw], AF.Sqrt),
                    waits=(tag + "_mm",), inc=tag + "_sc")
                P.do("vector", lambda e: e.tensor_scalar(
                    S_(1), S_(0), eps, None, ALU.max),
                    waits=(tag + "_sc",), inc=tag + "_sv")
                P.do("vector", lambda e: e.reciprocal(S_(0), S_(1)),
                    waits=(tag + "_sv",), inc=tag + "_sv")
                if outscale != 1.0:
                    P.do("vector", lambda e: e.tensor_scalar(
                        S_(0), S_(0), outscale, None, ALU.mult),
                        waits=(tag + "_sv",), inc=tag + "_sv")
                P.wait("tensor", tag + "_sv")
            P.do("tensor", lambda e: e.matmul(
                self.PS[7][:, 0:tbw], self.ONESROWF[0:1, :],
                S_(0), start=True, stop=True),
                waits=((tag + "_ap", P.cnt(tag + "_ap")),),
                inc=tag + "_bc")
            P.wait("vector", tag + "_bc")
            for k in range(nK):
                P.do("vector", lambda e, k=k: e.tensor_tensor(
                    self.FB[:, k * tbw:(k + 1) * tbw],
                    self.FA[:, k * tbw:(k + 1) * tbw],
                    self.PS[7][:, 0:tbw], ALU.mult),
                    inc=tag + "_ap1")
                if gcol0 is not None:
                    P.do("vector", lambda e, k=k: e.tensor_scalar(
                        self.OUTS[:, (k % 2) * 512:(k % 2) * 512 + tbw],
                        self.FB[:, k * tbw:(k + 1) * tbw],
                        self.BIAS[:, gcol0 + k:gcol0 + k + 1], None,
                        ALU.mult),
                        waits=((tag + "_ap1", P.cnt(tag + "_ap1")),
                               (tag + "_st", P.cnt(tag + "_st"))),
                        inc=tag + "_ap")
                else:
                    P.do("vector", lambda e, k=k: e.tensor_copy(
                        self.OUTS[:, (k % 2) * 512:(k % 2) * 512 + tbw],
                        self.FB[:, k * tbw:(k + 1) * tbw]),
                        waits=((tag + "_ap1", P.cnt(tag + "_ap1")),
                               (tag + "_st", P.cnt(tag + "_st"))),
                        inc=tag + "_ap")
                P.do("sync", lambda e, k=k, t0=t0: e.dma_start(
                    out=dst_fm[k * 128:(k + 1) * 128, t0:t0 + tbw],
                    in_=self.OUTS[:, (k % 2) * 512:(k % 2) * 512 + tbw]),
                    waits=((tag + "_ap", P.cnt(tag + "_ap")),),
                    inc=tag + "_st", inc_n=16)
        P.barrier()

    def emit_fm_to_tm(self, tag, src_fm, dst_tm, D_, dtype_f32=False):
        P = self.P
        nK = D_ // 128
        ident = self.IDENT[:, :] if dtype_f32 else self.IDENTB[:, :]
        XS = self.XA if not dtype_f32 else None
        for tb in range(T // 512):
            t0 = tb * 512
            for k in range(nK):
                eng = "sync"
                buf = self.FA if dtype_f32 else self.XA
                P.do(eng, lambda e, k=k, t0=t0, buf=buf: e.dma_start(
                    out=buf[:, k * 512:(k + 1) * 512],
                    in_=src_fm[k * 128:(k + 1) * 128, t0:t0 + 512]),
                    inc=tag + "_ld", inc_n=16,
                    waits=((tag + "_mm", P.cnt(tag + "_mm")),)
                    if k == 0 else ())
            P.wait("tensor", tag + "_ld")
            for j in range(4):
                for half in range(max(1, nK // 4)):
                    bank = self.PS[half % 2]
                    P.wait("tensor", (tag + "_ev", P.cnt(tag + "_ev")))
                    nsub = min(4, nK - half * 4)
                    for kk in range(nsub):
                        k = half * 4 + kk
                        buf = self.FA if dtype_f32 else self.XA
                        P.do("tensor", lambda e, k=k, kk=kk, j=j, bank=bank,
                             buf=buf, ident=ident: e.matmul(
                                 bank[:, kk * 128:(kk + 1) * 128],
                                 buf[:, k * 512 + j * 128:
                                     k * 512 + (j + 1) * 128],
                                 ident, start=True, stop=True),
                             inc=(tag + "_mm") if kk == nsub - 1 else None)
                    out_sb = self.FB if dtype_f32 else self.OUTS
                    P.do("vector", lambda e, half=half, bank=bank, nsub=nsub,
                         out_sb=out_sb: e.tensor_copy(
                             out_sb[:, half * 512:half * 512 + nsub * 128],
                             bank[:, 0:nsub * 128]),
                        waits=((tag + "_mm", P.cnt(tag + "_mm")),),
                        inc=tag + "_ev")
                out_sb = self.FB if dtype_f32 else self.OUTS
                P.do("sync", lambda e, j=j, t0=t0, out_sb=out_sb:
                     e.dma_start(
                         out=dst_tm[t0 + j * 128:t0 + (j + 1) * 128, :],
                         in_=out_sb[:, 0:D_]),
                     waits=((tag + "_ev", P.cnt(tag + "_ev")),),
                     inc=tag + "_stt", inc_n=16)
                P.wait("vector", (tag + "_stt", P.cnt(tag + "_stt")))
        P.barrier()

    def emit_proj_tm(self, tag, act_hbm, Din, Dout, w_src, dst_tm,
                     pre_transposed=False, act_is_f32=False, epilogue=None,
                     out_f32=False):
        P = self.P
        nK = Din // 128
        assert nK * T <= 32768
        for k in range(nK):
            base = self.XA if k < 4 else self.XB
            eng = "gpsimd" if act_is_f32 else "sync"
            P.do(eng, lambda e, k=k, base=base: e.dma_start(
                out=base[:, (k % 4) * T:(k % 4) * T + T],
                in_=act_hbm[k * 128:(k + 1) * 128, :]),
                inc=tag + "_ald", inc_n=16)
        OUTB = self.FB if out_f32 else self.OUTS

        def act_view(k, t0):
            base = self.XA if k < 4 else self.XB
            return base[:, (k % 4) * T + t0:(k % 4) * T + t0 + 128]

        mm_base = P.cnt(tag + "_mm")
        for ob in range(cdiv(Dout, 512)):
            o0 = ob * 512
            ow = min(512, Dout - o0)
            h = (ob % 2) * 8192
            if ob >= 2:
                wld_wait = (tag + "_mm", mm_base + (ob - 1) * (T // 128))
            else:
                wld_wait = (tag + "_mm", 0)
            weng = "sync" if pre_transposed else "gpsimd"
            for k in range(nK):
                P.do(weng, lambda e, k=k, o0=o0, ow=ow, h=h: e.dma_start(
                    out=self.WA[:, h + k * 512:h + k * 512 + ow],
                    in_=w_src[k * 128:(k + 1) * 128, o0:o0 + ow]),
                    inc=tag + "_wld", inc_n=16,
                    waits=(wld_wait,) if k == 0 else ())
            for tt in range(T // 128):
                gi = P.cnt(tag + "_mm")
                bank = self.PS[gi % 4]
                P.wait("tensor",
                       (tag + "_evd", max(0, P.cnt(tag + "_evd") - 3)),
                       (tag + "_wld", P.cnt(tag + "_wld")),
                       (tag + "_ald", P.cnt(tag + "_ald")))
                for k in range(nK):
                    P.do("tensor", lambda e, k=k, h=h, bank=bank, tt=tt,
                         ow=ow: e.matmul(
                             bank[:, 0:ow], act_view(k, tt * 128),
                             self.WA[:, h + k * 512:h + k * 512 + ow],
                             start=(k == 0), stop=(k == nK - 1)),
                         inc=(tag + "_mm") if k == nK - 1 else None)
                oh = (gi % 2) * 512
                out_view = OUTB[:, oh:oh + ow]
                prewaits = ((tag + "_mm", P.cnt(tag + "_mm")),
                            (tag + "_st", max(0, P.cnt(tag + "_st") - 16)))
                if epilogue is None:
                    P.do("scalar", lambda e, bank=bank, out_view=out_view,
                         ow=ow: e.activation(out_view, bank[:, 0:ow],
                                             AF.Copy),
                         waits=prewaits, inc=tag + "_evd")
                else:
                    epilogue(self, tag, bank, ob, ow, (tt * 128,),
                             out_view, prewaits)
                P.do("sync", lambda e, tt=tt, o0=o0, ow=ow,
                     out_view=out_view: e.dma_start(
                         out=dst_tm[tt * 128:(tt + 1) * 128, o0:o0 + ow],
                         in_=out_view),
                     waits=(tag + "_evd",), inc=tag + "_st", inc_n=16)
        P.barrier()

    def emit_fp4_tile(self, tag, x_view, out_view, W_, ab, v, prewaits=()):
        P = self.P
        nb = W_ // 16
        AB = ab
        V = v
        ACC = ab
        S2 = self.ROWS[:, 0:nb]
        R = self.ROWS[:, 256:256 + nb]
        t = tag
        P.do("scalar", lambda e: e.activation(AB, x_view, AF.Abs),
             waits=prewaits + ((t + "_q", P.cnt(t + "_q")),), inc=t + "_qs")
        P.do("vector", lambda e: e.reduce_max(
            S2, AB.rearrange("p (b w) -> p b w", w=16), axis=AX.X),
            waits=((t + "_qs", P.cnt(t + "_qs")),), inc=t + "_q")
        P.do("vector", lambda e: e.tensor_scalar(
            S2, S2, 1.0 / 12.0, 0.5e-12, ALU.mult, ALU.max),
            waits=((t + "_q", P.cnt(t + "_q")),), inc=t + "_q")
        P.do("vector", lambda e: e.reciprocal(R, S2),
             waits=((t + "_q", P.cnt(t + "_q")),), inc=t + "_q")
        P.do("vector", lambda e: e.tensor_tensor(
            V.rearrange("p (b w) -> p b w", w=16),
            AB.rearrange("p (b w) -> p b w", w=16),
            R.broadcast_to([128, nb, 16]), ALU.mult),
            waits=((t + "_q", P.cnt(t + "_q")),), inc=t + "_q")
        thr = [0.5, 1.5, 2.5, 3.5, 5.0, 5.0, 7.0, 7.0,
               10.0, 10.0, 10.0, 10.0]
        P.do("vector", lambda e: e.tensor_single_scalar(
            ACC, V, thr[0], ALU.is_gt),
            waits=((t + "_q", P.cnt(t + "_q")),), inc=t + "_q")
        for m in thr[1:]:
            P.do("vector", lambda e, m=m: e.scalar_tensor_tensor(
                ACC, V, m, ACC, ALU.is_gt, ALU.add),
                waits=((t + "_q", P.cnt(t + "_q")),), inc=t + "_q")
        P.do("scalar", lambda e: e.activation(V, x_view, AF.Sign),
             waits=((t + "_q", P.cnt(t + "_q")),), inc=t + "_qs")
        P.do("vector", lambda e: e.tensor_tensor(ACC, ACC, V, ALU.mult),
             waits=((t + "_qs", P.cnt(t + "_qs")),), inc=t + "_q")
        P.do("vector", lambda e: e.tensor_tensor(
            out_view.rearrange("p (b w) -> p b w", w=16),
            ACC.rearrange("p (b w) -> p b w", w=16),
            S2.broadcast_to([128, nb, 16]), ALU.mult),
            waits=((t + "_q", P.cnt(t + "_q")),), inc=t + "_q")

    def emit_l0_ffn(self):
        P = self.P
        xq_fm = self.scratch("l0f_xq", (1024, T), BF16)
        hq_fm = self.scratch("l0f_hq", (2816, T), BF16)
        gate_tm = self.scratch("l0f_gate", (T, 2816))
        up_tm = self.scratch("l0f_up", (T, 2816))
        natq = self.scratch("l0f_natq", (2816, 2816), BF16)
        twq = {}
        for nm in ("gate", "up", "down"):
            I = 2816 if nm == "down" else 1024
            O = 1024 if nm == "down" else 2816
            twq[nm] = self.scratch("twq_" + nm, (I, O), BF16)
        for tb in range(T // 512):
            t0 = tb * 512
            for k in range(8):
                P.do("sync", lambda e, k=k, t0=t0: e.dma_start(
                    out=self.FA[:, k * 512:(k + 1) * 512],
                    in_=self.xn_fm[k * 128:(k + 1) * 128, t0:t0 + 512]),
                    inc="l0x_ld", inc_n=16,
                    waits=(("l0x_mm2", P.cnt("l0x_mm2")),) if k == 0 else ())
            P.wait("tensor", "l0x_ld")
            for j in range(4):
                for half in range(2):
                    bank = self.PS[half]
                    P.wait("tensor", ("l0x_ev", P.cnt("l0x_ev")))
                    for kk in range(4):
                        k = half * 4 + kk
                        P.do("tensor", lambda e, k=k, kk=kk, j=j, bank=bank:
                             e.matmul(bank[:, kk * 128:(kk + 1) * 128],
                                      self.FA[:, k * 512 + j * 128:
                                              k * 512 + (j + 1) * 128],
                                      self.IDENT[:, :], start=True, stop=True),
                             inc="l0x_mm" if kk == 3 else None)
                    P.do("vector", lambda e, half=half, bank=bank:
                         e.tensor_copy(
                             self.FB[:, half * 512:(half + 1) * 512],
                             bank[:, :]),
                         waits=(("l0x_mm", P.cnt("l0x_mm")),),
                         inc="l0x_ev")
                P.wait("vector", ("l0x_ev", P.cnt("l0x_ev")))
                self.emit_fp4_tile("l0x", self.FB[:, 0:1024],
                                   self.XA[:, 0:1024], 1024,
                                   self.FB[:, 1024:2048],
                                   self.FB[:, 2048:3072],
                                   prewaits=(("l0x_ev", P.cnt("l0x_ev")),))
                for half in range(2):
                    bank = self.PS[2 + half]
                    P.wait("tensor", ("l0x_ev2", P.cnt("l0x_ev2")),
                           ("l0x_q", P.cnt("l0x_q")))
                    for kk in range(4):
                        k = half * 4 + kk
                        P.do("tensor", lambda e, k=k, kk=kk, bank=bank:
                             e.matmul(bank[:, kk * 128:(kk + 1) * 128],
                                      self.XA[:, k * 128:(k + 1) * 128],
                                      self.IDENTB[:, :], start=True, stop=True),
                             inc="l0x_mm2" if kk == 3 else None)
                    P.do("vector", lambda e, half=half, bank=bank:
                         e.tensor_copy(
                             self.OUTS[:, half * 512:(half + 1) * 512],
                             bank[:, :]),
                         waits=(("l0x_mm2", P.cnt("l0x_mm2")),),
                         inc="l0x_ev2")
                    P.do("sync", lambda e, half=half, t0=t0, j=j:
                         e.dma_start(
                             out=xq_fm[half * 512:(half + 1) * 512,
                                       t0 + j * 128:t0 + (j + 1) * 128]
                             .rearrange("(c p) t -> p c t", p=128),
                             in_=self.OUTS[:, half * 512:(half + 1) * 512]
                             .rearrange("p (c t) -> p c t", t=128)),
                         waits=(("l0x_ev2", P.cnt("l0x_ev2")),),
                         inc="l0x_st", inc_n=16)
                P.wait("vector", ("l0x_st", P.cnt("l0x_st")))
        P.barrier()
        for nm, w_ap, O, I in (
                ("gate", self.IN["L0_ffn_w_gate"], 2816, 1024),
                ("up", self.IN["L0_ffn_w_up"], 2816, 1024),
                ("down", self.IN["L0_ffn_w_down"], 1024, 2816)):
            for r in range(O // 128):
                P.do("sync", lambda e, r=r, w_ap=w_ap, I=I: e.dma_start(
                    out=self.FA[:, 0:I], in_=w_ap[r * 128:(r + 1) * 128, :]),
                    inc="l0w_ld", inc_n=16,
                    waits=(("l0w_st", P.cnt("l0w_st")),))
                P.wait("vector", ("l0w_ld", P.cnt("l0w_ld")))
                self.emit_fp4_tile("l0w", self.FA[:, 0:I],
                                   self.XA[:, 0:I], I,
                                   self.FA[:, 2816:2816 + I],
                                   self.FB[:, 0:I],
                                   prewaits=(("l0w_ld", P.cnt("l0w_ld")),))
                P.do("sync", lambda e, r=r, I=I, natq=natq: e.dma_start(
                    out=natq[r * 128:(r + 1) * 128, 0:I],
                    in_=self.XA[:, 0:I]),
                    waits=(("l0w_q", P.cnt("l0w_q")),),
                    inc="l0w_st", inc_n=16)
            for i0 in range(0, I, 128):
                for o0 in range(0, O, 2048):
                    ff = min(2048, O - o0)
                    P.do("sync", lambda e, i0=i0, o0=o0, ff=ff, natq=natq:
                         e.dma_start_transpose(
                             self.WB[:, 0:ff], natq[o0:o0 + ff,
                                                    i0:i0 + 128]),
                         waits=("l0w_st",
                                ("l0w_tst", max(0, P.cnt("l0w_tst") - 16))),
                         inc="l0w_t", inc_n=16)
                    P.do("sync", lambda e, i0=i0, o0=o0, ff=ff, nm=nm:
                         e.dma_start(out=twq[nm][i0:i0 + 128, o0:o0 + ff],
                                     in_=self.WB[:, 0:ff]),
                         waits=("l0w_t",), inc="l0w_tst", inc_n=16)
            P.barrier()
        self.emit_proj_tm("l0g", xq_fm, 1024, 2816, twq["gate"], gate_tm,
                          pre_transposed=True, out_f32=True)
        self.emit_proj_tm("l0u", xq_fm, 1024, 2816, twq["up"], up_tm,
                          pre_transposed=True, out_f32=True)
        for tt in range(T // 128):
            P.do("sync", lambda e, tt=tt: e.dma_start(
                out=self.FA[:, 0:2816], in_=gate_tm[tt * 128:(tt + 1) * 128, :]),
                inc="l0h_ld", inc_n=16,
                waits=(("l0h_mm", P.cnt("l0h_mm")),))
            P.do("sync", lambda e, tt=tt: e.dma_start(
                out=self.FA[:, 4096:4096 + 2816],
                in_=up_tm[tt * 128:(tt + 1) * 128, :]),
                inc="l0h_ld", inc_n=16)
            P.wait("scalar", "l0h_ld")
            P.do("scalar", lambda e: e.activation(
                self.FA[:, 0:2816], self.FA[:, 0:2816], AF.Silu),
                waits=(("l0h_q", P.cnt("l0h_q")),), inc="l0h_si")
            P.do("vector", lambda e: e.tensor_tensor(
                self.FA[:, 0:2816], self.FA[:, 0:2816],
                self.FA[:, 4096:4096 + 2816], ALU.mult),
                waits=(("l0h_si", P.cnt("l0h_si")),), inc="l0h_hm")
            P.wait("vector", ("l0h_hm", P.cnt("l0h_hm")))
            self.emit_fp4_tile("l0h", self.FA[:, 0:2816],
                               self.XA[:, 0:2816], 2816,
                               self.FA[:, 2816:5632],
                               self.FB[:, 0:2816],
                               prewaits=(("l0h_hm", P.cnt("l0h_hm")),))
            for g in range(6):
                bank = self.PS[4 + (g % 2)]
                nsub = min(4, 22 - g * 4)
                P.wait("tensor", ("l0h_ev", P.cnt("l0h_ev")),
                       ("l0h_q", P.cnt("l0h_q")))
                for kk in range(nsub):
                    k = g * 4 + kk
                    P.do("tensor", lambda e, k=k, kk=kk, bank=bank:
                         e.matmul(bank[:, kk * 128:(kk + 1) * 128],
                                  self.XA[:, k * 128:(k + 1) * 128],
                                  self.IDENTB[:, :], start=True, stop=True),
                         inc="l0h_mm" if kk == nsub - 1 else None)
                P.do("vector", lambda e, bank=bank, nsub=nsub, g=g:
                     e.tensor_copy(
                         self.OUTS[:, 0:nsub * 128], bank[:, 0:nsub * 128]),
                     waits=(("l0h_mm", P.cnt("l0h_mm")),
                            ("l0h_st", P.cnt("l0h_st"))),
                     inc="l0h_ev")
                P.do("sync", lambda e, g=g, tt=tt, nsub=nsub: e.dma_start(
                    out=hq_fm[g * 512:g * 512 + nsub * 128,
                              tt * 128:(tt + 1) * 128]
                    .rearrange("(c p) t -> p c t", p=128),
                    in_=self.OUTS[:, 0:nsub * 128]
                    .rearrange("p (c t) -> p c t", t=128)),
                    waits=(("l0h_ev", P.cnt("l0h_ev")),),
                    inc="l0h_st", inc_n=16)
        P.barrier()
        self.emit_proj("l0d", hq_fm, 2816, 1024, twq["down"], self.g_fm,
                       pre_transposed=True, act_is_f32=False,
                       epilogue=self.ep_bias_scale(None, 1.0, AF.Silu),
                       o_slab=256, act_resident=False, t_block=512)

    def emit_l2_ffn(self, phases=(0, 1, 2, 3, 4, 5, 6, 7, 8)):
        P = self.P
        phases = set(phases)
        xn2_fm = self.scratch("l2f_xn2", (1024, T), BF16)
        xn2_tm = self.scratch("l2f_xn2tm", (T, 1024), BF16)
        se_nat = self.scratch("l2f_senat", (4096, 1024), BF16)
        se_fm = self.scratch("l2f_se", (1024, 4096), BF16)
        L_scr = self.scratch("l2f_L", (T, 4096))
        pc_scr = self.scratch("l2f_pc", (T, 4096), BF16)
        disp_scr = self.scratch("l2f_disp", (T, 4096), BF16)
        slots_fm = self.scratch("l2f_slots", (1024, 4096), BF16)
        h2_fm = self.scratch("l2f_h2", (4096, T), BF16)
        proc_fm = self.scratch("l2f_proc", (1024, T), BF16)
        proc_tm = self.scratch("l2f_proctm", (T, 1024), BF16)
        maxrow_scr = self.scratch("l2f_maxrow", (32, 128))

        if 0 in phases:
            self.emit_rms_fm("l2x", self.xn_fm, xn2_fm, 1024,
                         self.IN["L2_ffn_gamma_x"], "maxnorm",
                         eps=1e-12, outscale=32.0)
            self.emit_fm_to_tm("l2xt", xn2_fm, xn2_tm, 1024)

        if 1 not in phases:
            return
        gs = self.IN["L2_ffn_gamma_slot"]
        se_in = self.IN["L2_ffn_slot_embeds"].rearrange("e t d -> (e t) d")
        P.do("sync", lambda e: e.dma_start(
            out=self.COEFR[0:1, 0:1024], in_=gs[:]),
            inc="l2s_gld", inc_n=16)
        for c in range(2):
            P.do("tensor", lambda e, c=c: e.matmul(
                self.PS[6][:, 0:512], self.ONESROWF[0:1, :],
                self.COEFR[0:1, c * 512:(c + 1) * 512],
                start=True, stop=True),
                waits=("l2s_gld", ("l2s_gev", P.cnt("l2s_gev"))),
                inc="l2s_gbc")
            P.do("vector", lambda e, c=c: e.tensor_copy(
                self.FB[:, c * 512:(c + 1) * 512], self.PS[6][:, 0:512]),
                waits=(("l2s_gbc", P.cnt("l2s_gbc")),), inc="l2s_gev")
        for r in range(32):
            P.do("sync", lambda e, r=r: e.dma_start(
                out=self.FA[:, 0:1024], in_=se_in[r * 128:(r + 1) * 128, :]),
                inc="l2s_ld", inc_n=16,
                waits=(("l2s_st", P.cnt("l2s_st")),))
            P.do("vector", lambda e: e.scalar_tensor_tensor(
                self.FA[:, 1024:2048], self.FA[:, 0:1024], 1.0,
                self.FA[:, 0:1024], ALU.mult, ALU.mult,
                accum_out=self.ROWS[:, 0:1]),
                waits=(("l2s_ld", P.cnt("l2s_ld")), "l2s_gev",
                       ("l2s_q", P.cnt("l2s_q"))),
                inc="l2s_v")
            P.do("scalar", lambda e: e.activation(
                self.ROWS[:, 1:2], self.ROWS[:, 0:1], AF.Sqrt),
                waits=(("l2s_v", P.cnt("l2s_v")),), inc="l2s_c")
            P.do("vector", lambda e: e.tensor_scalar(
                self.ROWS[:, 2:3], self.ROWS[:, 1:2], 1e-12, None, ALU.max),
                waits=(("l2s_c", P.cnt("l2s_c")),), inc="l2s_v")
            P.do("vector", lambda e: e.reciprocal(
                self.ROWS[:, 3:4], self.ROWS[:, 2:3]),
                waits=(("l2s_v", P.cnt("l2s_v")),), inc="l2s_v")
            P.do("vector", lambda e: e.tensor_scalar(
                self.ROWS[:, 3:4], self.ROWS[:, 3:4], 32.0, None, ALU.mult),
                waits=(("l2s_v", P.cnt("l2s_v")),), inc="l2s_v")
            P.do("vector", lambda e: e.tensor_tensor(
                self.FA[:, 1024:2048], self.FA[:, 0:1024],
                self.FB[:, 0:1024], ALU.mult),
                waits=(("l2s_v", P.cnt("l2s_v")),), inc="l2s_v")
            P.do("vector", lambda e: e.tensor_scalar(
                self.XA[:, 0:1024], self.FA[:, 1024:2048],
                self.ROWS[:, 3:4], None, ALU.mult),
                waits=(("l2s_v", P.cnt("l2s_v")),), inc="l2s_q")
            P.do("sync", lambda e, r=r: e.dma_start(
                out=se_nat[r * 128:(r + 1) * 128, :],
                in_=self.XA[:, 0:1024]),
                waits=(("l2s_q", P.cnt("l2s_q")),),
                inc="l2s_st", inc_n=16)
        for i0 in range(0, 1024, 128):
            for o0 in range(0, 4096, 2048):
                P.do("sync", lambda e, i0=i0, o0=o0: e.dma_start_transpose(
                    self.WB[:, 0:2048], se_nat[o0:o0 + 2048, i0:i0 + 128]),
                    waits=("l2s_st", ("l2s_t2", P.cnt("l2s_t2"))),
                    inc="l2s_t", inc_n=16)
                P.do("sync", lambda e, i0=i0, o0=o0: e.dma_start(
                    out=se_fm[i0:i0 + 128, o0:o0 + 2048],
                    in_=self.WB[:, 0:2048]),
                    waits=("l2s_t",), inc="l2s_t2", inc_n=16)
        P.barrier()
        if 2 not in phases:
            return
        for k in range(8):
            base = self.XA if k < 4 else self.XB
            P.do("sync", lambda e, k=k, base=base: e.dma_start(
                out=base[:, (k % 4) * T:(k % 4) * T + T],
                in_=xn2_fm[k * 128:(k + 1) * 128, :]),
                inc="l2L_ald", inc_n=16)
            base2 = self.WA if k < 4 else self.WB
            P.do("gpsimd", lambda e, k=k, base2=base2: e.dma_start(
                out=base2[:, (k % 4) * 4096:(k % 4) * 4096 + 4096],
                in_=se_fm[k * 128:(k + 1) * 128, :]),
                inc="l2L_ald", inc_n=16)
        MAXC = self.ROWS[:, 32:64]
        TMPR = self.ROWS[:, 64:68]
        for st_ in range(32):
            s0 = st_ * 128
            for nb in range(8):
                gi = P.cnt("l2L_mm")
                bank = self.PS[gi % 2]
                P.wait("tensor", ("l2L_ev", max(0, P.cnt("l2L_ev") - 1)),
                       ("l2L_ald", P.cnt("l2L_ald")))
                for k in range(8):
                    xbase = self.XA if k < 4 else self.XB
                    wbase = self.WA if k < 4 else self.WB
                    P.do("tensor", lambda e, k=k, nb=nb, s0=s0, bank=bank,
                         xbase=xbase, wbase=wbase: e.matmul(
                             bank[:, 0:512],
                             xbase[:, (k % 4) * T + s0:(k % 4) * T + s0 + 128],
                             wbase[:, (k % 4) * 4096 + nb * 512:
                                   (k % 4) * 4096 + (nb + 1) * 512],
                             start=(k == 0), stop=(k == 7)),
                         inc="l2L_mm" if k == 7 else None)
                P.do("scalar", lambda e, nb=nb, bank=bank: e.activation(
                    self.FA[:, nb * 512:(nb + 1) * 512], bank[:, 0:512],
                    AF.Copy),
                    waits=(("l2L_mm", P.cnt("l2L_mm")),
                           ("l2L_tp", P.cnt("l2L_tp")),
                           ("l2L_st", P.cnt("l2L_st"))),
                    inc="l2L_ev")
            P.do("sync", lambda e, s0=s0: e.dma_start(
                out=L_scr[s0:s0 + 128, :], in_=self.FA[:, 0:4096]),
                waits=("l2L_ev",), inc="l2L_st", inc_n=16)
            for nb in range(8):
                bank = self.PS[2 + (nb % 2)]
                P.wait("tensor", ("l2L_mx", P.cnt("l2L_mx")))
                for j in range(4):
                    P.do("tensor", lambda e, nb=nb, j=j, bank=bank: e.matmul(
                        bank[:, j * 128:(j + 1) * 128],
                        self.FA[:, nb * 512 + j * 128:
                                nb * 512 + (j + 1) * 128],
                        self.IDENT[:, :], start=True, stop=True),
                        waits=(("l2L_ev", P.cnt("l2L_ev")),)
                        if j == 0 else (),
                        inc="l2L_tp" if j == 3 else None)
                if st_ == 0:
                    P.do("vector", lambda e, nb=nb, bank=bank: e.reduce_max(
                        MAXC[:, nb * 4:(nb + 1) * 4],
                        bank.rearrange("p (a b) -> p a b", b=128),
                        axis=AX.X),
                        waits=(("l2L_tp", P.cnt("l2L_tp")),),
                        inc="l2L_mx")
                else:
                    P.do("vector", lambda e, nb=nb, bank=bank: e.reduce_max(
                        TMPR[:, 0:4],
                        bank.rearrange("p (a b) -> p a b", b=128),
                        axis=AX.X),
                        waits=(("l2L_tp", P.cnt("l2L_tp")),),
                        inc="l2L_mx")
                    P.do("vector", lambda e, nb=nb: e.tensor_tensor(
                        MAXC[:, nb * 4:(nb + 1) * 4],
                        MAXC[:, nb * 4:(nb + 1) * 4],
                        TMPR[:, 0:4], ALU.max),
                        waits=(("l2L_mx", P.cnt("l2L_mx")),),
                        inc="l2L_mx")
        P.barrier()
        if 3 not in phases:
            return
        P.do("tensor", lambda e: e.matmul(
            self.PS[4][0:32, 0:128], MAXC, self.IDENT[:, :],
            start=True, stop=True), inc="l2B_t")
        P.do("vector", lambda e: e.tensor_scalar(
            self.FC[0:32, 0:128], self.PS[4][0:32, 0:128], -1.0, None,
            ALU.mult),
            waits=("l2B_t",), inc="l2B_v")
        P.do("sync", lambda e: e.dma_start(
            out=maxrow_scr[:, :], in_=self.FC[0:32, 0:128]),
            waits=("l2B_v",), inc="l2B_st", inc_n=16)
        P.do("sync", lambda e: e.dma_start(
            out=self.COEFR[0:1, 0:4096],
            in_=maxrow_scr.rearrange("a b -> (a b)")),
            waits=("l2B_st",), inc="l2B_ld", inc_n=16)
        for c in range(8):
            P.do("tensor", lambda e, c=c: e.matmul(
                self.PS[4][:, 0:512], self.ONESROWF[0:1, :],
                self.COEFR[0:1, c * 512:(c + 1) * 512],
                start=True, stop=True),
                waits=("l2B_ld", ("l2B_ev", P.cnt("l2B_ev"))),
                inc="l2B_bc")
            P.do("vector", lambda e, c=c: e.tensor_copy(
                self.FB[:, c * 512:(c + 1) * 512], self.PS[4][:, 0:512]),
                waits=(("l2B_bc", P.cnt("l2B_bc")),), inc="l2B_ev")
        P.barrier()
        if 4 not in phases:
            return
        for st_ in range(32):
            s0 = st_ * 128
            P.do("sync", lambda e, s0=s0: e.dma_start(
                out=self.FA[:, 0:4096], in_=L_scr[s0:s0 + 128, :]),
                inc="l2C_ld", inc_n=16,
                waits=(("l2C_dx", P.cnt("l2C_dx")),))
            P.do("vector", lambda e: e.reduce_max(
                self.ROWS[:, 0:1], self.FA[:, 0:4096], axis=AX.X,
                negate=True),
                waits=(("l2C_ld", P.cnt("l2C_ld")),
                       ("l2C_st", P.cnt("l2C_st"))),
                inc="l2C_v")
            P.do("scalar", lambda e: e.activation(
                self.XB[:, 0:4096], self.FA[:, 0:4096], AF.Exp,
                bias=self.ROWS[:, 0:1], accum_out=self.ROWS[:, 1:2]),
                waits=(("l2C_v", P.cnt("l2C_v")),), inc="l2C_x")
            P.do("vector", lambda e: e.reciprocal(
                self.ROWS[:, 2:3], self.ROWS[:, 1:2]),
                waits=(("l2C_x", P.cnt("l2C_x")),), inc="l2C_v")
            P.do("vector", lambda e: e.tensor_scalar(
                self.XB[:, 0:4096], self.XB[:, 0:4096],
                self.ROWS[:, 2:3], None, ALU.mult),
                waits=(("l2C_v", P.cnt("l2C_v")),), inc="l2C_pc")
            P.do("sync", lambda e, s0=s0: e.dma_start(
                out=pc_scr[s0:s0 + 128, :], in_=self.XB[:, 0:4096]),
                waits=(("l2C_pc", P.cnt("l2C_pc")),),
                inc="l2C_st", inc_n=16)
            P.do("vector", lambda e: e.tensor_tensor(
                self.FA[:, 0:4096], self.FA[:, 0:4096],
                self.FB[:, 0:4096], ALU.add),
                waits=(("l2C_x", P.cnt("l2C_x")),), inc="l2C_v")
            P.do("scalar", lambda e: e.activation(
                self.XB[:, 4096:8192], self.FA[:, 0:4096], AF.Exp),
                waits=(("l2C_v", P.cnt("l2C_v")),
                       ("l2C_st", max(0, P.cnt("l2C_st") - 16))),
                inc="l2C_dx")
            P.do("sync", lambda e, s0=s0: e.dma_start(
                out=disp_scr[s0:s0 + 128, :], in_=self.XB[:, 4096:8192]),
                waits=(("l2C_dx", P.cnt("l2C_dx")),),
                inc="l2C_st", inc_n=16)
        P.barrier()
        if 5 not in phases:
            return
        for c in range(32):
            base = self.XB if c < 16 else self.WA
            off = (c % 16) * 1024
            P.do("sync", lambda e, c=c, base=base, off=off: e.dma_start(
                out=base[:, off:off + 1024],
                in_=xn2_tm[c * 128:(c + 1) * 128, :]),
                inc="l2S_tld", inc_n=16)
        ones_col = self.const(1.0)
        for nb in range(8):
            for c in range(32):
                P.do("gpsimd", lambda e, c=c, nb=nb: e.dma_start(
                    out=self.XA[:, c * 512:(c + 1) * 512],
                    in_=disp_scr[c * 128:(c + 1) * 128,
                                 nb * 512:(nb + 1) * 512]),
                    inc="l2S_dld", inc_n=16,
                    waits=(("l2S_mm", P.cnt("l2S_mm")),
                           ("l2S_sm", P.cnt("l2S_sm"))) if c == 0 else ())
            P.wait("tensor", ("l2S_dld", P.cnt("l2S_dld")),
                   ("l2S_tld", P.cnt("l2S_tld")))
            for d in range(8):
                bank = self.PS[d % 2]
                P.wait("tensor", ("l2S_ev", max(0, P.cnt("l2S_ev") - 1)))
                for c in range(32):
                    base = self.XB if c < 16 else self.WA
                    off = (c % 16) * 1024 + d * 128
                    P.do("tensor", lambda e, c=c, base=base, off=off,
                         bank=bank: e.matmul(
                             bank[:, 0:512], base[:, off:off + 128],
                             self.XA[:, c * 512:(c + 1) * 512],
                             start=(c == 0), stop=(c == 31)),
                         inc="l2S_mm" if c == 31 else None)
                P.do("scalar", lambda e, bank=bank: e.activation(
                    self.OUTS[:, 0:512], bank[:, 0:512], AF.Copy),
                    waits=(("l2S_mm", P.cnt("l2S_mm")),
                           ("l2S_sst", P.cnt("l2S_sst"))),
                    inc="l2S_ev")
                P.do("sync", lambda e, d=d, nb=nb: e.dma_start(
                    out=slots_fm[d * 128:(d + 1) * 128,
                                 nb * 512:(nb + 1) * 512],
                    in_=self.OUTS[:, 0:512]),
                    waits=(("l2S_ev", P.cnt("l2S_ev")),),
                    inc="l2S_sst", inc_n=16)
            P.wait("tensor", ("l2S_sm", P.cnt("l2S_sm")))
            for c in range(32):
                P.do("tensor", lambda e, c=c: e.matmul(
                    self.PS[2][0:1, 0:512], self.ONESCOLB[:, 0:1],
                    self.XA[:, c * 512:(c + 1) * 512],
                    start=(c == 0), stop=(c == 31)),
                    inc="l2S_smm" if c == 31 else None)
            P.do("vector", lambda e, nb=nb: e.tensor_copy(
                self.COEFR[0:1, nb * 512:(nb + 1) * 512],
                self.PS[2][0:1, 0:512]),
                waits=(("l2S_smm", P.cnt("l2S_smm")),), inc="l2S_sm")
        P.do("vector", lambda e: e.reciprocal(
            self.COEFR[0:1, 0:4096], self.COEFR[0:1, 0:4096]),
            waits=(("l2S_sm", P.cnt("l2S_sm")),), inc="l2S_rc")
        for c in range(8):
            P.do("tensor", lambda e, c=c: e.matmul(
                self.PS[3][:, 0:512], self.ONESROWF[0:1, :],
                self.COEFR[0:1, c * 512:(c + 1) * 512],
                start=True, stop=True),
                waits=(("l2S_rc", P.cnt("l2S_rc")),
                       ("l2S_rbe", P.cnt("l2S_rbe"))),
                inc="l2S_rb")
            P.do("vector", lambda e, c=c: e.tensor_copy(
                self.FB[:, c * 512:(c + 1) * 512], self.PS[3][:, 0:512]),
                waits=(("l2S_rb", P.cnt("l2S_rb")),), inc="l2S_rbe")
        P.barrier()
        if 6 not in phases:
            return
        def ep_mult_gelu(b, tag, bank, oc_abs, ocw, tcols, out_view,
                         prewaits):
            Pq = b.P
            tw = tcols[1] - tcols[0]
            gi = Pq.cnt(tag + "_evd")
            r = b.FC[:, (gi % 2) * 512:(gi % 2) * 512 + tw]
            Pq.do("vector", lambda e: e.tensor_tensor(
                r, bank[:, 0:tw], b.FB[:, tcols[0]:tcols[1]], ALU.mult),
                waits=prewaits + ((tag + "_epv",
                                   max(0, Pq.cnt(tag + "_epv") - 1)),),
                inc=tag + "_epm")
            Pq.do("scalar", lambda e: e.activation(out_view, r, AF.Gelu),
                  waits=((tag + "_epm", Pq.cnt(tag + "_epm")),),
                  inc=tag + "_epv")
            Pq.do("scalar", lambda e: e.nop(),
                  waits=((tag + "_epv", Pq.cnt(tag + "_epv")),),
                  inc=tag + "_evd")
        self.emit_proj("l2f1", slots_fm, 1024, 4096,
                       self.IN["L2_ffn_w1"], h2_fm,
                       pre_transposed=False, act_is_f32=False,
                       epilogue=ep_mult_gelu)
        self.emit_proj("l2f2", h2_fm, 4096, 1024, self.IN["L2_ffn_w2"],
                       proc_fm, pre_transposed=False, act_is_f32=False,
                       o_slab=256, t_block=512, act_resident=False)
        self.emit_fm_to_tm("l2pt", proc_fm, proc_tm, 1024)
        if 8 not in phases:
            return
        for c in range(32):
            base = self.WA if c < 16 else self.WB
            off = (c % 16) * 1024
            P.do("sync", lambda e, c=c, base=base, off=off: e.dma_start(
                out=base[:, off:off + 1024],
                in_=proc_tm[c * 128:(c + 1) * 128, :]),
                inc="l2O_pld", inc_n=16)
        for sb in range(8):
            s0 = sb * 512
            for j in range(4):
                P.do("gpsimd", lambda e, j=j, s0=s0: e.dma_start(
                    out=self.XA[:, j * 4096:(j + 1) * 4096],
                    in_=pc_scr[s0 + j * 128:s0 + (j + 1) * 128, :]),
                    inc="l2O_cld", inc_n=16,
                    waits=(("l2O_tp", P.cnt("l2O_tp")),) if j == 0 else ())
            P.wait("tensor", ("l2O_cld", P.cnt("l2O_cld")))
            for nc_ in range(32):
                bank = self.PS[4 + (nc_ % 2)]
                P.wait("tensor", ("l2O_te", max(0, P.cnt("l2O_te") - 1)))
                for j in range(4):
                    P.do("tensor", lambda e, nc_=nc_, j=j, bank=bank:
                         e.matmul(
                             bank[:, j * 128:(j + 1) * 128],
                             self.XA[:, j * 4096 + nc_ * 128:
                                     j * 4096 + (nc_ + 1) * 128],
                             self.IDENTB[:, :], start=True, stop=True),
                         inc="l2O_tp" if j == 3 else None)
                P.do("vector", lambda e, nc_=nc_, bank=bank: e.tensor_copy(
                    self.XB[:, nc_ * 512:(nc_ + 1) * 512], bank[:, :]),
                    waits=(("l2O_tp", P.cnt("l2O_tp")),
                           ("l2O_mm", P.cnt("l2O_mm"))),
                    inc="l2O_te")
            P.wait("tensor", ("l2O_te", P.cnt("l2O_te")),
                   ("l2O_pld", P.cnt("l2O_pld")))
            for d in range(8):
                bank = self.PS[d % 2]
                P.wait("tensor", ("l2O_oe", max(0, P.cnt("l2O_oe") - 1)))
                for c in range(32):
                    base = self.WA if c < 16 else self.WB
                    off = (c % 16) * 1024 + d * 128
                    P.do("tensor", lambda e, c=c, base=base, off=off,
                         bank=bank: e.matmul(
                             bank[:, 0:512], base[:, off:off + 128],
                             self.XB[:, c * 512:(c + 1) * 512],
                             start=(c == 0), stop=(c == 31)),
                         inc="l2O_mm" if c == 31 else None)
                P.do("scalar", lambda e, bank=bank: e.activation(
                    self.OUTS[:, 0:512], bank[:, 0:512], AF.Silu),
                    waits=(("l2O_mm", P.cnt("l2O_mm")),
                           ("l2O_gst", P.cnt("l2O_gst"))),
                    inc="l2O_oe")
                P.do("sync", lambda e, d=d, s0=s0: e.dma_start(
                    out=self.g_fm[d * 128:(d + 1) * 128, s0:s0 + 512],
                    in_=self.OUTS[:, 0:512]),
                    waits=(("l2O_oe", P.cnt("l2O_oe")),),
                    inc="l2O_gst", inc_n=16)
        P.barrier()

    def emit_l1_attn(self):
        P = self.P
        h_fm = self.scratch("g1_h", (1024, T), BF16)
        q_fm = self.scratch("g1_q", (512, T), BF16)
        k_fm = self.scratch("g1_k", (512, T), BF16)
        v_fm = self.scratch("g1_v", (1024, T), BF16)
        go_fm = self.scratch("g1_go", (1024, T), BF16)
        gl_fm = self.scratch("g1_gl", (512, T))
        o_fm = self.scratch("g1_o", (1024, T), BF16)
        ao_fm = self.scratch("g1_ao", (1024, T), BF16)
        x2_fm = self.scratch("g1_x2", (1024, T))
        n_fm = self.scratch("g1_n", (1024, T), BF16)
        mg_fm = self.scratch("g1_mg", (2816, T), BF16)
        mu_fm = self.scratch("g1_mu", (2816, T), BF16)
        mid_fm = self.scratch("g1_mid", (2816, T), BF16)

        self.emit_rms_fm("g1h", self.xn_fm, h_fm, 1024,
                         self.IN["L1_attn_attn_norm_w"], "rms", eps=1e-6)
        self.emit_proj("g1q", h_fm, 1024, 512, self.IN["L1_attn_Wq"],
                       q_fm, act_is_f32=False)
        self.emit_proj("g1k", h_fm, 1024, 512, self.IN["L1_attn_Wk"],
                       k_fm, act_is_f32=False)
        self.emit_proj("g1v", h_fm, 1024, 1024, self.IN["L1_attn_Wv"],
                       v_fm, act_is_f32=False)
        self.emit_proj("g1g", h_fm, 1024, 1024, self.IN["L1_attn_Wg"],
                       go_fm, act_is_f32=False)
        for k in range(8):
            P.do("gpsimd", lambda e, k=k: e.dma_start(
                out=self.WA[:, k * 16:(k + 1) * 16],
                in_=self.IN["L1_attn_Wgk1"][k * 128:(k + 1) * 128, :]),
                inc="g1l_w1", inc_n=16)
            base = self.XA if k < 4 else None
            P.do("sync", lambda e, k=k: e.dma_start(
                out=(self.XA[:, (k % 4) * T:(k % 4) * T + T] if k < 4 else
                     self.WB[:, (k % 4) * T:(k % 4) * T + T]),
                in_=h_fm[k * 128:(k + 1) * 128, :]),
                inc="g1l_h", inc_n=16)
        P.do("gpsimd", lambda e: e.dma_start(
            out=self.WA[:, 256:256 + 64].rearrange("p (a b) -> p a b", a=4)
            if False else self.WA[0:16, 256:256 + 512],
            in_=self.IN["L1_attn_Wgk2"][:, :]),
            inc="g1l_w2", inc_n=16)
        for oc in range(4):
            P.do("gpsimd", lambda e, oc=oc: e.dma_start(
                out=self.BIAS[:, 8 + oc:9 + oc],
                in_=self.IN["L1_attn_bgk2"][oc * 128:(oc + 1) * 128]),
                inc="g1l_b", inc_n=16)
        P.wait("tensor", "g1l_w1", "g1l_h")
        for tb in range(8):
            t0 = tb * 512
            for k in range(8):
                src_t = (self.XA if k < 4 else self.WB)
                P.do("tensor", lambda e, k=k, t0=t0, src_t=src_t: e.matmul(
                    self.PS[5][0:16, 0:512],
                    self.WA[:, k * 16:(k + 1) * 16],
                    src_t[:, (k % 4) * T + t0:(k % 4) * T + t0 + 512],
                    start=(k == 0), stop=(k == 7)),
                    waits=(("g1l_u", P.cnt("g1l_u")),) if k == 0 else (),
                    inc="g1l_m1" if k == 7 else None)
            P.do("vector", lambda e, tb=tb: e.tensor_copy(
                self.XB[0:16, tb * 512:(tb + 1) * 512],
                self.PS[5][0:16, 0:512]),
                waits=(("g1l_m1", P.cnt("g1l_m1")),), inc="g1l_u")
        ln1 = self.const(1.0)
        P.wait("tensor", "g1l_u", "g1l_w2")
        for tb in range(8):
            t0 = tb * 512
            for oc in range(4):
                P.do("tensor", lambda e, oc=oc, t0=t0: e.matmul(
                    self.PS[6][:, 0:512],
                    self.WA[0:16, 256 + oc * 128:256 + (oc + 1) * 128],
                    self.XB[0:16, t0:t0 + 512], start=True, stop=True),
                    waits=(("g1l_ev", P.cnt("g1l_ev")),),
                    inc="g1l_m2")
                FAv = self.FA
                P.do("scalar", lambda e, oc=oc: e.activation(
                    FAv[:, 0:512], self.PS[6][:, 0:512], AF.Identity,
                    bias=self.BIAS[:, 8 + oc:9 + oc]),
                    waits=(("g1l_m2", P.cnt("g1l_m2")),
                           ("g1l_gv", P.cnt("g1l_gv")),
                           ("g1l_b", P.cnt("g1l_b"))),
                    inc="g1l_ev")
                P.do("scalar", lambda e: e.activation(
                    FAv[:, 512:1024], FAv[:, 0:512], AF.Abs),
                    waits=(("g1l_ev", P.cnt("g1l_ev")),), inc="g1l_s")
                P.do("scalar", lambda e: e.activation(
                    FAv[:, 512:1024], FAv[:, 512:1024], AF.Exp, scale=-1.0),
                    waits=(("g1l_s", P.cnt("g1l_s")),), inc="g1l_s")
                P.do("scalar", lambda e: e.activation(
                    FAv[:, 512:1024], FAv[:, 512:1024], AF.Ln, bias=ln1),
                    waits=(("g1l_s", P.cnt("g1l_s")),), inc="g1l_s")
                P.do("vector", lambda e: e.tensor_scalar(
                    FAv[:, 1024:1536], FAv[:, 0:512], 0.0, None, ALU.min),
                    waits=(("g1l_s", P.cnt("g1l_s")),), inc="g1l_gv")
                P.do("vector", lambda e: e.tensor_tensor(
                    FAv[:, 1024:1536], FAv[:, 1024:1536], FAv[:, 512:1024],
                    ALU.subtract),
                    waits=(("g1l_gv", P.cnt("g1l_gv")),), inc="g1l_gv")
                P.do("vector", lambda e: e.tensor_scalar(
                    FAv[:, 1024:1536], FAv[:, 1024:1536], 1.0 / 16.0, None,
                    ALU.mult),
                    waits=(("g1l_gv", P.cnt("g1l_gv")),), inc="g1l_gv")
                P.do("sync", lambda e, oc=oc, t0=t0: e.dma_start(
                    out=gl_fm[oc * 128:(oc + 1) * 128, t0:t0 + 512],
                    in_=self.FA[:, 1024:1536]),
                    waits=(("g1l_gv", P.cnt("g1l_gv")),),
                    inc="g1l_st", inc_n=16)
                P.wait("vector", ("g1l_st", P.cnt("g1l_st")))
        P.barrier()

        P.do("sync", lambda e: e.dma_start(
            out=self.COEFR[0:1, 0:256], in_=self.IN["L1_attn_gnorm_w"][:]),
            inc="gc_gn", inc_n=16)
        P.do("tensor", lambda e: e.matmul(
            self.PS[7][:, 0:256], self.ONESROWF[0:1, :],
            self.COEFR[0:1, 0:256], start=True, stop=True),
            waits=("gc_gn",), inc="gc_gnb")
        P.do("vector", lambda e: e.tensor_copy(
            self.FC[:, 256:512], self.PS[7][:, 0:256]),
            waits=(("gc_gnb", P.cnt("gc_gnb")),), inc="gc_gne")

        P.barrier()
        LNC = float(np.log(128.0 ** -0.5))
        lnc_col = self.const(LNC)
        for hd in range(4):
            P.do("sync", lambda e, hd=hd: e.dma_start(
                out=self.XA[:, 0:T], in_=q_fm[hd * 128:(hd + 1) * 128, :]),
                inc="gc_ld", inc_n=16)
            P.do("sync", lambda e, hd=hd: e.dma_start(
                out=self.XA[:, T:2 * T], in_=k_fm[hd * 128:(hd + 1) * 128, :]),
                inc="gc_ld", inc_n=16)
            for half in range(2):
                P.do("sync", lambda e, hd=hd, half=half: e.dma_start(
                    out=self.XB[:, half * T:(half + 1) * T],
                    in_=v_fm[hd * 256 + half * 128:
                             hd * 256 + half * 128 + 128, :]),
                    inc="gc_ld", inc_n=16)
            P.do("sync", lambda e, hd=hd: e.dma_start(
                out=self.FB[:, 0:T], in_=gl_fm[hd * 128:(hd + 1) * 128, :]),
                inc="gc_ld", inc_n=16)
            P.wait("vector", "gc_ld")
            for c in range(64):
                P.do("vector", lambda e, c=c: e.tensor_tensor_scan(
                    self.FA[:, 4096 + c * 64:4096 + (c + 1) * 64],
                    self.FB[:, c * 64:(c + 1) * 64],
                    self.FB[:, c * 64:(c + 1) * 64], 0.0,
                    ALU.add, ALU.bypass),
                    inc="gc_b" if c == 63 else None)
            P.do("vector", lambda e: e.memset(self.FC[:, 0:256], 0.0),
                 inc="gc_s0")
            P.wait("scalar", "gc_b", "gc_ld")
            for c in range(64):
                t0 = c * 64
                B_ = self.FA[:, 4096 + t0:4096 + t0 + 64]
                P.do("scalar", lambda e, B_=B_: e.activation(
                    self.FA[:, 0:64], B_, AF.Exp, bias=lnc_col),
                    waits=(("gc_i5", P.cnt("gc_i5")),), inc="gc_e1")
                P.do("vector", lambda e, t0=t0: e.tensor_tensor(
                    self.FA[:, 64:128], self.XA[:, t0:t0 + 64],
                    self.FA[:, 0:64], ALU.mult),
                    waits=(("gc_e1", P.cnt("gc_e1")),
                           ("gc_m2", P.cnt("gc_m2"))), inc="gc_i1")
                P.do("vector", lambda e: e.tensor_copy(
                    self.XA[:, 2 * T:2 * T + 64], self.FA[:, 64:128]),
                    waits=(("gc_i1", P.cnt("gc_i1")),
                           ("gc_m1", P.cnt("gc_m1"))), inc="gc_i2")
                P.do("scalar", lambda e, B_=B_: e.activation(
                    self.FA[:, 128:192], B_, AF.Exp, scale=-1.0),
                    waits=(("gc_i3", P.cnt("gc_i3")),), inc="gc_e2")
                P.do("vector", lambda e, t0=t0: e.tensor_tensor(
                    self.XA[:, 2 * T + 64:2 * T + 128],
                    self.XA[:, T + t0:T + t0 + 64],
                    self.FA[:, 128:192], ALU.mult),
                    waits=(("gc_e2", P.cnt("gc_e2")),
                           ("gc_m1", P.cnt("gc_m1"))), inc="gc_i3")
                P.do("vector", lambda e, B_=B_: e.tensor_scalar(
                    self.FA[:, 192:256], B_, -1.0, B_[:, 63:64],
                    ALU.mult, ALU.add),
                    waits=(("gc_e3", P.cnt("gc_e3")),), inc="gc_i4")
                P.do("scalar", lambda e: e.activation(
                    self.FA[:, 192:256], self.FA[:, 192:256], AF.Exp),
                    waits=(("gc_i4", P.cnt("gc_i4")),), inc="gc_e3")
                P.do("vector", lambda e, t0=t0: e.tensor_tensor(
                    self.XA[:, 2 * T + 128:2 * T + 192],
                    self.XA[:, T + t0:T + t0 + 64],
                    self.FA[:, 192:256], ALU.mult),
                    waits=(("gc_e3", P.cnt("gc_e3")),
                           ("gc_kt", P.cnt("gc_kt"))), inc="gc_i5")
                P.do("tensor", lambda e: e.matmul(
                    self.PS[0][0:64, 0:64], self.XA[:, 2 * T + 64:2 * T + 128],
                    self.XA[:, 2 * T:2 * T + 64], start=True, stop=True),
                    waits=(("gc_i5", P.cnt("gc_i5")),
                           ("gc_i2", P.cnt("gc_i2")),
                           ("gc_at", P.cnt("gc_at"))),
                    inc="gc_m1")
                P.do("scalar", lambda e: e.activation(
                    self.FA[0:64, 256:320], self.PS[0][0:64, 0:64],
                    AF.Copy),
                    waits=(("gc_m1", P.cnt("gc_m1")),
                           ("gc_msk", P.cnt("gc_msk"))), inc="gc_at")
                P.do("gpsimd", lambda e: e.affine_select(
                    self.XA[0:64, 2 * T + 192:2 * T + 256],
                    self.FA[0:64, 256:320],
                    [[1, 64]], ALU.is_ge, self._fill0_reg,
                    base=0, channel_multiplier=-1),
                    waits=(("gc_at", P.cnt("gc_at")),
                           ("gc_m2", P.cnt("gc_m2"))),
                    inc="gc_msk")
                for half in range(2):
                    P.do("tensor", lambda e, half=half, t0=t0: e.matmul(
                        self.PS[1][0:64, half * 128:(half + 1) * 128],
                        self.XB[:, half * T + t0:half * T + t0 + 64],
                        self.IDENTB[:, :], start=True, stop=True),
                        waits=(("gc_vte", P.cnt("gc_vte")),)
                        if half == 0 else (),
                        inc="gc_vt" if half == 1 else None)
                P.do("vector", lambda e: e.tensor_copy(
                    self.XB[0:64, 2 * T:2 * T + 256], self.PS[1][0:64, 0:256]),
                    waits=(("gc_vt", P.cnt("gc_vt")),
                           ("gc_m2", P.cnt("gc_m2")),
                           ("gc_m3", P.cnt("gc_m3"))), inc="gc_vte")
                P.do("tensor", lambda e: e.matmul(
                    self.PS[2][0:64, 0:128], self.XA[:, 2 * T + 128:2 * T + 192],
                    self.IDENTB[:, :], start=True, stop=True),
                    waits=(("gc_kte", P.cnt("gc_kte")),), inc="gc_kt")
                P.do("vector", lambda e: e.tensor_copy(
                    self.XB[0:64, 2 * T + 256:2 * T + 384],
                    self.PS[2][0:64, 0:128]),
                    waits=(("gc_kt", P.cnt("gc_kt")),
                           ("gc_m3", P.cnt("gc_m3"))), inc="gc_kte")
                P.do("tensor", lambda e: e.matmul(
                    self.PS[3][0:64, 0:256], self.XA[0:64, 2 * T + 192:2 * T + 256],
                    self.XB[0:64, 2 * T:2 * T + 256], start=True, stop=False),
                    waits=(("gc_msk", P.cnt("gc_msk")),
                           ("gc_vte", P.cnt("gc_vte")),
                           ("gc_oev", P.cnt("gc_oev"))),
                    inc=None)
                P.do("tensor", lambda e: e.matmul(
                    self.PS[3][0:64, 0:256], self.FA[:, 64:128],
                    self.FC[:, 0:256], start=False, stop=True),
                    waits=(("gc_sup", P.cnt("gc_sup")),
                           ("gc_s0", P.cnt("gc_s0"))),
                    inc="gc_m2")
                P.do("tensor", lambda e: e.matmul(
                    self.PS[4][:, 0:256], self.XB[0:64, 2 * T + 256:2 * T + 384],
                    self.XB[0:64, 2 * T:2 * T + 256], start=True, stop=True),
                    waits=(("gc_kte", P.cnt("gc_kte")),
                           ("gc_sup", P.cnt("gc_sup"))),
                    inc="gc_m3")
                P.do("scalar", lambda e, B_=B_: e.activation(
                    self.FA[:, 320:321], B_[:, 63:64], AF.Exp),
                    waits=(("gc_sup", P.cnt("gc_sup")),), inc="gc_e4")
                P.do("vector", lambda e: e.tensor_scalar(
                    self.FC[:, 0:256], self.FC[:, 0:256],
                    self.FA[:, 320:321], None, ALU.mult),
                    waits=(("gc_e4", P.cnt("gc_e4")),
                           ("gc_m2", P.cnt("gc_m2"))),
                    inc="gc_sc")
                P.do("vector", lambda e: e.tensor_tensor(
                    self.FC[:, 0:256], self.FC[:, 0:256],
                    self.PS[4][:, 0:256], ALU.add),
                    waits=(("gc_sc", P.cnt("gc_sc")),
                           ("gc_m3", P.cnt("gc_m3"))),
                    inc="gc_sup")
                P.do("scalar", lambda e: e.activation(
                    self.FA[0:64, 512:768], self.PS[3][0:64, 0:256], AF.Copy),
                    waits=(("gc_m2", P.cnt("gc_m2")),
                           ("gc_og", P.cnt("gc_og"))),
                    inc="gc_oev")
                P.do("vector", lambda e: e.scalar_tensor_tensor(
                    self.FA[0:64, 768:1024], self.FA[0:64, 512:768], 1.0,
                    self.FA[0:64, 512:768], ALU.mult, ALU.mult,
                    accum_out=self.ROWS[0:64, 0:1]),
                    waits=(("gc_oev", P.cnt("gc_oev")),), inc="gc_ov")
                P.do("scalar", lambda e: e.activation(
                    self.ROWS[0:64, 1:2], self.ROWS[0:64, 0:1], AF.Ln,
                    bias=self.const(1e-6)[0:64, :], scale=1.0 / 256.0),
                    waits=(("gc_ov", P.cnt("gc_ov")),), inc="gc_oc")
                P.do("scalar", lambda e: e.activation(
                    self.ROWS[0:64, 2:3], self.ROWS[0:64, 1:2], AF.Exp,
                    scale=-0.5),
                    waits=(("gc_oc", P.cnt("gc_oc")),), inc="gc_oc")
                P.do("vector", lambda e: e.tensor_scalar(
                    self.FA[0:64, 512:768], self.FA[0:64, 512:768],
                    self.ROWS[0:64, 2:3], None, ALU.mult),
                    waits=(("gc_oc", P.cnt("gc_oc")),), inc="gc_ov")
                P.do("vector", lambda e: e.tensor_tensor(
                    self.XB[0:64, 2 * T + 384:2 * T + 640],
                    self.FA[0:64, 512:768], self.FC[0:64, 256:512],
                    ALU.mult),
                    waits=(("gc_ov", P.cnt("gc_ov")),
                           ("gc_ot", P.cnt("gc_ot"))), inc="gc_og")
                for half in range(2):
                    P.do("tensor", lambda e, half=half: e.matmul(
                        self.PS[5][:, half * 64:(half + 1) * 64],
                        self.XB[0:64, 2 * T + 384 + half * 128:
                                2 * T + 384 + (half + 1) * 128],
                        self.IDENTB[0:64, 0:64], start=True, stop=True),
                        waits=(("gc_ote", P.cnt("gc_ote")),
                               ("gc_og", P.cnt("gc_og")))
                        if half == 0 else (),
                        inc="gc_ot" if half == 1 else None)
                P.do("vector", lambda e: e.tensor_copy(
                    self.OUTS[:, 0:128], self.PS[5][:, 0:128]),
                    waits=(("gc_ot", P.cnt("gc_ot")),
                           ("gc_ost", P.cnt("gc_ost"))),
                    inc="gc_ote")
                P.do("sync", lambda e, hd=hd, t0=t0: e.dma_start(
                    out=o_fm[hd * 256:hd * 256 + 256, t0:t0 + 64]
                    .rearrange("(a p) t -> p a t", p=128),
                    in_=self.OUTS[:, 0:128]
                    .rearrange("p (a t) -> p a t", t=64)),
                    waits=(("gc_ote", P.cnt("gc_ote")),),
                    inc="gc_ost", inc_n=16)
                if hd == 0 and c == 0:
                    dbgb = self.scratch("gla_dbg", (128, 192 + 256))
                    P.do("sync", lambda e: e.dma_start(
                        out=dbgb[:, 0:64], in_=self.FA[:, 4096:4160]),
                        waits=(("gc_b", P.cnt("gc_b")),),
                        inc="gc_dbg", inc_n=16)
                    P.do("gpsimd", lambda e: e.dma_start(
                        out=dbgb[0:64, 64:128],
                        in_=self.XA[0:64, 2 * T + 192:2 * T + 256]),
                        waits=(("gc_msk", P.cnt("gc_msk")),),
                        inc="gc_dbg", inc_n=16)
                    P.do("sync", lambda e: e.dma_start(
                        out=dbgb[0:64, 192:448],
                        in_=self.FA[0:64, 512:768]),
                        waits=(("gc_oev", P.cnt("gc_oev")),),
                        inc="gc_dbg", inc_n=16)
                    P.do("gpsimd", lambda e: e.dma_start(
                        out=dbgb[64:128, 64:192],
                        in_=self.XB[0:64, 2 * T:2 * T + 128]),
                        waits=(("gc_vte", P.cnt("gc_vte")),),
                        inc="gc_dbg", inc_n=16)
                    dbg2 = self.scratch("gla_dbg2", (128, 512))
                    P.do("gpsimd", lambda e: e.dma_start(
                        out=dbg2[0:64, 0:256],
                        in_=self.XB[0:64, 2 * T + 384:2 * T + 640]),
                        waits=(("gc_og", P.cnt("gc_og")),),
                        inc="gc_dbg", inc_n=16)
                    P.do("sync", lambda e: e.dma_start(
                        out=dbg2[0:64, 256:512],
                        in_=self.FC[0:64, 256:512]),
                        waits=(("gc_og", P.cnt("gc_og")),),
                        inc="gc_dbg", inc_n=16)
                    P.wait("gpsimd", ("gc_dbg", P.cnt("gc_dbg")))
                    P.wait("sync", ("gc_dbg", P.cnt("gc_dbg")))
            P.barrier()
        P.barrier()

        o_pre = self.scratch("g1_opre", (1024, T), BF16)
        P.do("sync", lambda e: e.dma_start(out=o_pre[:, :], in_=o_fm[:, :]),
             inc="gp_cp", inc_n=16)
        P.wait("sync", ("gp_cp", P.cnt("gp_cp")))
        P.barrier()
        for tb in range(8):
            t0 = tb * 512
            for k in range(8):
                P.do("sync", lambda e, k=k, t0=t0: e.dma_start(
                    out=self.XA[:, k * 512:(k + 1) * 512],
                    in_=o_fm[k * 128:(k + 1) * 128, t0:t0 + 512]),
                    inc="gg_lda", inc_n=16,
                    waits=(("gg_mu", P.cnt("gg_mu")),) if k == 0 else ())
                P.do("gpsimd", lambda e, k=k, t0=t0: e.dma_start(
                    out=self.FA[:, k * 512:(k + 1) * 512],
                    in_=go_fm[k * 128:(k + 1) * 128, t0:t0 + 512]),
                    inc="gg_ldb", inc_n=16,
                    waits=(("gg_si", P.cnt("gg_si")),) if k == 0 else ())
            P.wait("scalar", "gg_lda", "gg_ldb")
            for k in range(8):
                P.do("scalar", lambda e, k=k: e.activation(
                    self.FB[:, (k % 4) * 512:(k % 4) * 512 + 512],
                    self.FA[:, k * 512:(k + 1) * 512], AF.Silu),
                    waits=(("gg_mu", max(0, P.cnt("gg_mu") - 3)),),
                    inc="gg_si")
                P.do("vector", lambda e, k=k: e.tensor_tensor(
                    self.OUTS[:, (k % 2) * 512:(k % 2) * 512 + 512],
                    self.XA[:, k * 512:(k + 1) * 512],
                    self.FB[:, (k % 4) * 512:(k % 4) * 512 + 512], ALU.mult),
                    waits=(("gg_si", P.cnt("gg_si")),
                           ("gg_st", max(0, P.cnt("gg_st") - 16))),
                    inc="gg_mu")
                P.do("sync", lambda e, k=k, t0=t0: e.dma_start(
                    out=o_fm[k * 128:(k + 1) * 128, t0:t0 + 512],
                    in_=self.OUTS[:, (k % 2) * 512:(k % 2) * 512 + 512]),
                    waits=(("gg_mu", P.cnt("gg_mu")),),
                    inc="gg_st", inc_n=16)
        P.barrier()
        self.emit_proj("g1o", o_fm, 1024, 1024, self.IN["L1_attn_Wo"],
                       ao_fm, act_is_f32=False)
        for tb in range(8):
            t0 = tb * 512
            for k in range(8):
                P.do("sync", lambda e, k=k, t0=t0: e.dma_start(
                    out=self.FA[:, k * 512:(k + 1) * 512],
                    in_=self.xn_fm[k * 128:(k + 1) * 128, t0:t0 + 512]),
                    inc="gx_lda", inc_n=16,
                    waits=(("gx_ad", P.cnt("gx_ad")),) if k == 0 else ())
                P.do("gpsimd", lambda e, k=k, t0=t0: e.dma_start(
                    out=self.FB[:, k * 512:(k + 1) * 512],
                    in_=ao_fm[k * 128:(k + 1) * 128, t0:t0 + 512]),
                    inc="gx_ldb", inc_n=16,
                    waits=(("gx_ad", P.cnt("gx_ad")),) if k == 0 else ())
            P.wait("vector", "gx_lda", "gx_ldb")
            for k in range(8):
                P.do("vector", lambda e, k=k: e.tensor_tensor(
                    self.FA[:, k * 512:(k + 1) * 512],
                    self.FA[:, k * 512:(k + 1) * 512],
                    self.FB[:, k * 512:(k + 1) * 512], ALU.add),
                    inc="gx_ad")
                P.do("sync", lambda e, k=k, t0=t0: e.dma_start(
                    out=x2_fm[k * 128:(k + 1) * 128, t0:t0 + 512],
                    in_=self.FA[:, k * 512:(k + 1) * 512]),
                    waits=(("gx_ad", P.cnt("gx_ad")),),
                    inc="gx_st", inc_n=16)
            P.wait("sync", "gx_st")
        P.barrier()
        self.emit_rms_fm("g1n", x2_fm, n_fm, 1024,
                         self.IN["L1_attn_mlp_norm_w"], "rms", eps=1e-6)
        self.emit_proj("g1mg", n_fm, 1024, 2816, self.IN["L1_attn_W_gate"],
                       mg_fm, act_is_f32=False,
                       epilogue=self.ep_bias_scale(None, 1.0, AF.Silu))
        self.emit_proj("g1mu", n_fm, 1024, 2816, self.IN["L1_attn_W_up"],
                       mu_fm, act_is_f32=False)
        for tb in range(8):
            t0 = tb * 512
            for k in range(22):
                P.do("sync", lambda e, k=k, t0=t0: e.dma_start(
                    out=self.XA[:, (k % 11) * 512:(k % 11) * 512 + 512],
                    in_=mg_fm[k * 128:(k + 1) * 128, t0:t0 + 512]),
                    inc="gm_lda", inc_n=16,
                    waits=(("gm_mu", max(0, P.cnt("gm_mu") - 10)),))
                P.do("gpsimd", lambda e, k=k, t0=t0: e.dma_start(
                    out=self.XB[:, (k % 11) * 512:(k % 11) * 512 + 512],
                    in_=mu_fm[k * 128:(k + 1) * 128, t0:t0 + 512]),
                    inc="gm_ldb", inc_n=16,
                    waits=(("gm_mu", max(0, P.cnt("gm_mu") - 10)),))
                P.wait("vector", ("gm_lda", P.cnt("gm_lda")),
                       ("gm_ldb", P.cnt("gm_ldb")))
                P.do("vector", lambda e, k=k: e.tensor_tensor(
                    self.OUTS[:, (k % 2) * 512:(k % 2) * 512 + 512],
                    self.XA[:, (k % 11) * 512:(k % 11) * 512 + 512],
                    self.XB[:, (k % 11) * 512:(k % 11) * 512 + 512],
                    ALU.mult),
                    waits=(("gm_st", max(0, P.cnt("gm_st") - 16)),),
                    inc="gm_mu")
                P.do("sync", lambda e, k=k, t0=t0: e.dma_start(
                    out=mid_fm[k * 128:(k + 1) * 128, t0:t0 + 512],
                    in_=self.OUTS[:, (k % 2) * 512:(k % 2) * 512 + 512]),
                    waits=(("gm_mu", P.cnt("gm_mu")),),
                    inc="gm_st", inc_n=16)
        P.barrier()
        def ep_addx2(b, tag, bank, oc_abs, ocw, tcols, out_view, prewaits):
            Pq = b.P
            tw = tcols[1] - tcols[0]
            gi = Pq.cnt(tag + "_evd")
            r = b.FC[:, (gi % 2) * 512:(gi % 2) * 512 + tw]
            Pq.do("gpsimd", lambda e: e.dma_start(
                out=r, in_=x2_fm[oc_abs * 128:(oc_abs + 1) * 128,
                                 tcols[0]:tcols[1]]),
                waits=((tag + "_epv", max(0, Pq.cnt(tag + "_epv") - 1)),),
                inc=tag + "_epl", inc_n=16)
            Pq.do("vector", lambda e: e.tensor_tensor(
                out_view, bank[:, 0:tw], r, ALU.add),
                waits=prewaits + ((tag + "_epl", Pq.cnt(tag + "_epl")),),
                inc=tag + "_epv")
            Pq.do("vector", lambda e: e.nop(),
                  waits=((tag + "_epv", Pq.cnt(tag + "_epv")),),
                  inc=tag + "_evd")
        self.emit_proj("g1md", mid_fm, 2816, 1024,
                       self.IN["L1_attn_W_down"], self.s_fm,
                       act_is_f32=False, o_slab=256, t_block=512,
                       act_resident=False, epilogue=ep_addx2)

    def emit_l0_attn(self):
        P = self.P
        xin = self.xn_fm
        for blk in range(2):
            out_f32 = (blk == 0)
            dst = self.scratch(f"m{blk}_out", (1024, T),
                               F32 if out_f32 else BF16) if blk == 0 \
                else self.s_fm
            self._emit_mamba_block(blk, xin, dst, out_f32)
            xin = dst

    def _emit_mamba_block(self, blk, xin_fm, dst_fm, dst_f32):
        P = self.P
        t_ = f"m{blk}"
        h_fm = self.scratch(t_ + "_h", (1024, T), BF16)
        zx_fm = self.scratch(t_ + "_zx", (4224, T), BF16)
        dt_fm = self.scratch(t_ + "_dt", (32, T))
        xc_fm = self.scratch(t_ + "_xc", (2176, T), BF16)
        dacs_scr = self.scratch(t_ + "_dacs", (32, T))
        y_fm = self.scratch(t_ + "_y", (2048, T), BF16)
        yg_fm = self.scratch(t_ + "_yg", (2048, T), BF16)
        ipw = self.tw[f"l0_inproj{blk}"]
        opw = self.tw[f"l0_outproj{blk}"]

        self.emit_rms_fm(t_ + "h", xin_fm, h_fm, 1024,
                         self.IN["L0_attn_norm_w"][blk], "rms", eps=1e-5,
                         src_is_f32=True)
        self.emit_proj(t_ + "p", h_fm, 1024, 4224, ipw, zx_fm,
                       pre_transposed=True, act_is_f32=False)
        ln1 = self.const(1.0)
        P.do("gpsimd", lambda e: e.dma_start(
            out=self.BIAS[0:32, 16:17],
            in_=self.IN["L0_attn_dt_bias"][blk][:]),
            inc=t_ + "d_bl", inc_n=16)
        for k in range(8):
            P.do("sync", lambda e, k=k: e.dma_start(
                out=self.WA[:, k * 32:(k + 1) * 32],
                in_=ipw[k * 128:(k + 1) * 128, 4224:4256]),
                inc=t_ + "d_wl", inc_n=16)
            base = self.XA if k < 4 else self.XB
            P.do("gpsimd", lambda e, k=k, base=base: e.dma_start(
                out=base[:, (k % 4) * T:(k % 4) * T + T],
                in_=h_fm[k * 128:(k + 1) * 128, :]),
                inc=t_ + "d_hl", inc_n=16)
        P.wait("tensor", t_ + "d_wl", t_ + "d_hl")
        for tb in range(8):
            t0 = tb * 512
            for k in range(8):
                base = self.XA if k < 4 else self.XB
                P.do("tensor", lambda e, k=k, t0=t0, base=base: e.matmul(
                    self.PS[5][0:32, 0:512], self.WA[:, k * 32:(k + 1) * 32],
                    base[:, (k % 4) * T + t0:(k % 4) * T + t0 + 512],
                    start=(k == 0), stop=(k == 7)),
                    waits=((t_ + "d_sv", P.cnt(t_ + "d_sv")),)
                    if k == 0 else (),
                    inc=t_ + "d_mm" if k == 7 else None)
            P.do("scalar", lambda e: e.activation(
                self.FA[0:32, 0:512], self.PS[5][0:32, 0:512], AF.Identity,
                bias=self.BIAS[0:32, 16:17]),
                waits=((t_ + "d_mm", P.cnt(t_ + "d_mm")),
                       (t_ + "d_bl", P.cnt(t_ + "d_bl")),
                       (t_ + "d_sv", P.cnt(t_ + "d_sv"))),
                inc=t_ + "d_sv")
            P.do("scalar", lambda e: e.activation(
                self.FA[0:32, 512:1024], self.FA[0:32, 0:512], AF.Abs),
                waits=((t_ + "d_sv", P.cnt(t_ + "d_sv")),), inc=t_ + "d_sv")
            P.do("scalar", lambda e: e.activation(
                self.FA[0:32, 512:1024], self.FA[0:32, 512:1024], AF.Exp,
                scale=-1.0),
                waits=((t_ + "d_sv", P.cnt(t_ + "d_sv")),), inc=t_ + "d_sv")
            P.do("scalar", lambda e: e.activation(
                self.FA[0:32, 512:1024], self.FA[0:32, 512:1024], AF.Ln,
                bias=ln1[0:32, :]),
                waits=((t_ + "d_sv", P.cnt(t_ + "d_sv")),), inc=t_ + "d_sv")
            P.do("vector", lambda e: e.tensor_scalar(
                self.FA[0:32, 0:512], self.FA[0:32, 0:512], 0.0, None,
                ALU.max),
                waits=((t_ + "d_sv", P.cnt(t_ + "d_sv")),), inc=t_ + "d_vv")
            P.do("vector", lambda e: e.tensor_tensor(
                self.FA[0:32, 0:512], self.FA[0:32, 0:512],
                self.FA[0:32, 512:1024], ALU.add),
                waits=((t_ + "d_vv", P.cnt(t_ + "d_vv")),), inc=t_ + "d_vv")
            P.do("sync", lambda e, t0=t0: e.dma_start(
                out=dt_fm[:, t0:t0 + 512], in_=self.FA[0:32, 0:512]),
                waits=((t_ + "d_vv", P.cnt(t_ + "d_vv")),),
                inc=t_ + "d_st", inc_n=16)
            P.wait("scalar", (t_ + "d_st", P.cnt(t_ + "d_st")))
        P.barrier()
        for c in range(17):
            P.do("gpsimd", lambda e, c=c: e.dma_start(
                out=self.BIAS[:, 20:24],
                in_=self.IN["L0_attn_conv_w"][blk][c * 128:(c + 1) * 128, :]),
                inc=t_ + "c_wl", inc_n=16,
                waits=((t_ + "c_ac", P.cnt(t_ + "c_ac")),))
            P.do("gpsimd", lambda e, c=c: e.dma_start(
                out=self.BIAS[:, 24:25],
                in_=self.IN["L0_attn_conv_b"][blk][c * 128:(c + 1) * 128]),
                inc=t_ + "c_wl", inc_n=16)
            for tb in range(8):
                t0 = tb * 512
                if tb == 0:
                    P.do("vector", lambda e: e.memset(
                        self.XA[:, 0:3], 0.0), inc=t_ + "c_ms",
                        waits=((t_ + "c_ac", P.cnt(t_ + "c_ac")),))
                    P.do("sync", lambda e, c=c: e.dma_start(
                        out=self.XA[:, 3:515],
                        in_=zx_fm[2048 + c * 128:2048 + (c + 1) * 128,
                                  0:512]),
                        inc=t_ + "c_ld", inc_n=16)
                else:
                    P.do("sync", lambda e, c=c, t0=t0: e.dma_start(
                        out=self.XA[:, 0:515],
                        in_=zx_fm[2048 + c * 128:2048 + (c + 1) * 128,
                                  t0 - 3:t0 + 512]),
                        inc=t_ + "c_ld", inc_n=16,
                        waits=((t_ + "c_ac", P.cnt(t_ + "c_ac")),))
                P.wait("vector", (t_ + "c_ld", P.cnt(t_ + "c_ld")),
                       (t_ + "c_wl", P.cnt(t_ + "c_wl")),
                       (t_ + "c_ms", P.cnt(t_ + "c_ms")))
                P.do("vector", lambda e: e.tensor_scalar(
                    self.FA[:, 0:512], self.XA[:, 3:515],
                    self.BIAS[:, 23:24], self.BIAS[:, 24:25],
                    ALU.mult, ALU.add), inc=t_ + "c_ac")
                for i in range(3):
                    P.do("vector", lambda e, i=i: e.scalar_tensor_tensor(
                        self.FA[:, 0:512], self.XA[:, i:512 + i],
                        self.BIAS[:, 20 + i:21 + i], self.FA[:, 0:512],
                        ALU.mult, ALU.add),
                        waits=((t_ + "c_ac", P.cnt(t_ + "c_ac")),),
                        inc=t_ + "c_ac")
                P.do("scalar", lambda e: e.activation(
                    self.OUTS[:, 0:512], self.FA[:, 0:512], AF.Silu),
                    waits=((t_ + "c_ac", P.cnt(t_ + "c_ac")),
                           (t_ + "c_st", P.cnt(t_ + "c_st"))),
                    inc=t_ + "c_si")
                P.do("sync", lambda e, c=c, t0=t0: e.dma_start(
                    out=xc_fm[c * 128:(c + 1) * 128, t0:t0 + 512],
                    in_=self.OUTS[:, 0:512]),
                    waits=((t_ + "c_si", P.cnt(t_ + "c_si")),),
                    inc=t_ + "c_st", inc_n=16)
        P.barrier()
        P.do("gpsimd", lambda e: e.dma_start(
            out=self.BIAS[0:32, 17:18], in_=self.IN["L0_attn_A_log"][blk][:]),
            inc=t_ + "s_al", inc_n=16)
        P.do("scalar", lambda e: e.activation(
            self.BIAS[0:32, 18:19], self.BIAS[0:32, 17:18], AF.Exp),
            waits=((t_ + "s_al", P.cnt(t_ + "s_al")),), inc=t_ + "s_ac")
        P.do("vector", lambda e: e.tensor_scalar(
            self.BIAS[0:32, 18:19], self.BIAS[0:32, 18:19], -1.0, None,
            ALU.mult),
            waits=((t_ + "s_ac", P.cnt(t_ + "s_ac")),), inc=t_ + "s_av")
        P.do("sync", lambda e: e.dma_start(
            out=self.COEFR[0:1, 0:32], in_=self.IN["L0_attn_D"][blk][:]),
            inc=t_ + "s_dl", inc_n=16)
        P.do("tensor", lambda e: e.matmul(
            self.PS[0][:, 0:32], self.ONESROWF[0:1, :],
            self.COEFR[0:1, 0:32], start=True, stop=True),
            waits=(t_ + "s_dl",), inc=t_ + "s_db")
        P.do("vector", lambda e: e.tensor_copy(
            self.FB[:, 384:416], self.PS[0][:, 0:32]),
            waits=((t_ + "s_db", P.cnt(t_ + "s_db")),), inc=t_ + "s_dbe")
        P.do("vector", lambda e: e.memset(self.FC[0:64, 0:2048], 0.0),
             inc=t_ + "s_p0")
        P.barrier()
        for z in range(16):
            tz = z * 256
            for c in range(16):
                P.do("sync", lambda e, c=c, tz=tz: e.dma_start(
                    out=self.XA[:, c * 256:(c + 1) * 256],
                    in_=xc_fm[c * 128:(c + 1) * 128, tz:tz + 256]),
                    inc=t_ + "z_ld", inc_n=16,
                    waits=((t_ + "z_tp", P.cnt(t_ + "z_tp")),)
                    if c == 0 else ())
            P.do("sync", lambda e, tz=tz: e.dma_start(
                out=self.XA[0:64, 4096:4352],
                in_=xc_fm[2048:2112, tz:tz + 256]),
                inc=t_ + "z_ld", inc_n=16)
            P.do("sync", lambda e, tz=tz: e.dma_start(
                out=self.XA[0:64, 4352:4608],
                in_=xc_fm[2112:2176, tz:tz + 256]),
                inc=t_ + "z_ld", inc_n=16)
            P.do("gpsimd", lambda e, tz=tz: e.dma_start(
                out=self.FA[0:64, 1024:1280],
                in_=xc_fm[2112:2176, tz:tz + 256]),
                inc=t_ + "z_lc", inc_n=16,
                waits=((t_ + "z_om", P.cnt(t_ + "z_om")),))
            P.do("sync", lambda e, tz=tz: e.dma_start(
                out=self.FA[0:32, 0:256], in_=dt_fm[:, tz:tz + 256]),
                inc=t_ + "z_ld", inc_n=16)
            P.wait("vector", (t_ + "z_ld", P.cnt(t_ + "z_ld")),
                   (t_ + "s_av", P.cnt(t_ + "s_av")))
            P.do("vector", lambda e: e.tensor_scalar(
                self.FA[0:32, 256:512], self.FA[0:32, 0:256],
                self.BIAS[0:32, 18:19], None, ALU.mult),
                waits=((t_ + "z_sc", P.cnt(t_ + "z_sc")),),
                inc=t_ + "z_da")
            P.do("vector", lambda e: e.tensor_tensor_scan(
                self.FA[0:32, 256:512], self.FA[0:32, 256:512],
                self.FA[0:32, 256:512], 0.0, ALU.add, ALU.bypass),
                waits=((t_ + "z_da", P.cnt(t_ + "z_da")),),
                inc=t_ + "z_sc")
            P.do("sync", lambda e, tz=tz: e.dma_start(
                out=dacs_scr[:, tz:tz + 256], in_=self.FA[0:32, 256:512]),
                waits=((t_ + "z_sc", P.cnt(t_ + "z_sc")),),
                inc=t_ + "z_dst", inc_n=16)
            P.wait("tensor", (t_ + "z_dst", P.cnt(t_ + "z_dst")))
            for sub in range(2):
                P.do("tensor", lambda e, sub=sub: e.matmul(
                    self.PS[7][:, sub * 64:sub * 64 + 32],
                    self.FA[0:32, 256 + sub * 128:256 + (sub + 1) * 128],
                    self.IDENT[0:32, 0:32], start=True, stop=True),
                    waits=((t_ + "z_tte", P.cnt(t_ + "z_tte")),)
                    if sub == 0 else (),
                    inc=None)
                P.do("tensor", lambda e, sub=sub: e.matmul(
                    self.PS[7][:, 128 + sub * 64:128 + sub * 64 + 32],
                    self.FA[0:32, sub * 128:(sub + 1) * 128],
                    self.IDENT[0:32, 0:32], start=True, stop=True),
                    inc=(t_ + "z_tt") if sub == 1 else None)
            P.do("vector", lambda e: e.tensor_copy(
                self.FB[:, 0:32], self.PS[7][:, 0:32]),
                waits=((t_ + "z_tt", P.cnt(t_ + "z_tt")),), inc=t_ + "z_tte")
            P.do("vector", lambda e: e.tensor_copy(
                self.FB[:, 32:64], self.PS[7][:, 64:96]),
                inc=t_ + "z_tte")
            P.do("vector", lambda e: e.tensor_copy(
                self.FB[:, 128:160], self.PS[7][:, 128:160]),
                inc=t_ + "z_tte")
            P.do("vector", lambda e: e.tensor_copy(
                self.FB[:, 160:192], self.PS[7][:, 192:224]),
                inc=t_ + "z_tte")
            P.do("vector", lambda e: e.tensor_scalar(
                self.FB[:, 64:128], self.FB[:, 0:64], -1.0, None, ALU.mult),
                waits=((t_ + "z_tte", P.cnt(t_ + "z_tte")),),
                inc=t_ + "z_pre")
            P.do("scalar", lambda e: e.activation(
                self.FB[:, 192:256], self.FB[:, 0:64], AF.Exp),
                waits=((t_ + "z_tte", P.cnt(t_ + "z_tte")),),
                inc=t_ + "z_pre")
            def load_dalast(e, tz=tz):
                with self.nc.allow_non_contiguous_dma("tiny row"):
                    return e.dma_start(
                        out=self.COEFR[0:1, 4064:4096],
                        in_=dacs_scr[:, tz + 255])
            P.do("sync", load_dalast, waits=((t_ + "z_dst", P.cnt(t_ + "z_dst")),),
                 inc=t_ + "z_dll", inc_n=16)
            P.do("tensor", lambda e: e.matmul(
                self.PS[0][:, 0:32], self.ONESROWF[0:1, :],
                self.COEFR[0:1, 4064:4096], start=True, stop=True),
                waits=((t_ + "z_dll", P.cnt(t_ + "z_dll")),
                       (t_ + "z_dle", P.cnt(t_ + "z_dle")),
                       (t_ + "z_ye2", P.cnt(t_ + "z_ye2"))),
                inc=t_ + "z_dlb")
            P.do("vector", lambda e: e.tensor_copy(
                self.FB[:, 256:288], self.PS[0][:, 0:32]),
                waits=((t_ + "z_dlb", P.cnt(t_ + "z_dlb")),),
                inc=t_ + "z_dle")
            P.barrier()
            for c in range(16):
                for sub in range(2):
                    bank = self.PS[2 + ((c * 2 + sub) % 2)]
                    P.wait("tensor",
                           (t_ + "z_xte", P.cnt(t_ + "z_xte")))
                    P.do("tensor", lambda e, c=c, sub=sub, bank=bank:
                         e.matmul(
                             bank[:, 0:128],
                             self.XA[:, c * 256 + sub * 128:
                                     c * 256 + (sub + 1) * 128],
                             self.IDENTB[:, :], start=True, stop=True),
                         inc=t_ + "z_xt")
                    P.do("vector", lambda e, c=c, sub=sub, bank=bank:
                         e.tensor_copy(
                             self.XB[:, sub * 2048 + c * 128:
                                     sub * 2048 + (c + 1) * 128],
                             bank[:, 0:128]),
                         waits=((t_ + "z_xt", P.cnt(t_ + "z_xt")),),
                         inc=t_ + "z_xte")
            for sub in range(2):
                P.do("tensor", lambda e, sub=sub: e.matmul(
                    self.PS[4][:, sub * 64:(sub + 1) * 64],
                    self.XA[0:64, 4096 + sub * 128:4096 + (sub + 1) * 128],
                    self.IDENTB[0:64, 0:64], start=True, stop=True),
                    waits=((t_ + "z_bte", P.cnt(t_ + "z_bte")),)
                    if sub == 0 else (),
                    inc=(t_ + "z_bt") if sub == 1 else None)
            P.do("vector", lambda e: e.tensor_copy(
                self.XB[:, 4096:4224].rearrange("p (s c) -> p s c", c=64),
                self.PS[4][:, 0:128].rearrange("p (s c) -> p s c", c=64)),
                waits=((t_ + "z_bt", P.cnt(t_ + "z_bt")),), inc=t_ + "z_bte")
            for sub in range(2):
                bank = self.PS[2 + sub]
                P.do("tensor", lambda e, sub=sub, bank=bank: e.matmul(
                    bank[:, 0:256],
                    self.XA[0:64, 4096 + sub * 128:4096 + (sub + 1) * 128],
                    self.XA[0:64, 4352:4608], start=True, stop=True),
                    waits=((t_ + "z_xte", P.cnt(t_ + "z_xte")),
                           (t_ + "z_gte", P.cnt(t_ + "z_gte")))
                    if sub == 0 else (),
                    inc=(t_ + "z_gt") if sub == 1 else None)
            for sub in range(2):
                P.do("vector", lambda e, sub=sub: e.tensor_copy(
                    self.XA[:, 4608 + sub * 256:4608 + (sub + 1) * 256],
                    self.PS[2 + sub][:, 0:256]),
                    waits=((t_ + "z_gt", P.cnt(t_ + "z_gt")),),
                    inc=t_ + "z_gte")
            P.do("sync", lambda e, tz=tz: e.dma_start(
                out=self.COEFR[0:1, 0:4096]
                .rearrange("p (h t) -> p h t", t=256)[:, 0:16, :],
                in_=dacs_scr[0:16, tz:tz + 256]),
                waits=((t_ + "z_rb", P.cnt(t_ + "z_rb")),
                       (t_ + "z_dst", P.cnt(t_ + "z_dst"))),
                inc=t_ + "z_rl", inc_n=16)
            P.barrier(("tensor", "vector", "scalar", "sync", "gpsimd"))
            for hd in range(32):
                if hd % 16 == 0:
                    g = hd // 16
                    if g == 1:
                        P.do("sync", lambda e, tz=tz: e.dma_start(
                            out=self.COEFR[0:1, 0:4096]
                            .rearrange("p (h t) -> p h t", t=256)[:, 0:16, :],
                            in_=dacs_scr[16:32, tz:tz + 256]),
                            waits=((t_ + "z_rb", P.cnt(t_ + "z_rb")),),
                            inc=t_ + "z_rl", inc_n=16)
                    P.wait("tensor", (t_ + "z_rl", P.cnt(t_ + "z_rl")))
                hh = hd % 16
                dacs_row = self.COEFR[0:1, hh * 256:(hh + 1) * 256]
                nA0 = self.FB[:, 64 + hd:65 + hd] if False else None
                sub_col = lambda base, sub, hd=hd: \
                    self.FB[:, base + sub * 32 + hd % 32:
                            base + sub * 32 + hd % 32 + 1]
                dacscol = lambda sub: self.FB[:, sub * 32 + hd:sub * 32 + hd + 1]
                negcol = lambda sub: self.FB[:, 64 + sub * 32 + hd:
                                             64 + sub * 32 + hd + 1]
                dtcol = lambda sub: self.FB[:, 128 + sub * 32 + hd:
                                            128 + sub * 32 + hd + 1]
                eacol = lambda sub: self.FB[:, 192 + sub * 32 + hd:
                                            192 + sub * 32 + hd + 1]
                dlcol = self.FB[:, 256 + hd:257 + hd]
                dcol = self.FB[:, 288:289]
                ccol = self.FB[:, 289:290]
                P.do("vector", lambda e, hd=hd: e.tensor_copy(
                    self.BIAS[:, 25:27],
                    self.FB[:, 64 + hd:97 + hd:32]),
                    waits=((t_ + "z_pre", P.cnt(t_ + "z_pre")),
                           (t_ + "z_dx", P.cnt(t_ + "z_dx")),
                           (t_ + "z_oe", P.cnt(t_ + "z_oe"))),
                    inc=t_ + "z_cp")
                P.do("vector", lambda e, hd=hd: e.tensor_copy(
                    self.BIAS[:, 27:29],
                    self.FB[:, 192 + hd:225 + hd:32]),
                    inc=t_ + "z_cp")
                P.do("tensor", lambda e, dacs_row=dacs_row: e.matmul(
                    self.PS[1][:, 0:256], self.ONESROWF[0:1, :],
                    dacs_row, start=True, stop=True),
                    waits=((t_ + "z_dx", P.cnt(t_ + "z_dx")),),
                    inc=t_ + "z_rb")
                for sub in range(2):
                    P.do("scalar", lambda e, sub=sub: e.activation(
                        self.FA[:, 512:768], self.PS[1][:, 0:256], AF.Exp,
                        bias=self.BIAS[:, 25 + sub:26 + sub]),
                        waits=((t_ + "z_rb", P.cnt(t_ + "z_rb")),
                               (t_ + "z_mk", P.cnt(t_ + "z_mk")),
                               (t_ + "z_cp", P.cnt(t_ + "z_cp"))),
                        inc=t_ + "z_dx")
                    P.do("gpsimd", lambda e, sub=sub: e.affine_select(
                        self.XA[:, 5120:5376], self.FA[:, 512:768],
                        [[1, 256]], ALU.is_ge, self._fill0_reg,
                        base=-sub * 128, channel_multiplier=-1),
                        waits=((t_ + "z_dx", P.cnt(t_ + "z_dx")),
                               (t_ + "z_mp", P.cnt(t_ + "z_mp"))),
                        inc=t_ + "z_mk")
                    P.do("vector", lambda e, sub=sub: e.scalar_tensor_tensor(
                        self.XA[:, 5376 + sub * 256:5376 + (sub + 1) * 256],
                        self.XA[:, 4608 + sub * 256:4608 + (sub + 1) * 256],
                        dtcol(sub), self.XA[:, 5120:5376],
                        ALU.mult, ALU.mult),
                        waits=((t_ + "z_mk", P.cnt(t_ + "z_mk")),
                               (t_ + "z_ym", P.cnt(t_ + "z_ym")),
                               (t_ + "z_gte", P.cnt(t_ + "z_gte"))),
                        inc=t_ + "z_mp")
                    if blk == 0 and z == 0 and hd == 0:
                        mdbg = self.SCR.get("m_dbg") or self.scratch(
                            "m_dbg", (128, 1024))
                        P.do("sync", lambda e, sub=sub: e.dma_start(
                            out=mdbg[:, sub * 256:(sub + 1) * 256],
                            in_=self.FA[:, 512:768]),
                            waits=((t_ + "z_dx", P.cnt(t_ + "z_dx")),),
                            inc=t_ + "z_dbg", inc_n=16)
                        P.do("gpsimd", lambda e, sub=sub: e.dma_start(
                            out=mdbg[:, 512 + sub * 256:512 + (sub + 1) * 256],
                            in_=self.XA[:, 5376 + sub * 256:
                                        5376 + (sub + 1) * 256]),
                            waits=((t_ + "z_mp", P.cnt(t_ + "z_mp")),),
                            inc=t_ + "z_dbg", inc_n=16)
                        P.do("sync", lambda e: e.dma_start(
                            out=mdbg[:, 768:832], in_=self.FB[:, 0:64]),
                            inc=t_ + "z_dbg", inc_n=16)
                        P.do("sync", lambda e: e.dma_start(
                            out=mdbg[0:1, 832:1024].rearrange("p c -> p c"),
                            in_=self.COEFR[0:1, 0:192]),
                            inc=t_ + "z_dbg", inc_n=16)
                        P.wait("sync", (t_ + "z_dbg", P.cnt(t_ + "z_dbg")))
                        P.wait("gpsimd", (t_ + "z_dbg", P.cnt(t_ + "z_dbg")))
                        P.barrier()
                for sl in range(2):
                    ybank = self.PS[2 + sl]
                    obank = self.PS[4 + sl]
                    P.wait("tensor", (t_ + "z_yv", P.cnt(t_ + "z_yv")),
                        (t_ + "z_mp", P.cnt(t_ + "z_mp")))
                    for ss in range(2):
                        P.do("tensor", lambda e, sl=sl, ss=ss, hd=hd,
                             ybank=ybank: e.matmul(
                                 ybank[:, 0:64],
                                 self.XA[:, 5376 + ss * 256 + sl * 128:
                                         5376 + ss * 256 + (sl + 1) * 128],
                                 self.XB[:, ss * 2048 + hd * 64:
                                         ss * 2048 + (hd + 1) * 64],
                                 start=(ss == 0), stop=(ss == 1)),
                             inc=(t_ + "z_ym") if ss == 1 else None)
                    P.do("tensor", lambda e, sl=sl, hd=hd, obank=obank:
                         e.matmul(
                             obank[:, 0:64],
                             self.FA[0:64, 1024 + sl * 128:
                                     1024 + (sl + 1) * 128],
                             self.FC[0:64, hd * 64:(hd + 1) * 64],
                             start=True, stop=True),
                         waits=((t_ + "z_pue", P.cnt(t_ + "z_pue")),
                                (t_ + "z_oe", P.cnt(t_ + "z_oe")),
                                (t_ + "z_lc", P.cnt(t_ + "z_lc"))),
                         inc=t_ + "z_om")
                for sl in range(2):
                    ybank = self.PS[2 + sl]
                    obank = self.PS[4 + sl]
                    P.do("scalar", lambda e, sl=sl: e.activation(
                        self.FA[:, 768 + sl * 64:768 + (sl + 1) * 64],
                        obank[:, 0:64], AF.Identity,
                        scale=self.BIAS[:, 27 + sl:28 + sl]),
                        waits=((t_ + "z_om", P.cnt(t_ + "z_om")),
                               (t_ + "z_yf", P.cnt(t_ + "z_yf")),
                               (t_ + "z_cp", P.cnt(t_ + "z_cp"))),
                        inc=t_ + "z_oe")
                    P.do("vector", lambda e, sl=sl, hd=hd:
                         e.scalar_tensor_tensor(
                             self.FA[:, 896 + sl * 64:896 + (sl + 1) * 64],
                             self.XB[:, sl * 2048 + hd * 64:
                                     sl * 2048 + (hd + 1) * 64],
                             self.FB[:, 384 + hd:385 + hd],
                             ybank[:, 0:64], ALU.mult, ALU.add),
                         waits=((t_ + "z_ym", P.cnt(t_ + "z_ym")),
                                (t_ + "z_oe", P.cnt(t_ + "z_oe")),
                                (t_ + "z_yf", P.cnt(t_ + "z_yf"))),
                         inc=t_ + "z_yv")
                    P.do("vector", lambda e, sl=sl: e.tensor_tensor(
                        self.XB[:, 4224 + sl * 64:4224 + (sl + 1) * 64],
                        self.FA[:, 768 + sl * 64:768 + (sl + 1) * 64],
                        self.FA[:, 896 + sl * 64:896 + (sl + 1) * 64],
                        ALU.add),
                        waits=((t_ + "z_yv", P.cnt(t_ + "z_yv")),
                               (t_ + "z_ybt", P.cnt(t_ + "z_ybt"))),
                        inc=t_ + "z_yf")
                for sub in range(2):
                    P.do("vector", lambda e, sub=sub: e.tensor_scalar(
                        dcol, negcol(sub), 1.0, dlcol, ALU.mult, ALU.add),
                        waits=((t_ + "z_se", P.cnt(t_ + "z_se")),
                               (t_ + "z_dle", P.cnt(t_ + "z_dle"))),
                        inc=t_ + "z_sv")
                    P.do("scalar", lambda e: e.activation(
                        dcol, dcol, AF.Exp),
                        waits=((t_ + "z_sv", P.cnt(t_ + "z_sv")),),
                        inc=t_ + "z_se")
                    P.do("vector", lambda e, sub=sub: e.tensor_tensor(
                        ccol, dcol, dtcol(sub), ALU.mult),
                        waits=((t_ + "z_se", P.cnt(t_ + "z_se")),),
                        inc=t_ + "z_sv")
                    P.do("vector", lambda e, sub=sub: e.tensor_scalar(
                        self.XB[:, 4352 + sub * 64:4352 + (sub + 1) * 64],
                        self.XB[:, 4096 + sub * 64:4096 + (sub + 1) * 64],
                        ccol, None, ALU.mult),
                        waits=((t_ + "z_sv", P.cnt(t_ + "z_sv")),
                               (t_ + "z_sm", P.cnt(t_ + "z_sm")),
                               (t_ + "z_bte", P.cnt(t_ + "z_bte"))),
                        inc=t_ + "z_bd")
                for sub in range(2):
                    P.do("tensor", lambda e, sub=sub, hd=hd: e.matmul(
                        self.PS[6][0:64, 0:64],
                        self.XB[:, 4352 + sub * 64:4352 + (sub + 1) * 64],
                        self.XB[:, sub * 2048 + hd * 64:
                                sub * 2048 + (hd + 1) * 64],
                        start=(sub == 0), stop=(sub == 1)),
                        waits=((t_ + "z_bd", P.cnt(t_ + "z_bd")),
                               (t_ + "z_pue", P.cnt(t_ + "z_pue")))
                        if sub == 0 else (),
                        inc=(t_ + "z_sm") if sub == 1 else None)
                P.do("scalar", lambda e: e.activation(
                    self.FB[:, 290:291], dlcol, AF.Exp),
                    waits=((t_ + "z_pue", P.cnt(t_ + "z_pue")),),
                    inc=t_ + "z_px")
                P.do("vector", lambda e, hd=hd: e.tensor_scalar(
                    self.FC[0:64, hd * 64:(hd + 1) * 64],
                    self.FC[0:64, hd * 64:(hd + 1) * 64],
                    self.FB[0:64, 290:291], None, ALU.mult),
                    waits=((t_ + "z_px", P.cnt(t_ + "z_px")),
                           (t_ + "z_om", P.cnt(t_ + "z_om"))),
                    inc=t_ + "z_pu")
                P.do("vector", lambda e, hd=hd: e.tensor_tensor(
                    self.FC[0:64, hd * 64:(hd + 1) * 64],
                    self.FC[0:64, hd * 64:(hd + 1) * 64],
                    self.PS[6][0:64, 0:64], ALU.add),
                    waits=((t_ + "z_pu", P.cnt(t_ + "z_pu")),
                           (t_ + "z_sm", P.cnt(t_ + "z_sm"))),
                    inc=t_ + "z_pue")
                for sl in range(2):
                    P.do("tensor", lambda e, sl=sl: e.matmul(
                        self.PS[0][0:64, sl * 128:(sl + 1) * 128],
                        self.XB[:, 4224 + sl * 64:4224 + (sl + 1) * 64],
                        self.IDENTB[:, :], start=True, stop=True),
                        waits=((t_ + "z_yf", P.cnt(t_ + "z_yf")),
                               (t_ + "z_ye2", P.cnt(t_ + "z_ye2")),
                               (t_ + "z_dle", P.cnt(t_ + "z_dle")))
                        if sl == 0 else (),
                        inc=(t_ + "z_ybt") if sl == 1 else None)
                P.do("vector", lambda e: e.tensor_copy(
                    self.OUTS[0:64, 0:256], self.PS[0][0:64, 0:256]),
                    waits=((t_ + "z_ybt", P.cnt(t_ + "z_ybt")),
                           (t_ + "z_yst", P.cnt(t_ + "z_yst"))),
                    inc=t_ + "z_ye2")
                P.do("sync", lambda e, hd=hd, tz=tz: e.dma_start(
                    out=y_fm[hd * 64:(hd + 1) * 64, tz:tz + 256],
                    in_=self.OUTS[0:64, 0:256]),
                    waits=((t_ + "z_ye2", P.cnt(t_ + "z_ye2")),),
                    inc=t_ + "z_yst", inc_n=16)
            P.barrier()
        for tb in range(8):
            t0 = tb * 512
            for k in range(16):
                P.do("sync", lambda e, k=k, t0=t0: e.dma_start(
                    out=self.XA[:, k * 512:(k + 1) * 512],
                    in_=y_fm[k * 128:(k + 1) * 128, t0:t0 + 512]),
                    inc=t_ + "g_la", inc_n=16,
                    waits=((t_ + "g_mu", P.cnt(t_ + "g_mu")),)
                    if k == 0 else ())
                P.do("gpsimd", lambda e, k=k, t0=t0: e.dma_start(
                    out=self.XB[:, k * 512:(k + 1) * 512],
                    in_=zx_fm[k * 128:(k + 1) * 128, t0:t0 + 512]),
                    inc=t_ + "g_lb", inc_n=16,
                    waits=((t_ + "g_si", P.cnt(t_ + "g_si")),)
                    if k == 0 else ())
            P.wait("scalar", (t_ + "g_la", P.cnt(t_ + "g_la")),
                   (t_ + "g_lb", P.cnt(t_ + "g_lb")))
            for k in range(16):
                P.do("scalar", lambda e, k=k: e.activation(
                    self.FA[:, (k % 8) * 512:(k % 8) * 512 + 512],
                    self.XB[:, k * 512:(k + 1) * 512], AF.Silu),
                    waits=((t_ + "g_mu", max(0, P.cnt(t_ + "g_mu") - 7)),),
                    inc=t_ + "g_si")
                P.do("vector", lambda e, k=k: e.tensor_tensor(
                    self.OUTS[:, (k % 2) * 512:(k % 2) * 512 + 512],
                    self.XA[:, k * 512:(k + 1) * 512],
                    self.FA[:, (k % 8) * 512:(k % 8) * 512 + 512], ALU.mult),
                    waits=((t_ + "g_si", P.cnt(t_ + "g_si")),
                           (t_ + "g_st", P.cnt(t_ + "g_st"))),
                    inc=t_ + "g_mu")
                P.do("sync", lambda e, k=k, t0=t0: e.dma_start(
                    out=yg_fm[k * 128:(k + 1) * 128, t0:t0 + 512],
                    in_=self.OUTS[:, (k % 2) * 512:(k % 2) * 512 + 512]),
                    waits=((t_ + "g_mu", P.cnt(t_ + "g_mu")),),
                    inc=t_ + "g_st", inc_n=16)
        P.barrier()
        yn_fm = self.scratch(t_ + "_yn", (2048, T), BF16)
        self.emit_rms_fm(t_ + "gn", yg_fm, yn_fm, 2048,
                         self.IN["L0_attn_gnorm_w"][blk], "rms", eps=1e-5,
                         src_is_f32=False)
        def ep_addx(b, tag, bank, oc_abs, ocw, tcols, out_view, prewaits):
            Pq = b.P
            tw2 = tcols[1] - tcols[0]
            gi = Pq.cnt(tag + "_evd")
            r = b.FC[:, 1024 + (gi % 2) * 512:1024 + (gi % 2) * 512 + tw2]
            Pq.do("gpsimd", lambda e: e.dma_start(
                out=r, in_=xin_fm[oc_abs * 128:(oc_abs + 1) * 128,
                                  tcols[0]:tcols[1]]),
                waits=((tag + "_epv", max(0, Pq.cnt(tag + "_epv") - 1)),),
                inc=tag + "_epl", inc_n=16)
            Pq.do("vector", lambda e: e.tensor_tensor(
                out_view, bank[:, 0:tw2], r, ALU.add),
                waits=prewaits + ((tag + "_epl", Pq.cnt(tag + "_epl")),),
                inc=tag + "_epv")
            Pq.do("vector", lambda e: e.nop(),
                  waits=((tag + "_epv", Pq.cnt(tag + "_epv")),),
                  inc=tag + "_evd")
        self.emit_proj(t_ + "o", yn_fm, 2048, 1024, opw, dst_fm,
                       pre_transposed=True, act_is_f32=False,
                       epilogue=ep_addx, out_f32=dst_f32)

    def emit_l3_ffn(self):
        P = self.P
        ce8 = self.scratch("l3_ce8", (8, T))
        h_scr = self.scratch("l3_h", (2048, T), BF16)
        g_acc = self.scratch("l3_acc", (1024, T))
        P.do("sync", lambda e: e.dma_start(
            out=self.FA[0:8, 0:1024], in_=self.IN["L3_ffn_router_w"][:, :]),
            inc="l3r_ld", inc_n=16)
        P.wait("tensor", "l3r_ld")
        for k in range(8):
            P.do("tensor", lambda e, k=k: e.matmul(
                self.PS[0][:, k * 8:(k + 1) * 8],
                self.FA[0:8, k * 128:(k + 1) * 128],
                self.IDENT[0:8, 0:8], start=True, stop=True),
                inc="l3r_mm" if k == 7 else None)
        P.do("vector", lambda e: e.tensor_copy(
            self.FB[:, 0:64], self.PS[0][:, 0:64]),
            waits=("l3r_mm",), inc="l3r_ev")
        P.do("vector", lambda e: e.tensor_copy(
            self.WB[:, 0:64], self.FB[:, 0:64]),
            waits=(("l3r_ev", P.cnt("l3r_ev")),), inc="l3r_ev")
        for k in range(8):
            base = self.XA if k < 4 else self.XB
            P.do("gpsimd", lambda e, k=k, base=base: e.dma_start(
                out=base[:, (k % 4) * T:(k % 4) * T + T],
                in_=self.xn_fm[k * 128:(k + 1) * 128, :]),
                inc="l3r_ald", inc_n=16)
        P.wait("tensor", "l3r_ev", "l3r_ald")
        for tt in range(32):
            P.wait("tensor", ("l3r_x", P.cnt("l3r_x")))
            for k in range(8):
                base = self.XA if k < 4 else self.XB
                P.do("tensor", lambda e, k=k, tt=tt, base=base: e.matmul(
                    self.PS[1][:, 0:8],
                    base[:, (k % 4) * T + tt * 128:
                         (k % 4) * T + (tt + 1) * 128],
                    self.WB[:, k * 8:(k + 1) * 8],
                    start=(k == 0), stop=(k == 7)),
                    inc="l3r_lm" if k == 7 else None)
            P.do("vector", lambda e: e.reduce_max(
                self.ROWS[:, 0:1], self.PS[1][:, 0:8], axis=AX.X,
                negate=True),
                waits=(("l3r_lm", P.cnt("l3r_lm")),), inc="l3r_v")
            P.do("scalar", lambda e: e.activation(
                self.FA[:, 0:8], self.PS[1][:, 0:8], AF.Exp,
                bias=self.ROWS[:, 0:1], accum_out=self.ROWS[:, 1:2]),
                waits=(("l3r_v", P.cnt("l3r_v")),), inc="l3r_x")
            P.do("vector", lambda e: e.reciprocal(
                self.ROWS[:, 2:3], self.ROWS[:, 1:2]),
                waits=(("l3r_x", P.cnt("l3r_x")),), inc="l3r_v")
            P.do("vector", lambda e: e.tensor_scalar(
                self.FA[:, 0:8], self.FA[:, 0:8], self.ROWS[:, 2:3], None,
                ALU.mult),
                waits=(("l3r_v", P.cnt("l3r_v")),), inc="l3r_v")
            P.do("vector", lambda e: e.max(
                self.FA[:, 8:16], self.FA[:, 0:8]),
                waits=(("l3r_v", P.cnt("l3r_v")),), inc="l3r_v")
            P.do("vector", lambda e: e.tensor_tensor(
                self.ROWS[:, 3:4], self.FA[:, 8:9], self.FA[:, 9:10],
                ALU.add),
                waits=(("l3r_v", P.cnt("l3r_v")),), inc="l3r_v")
            P.do("vector", lambda e: e.tensor_scalar(
                self.ROWS[:, 3:4], self.ROWS[:, 3:4], 1e-6, None, ALU.add),
                waits=(("l3r_v", P.cnt("l3r_v")),), inc="l3r_v")
            P.do("vector", lambda e: e.reciprocal(
                self.ROWS[:, 4:5], self.ROWS[:, 3:4]),
                waits=(("l3r_v", P.cnt("l3r_v")),), inc="l3r_v")
            P.do("vector", lambda e: e.tensor_scalar(
                self.ROWS[:, 5:6], self.FA[:, 8:9], self.ROWS[:, 4:5], None,
                ALU.mult),
                waits=(("l3r_v", P.cnt("l3r_v")),), inc="l3r_v")
            P.do("vector", lambda e: e.tensor_scalar(
                self.ROWS[:, 6:7], self.FA[:, 9:10], self.ROWS[:, 4:5], None,
                ALU.mult),
                waits=(("l3r_v", P.cnt("l3r_v")),), inc="l3r_v")
            P.do("vector", lambda e: e.tensor_scalar(
                self.FA[:, 16:24], self.FA[:, 0:8], self.FA[:, 8:9],
                self.ROWS[:, 5:6], ALU.is_equal, ALU.mult),
                waits=(("l3r_v", P.cnt("l3r_v")),), inc="l3r_v")
            P.do("vector", lambda e: e.tensor_scalar(
                self.FA[:, 24:32], self.FA[:, 0:8], self.FA[:, 9:10],
                self.ROWS[:, 6:7], ALU.is_equal, ALU.mult),
                waits=(("l3r_v", P.cnt("l3r_v")),), inc="l3r_v")
            P.do("vector", lambda e: e.tensor_tensor(
                self.FA[:, 16:24], self.FA[:, 16:24], self.FA[:, 24:32],
                ALU.add),
                waits=(("l3r_v", P.cnt("l3r_v")),), inc="l3r_v")
            P.do("tensor", lambda e: e.matmul(
                self.PS[2][0:8, 0:128], self.FA[:, 16:24],
                self.IDENT[:, :], start=True, stop=True),
                waits=(("l3r_v", P.cnt("l3r_v")),
                       ("l3r_ce", P.cnt("l3r_ce"))),
                inc="l3r_ct")
            P.do("vector", lambda e: e.tensor_copy(
                self.FB[0:8, 64:192], self.PS[2][0:8, 0:128]),
                waits=(("l3r_ct", P.cnt("l3r_ct")),
                       ("l3r_cst", P.cnt("l3r_cst"))),
                inc="l3r_ce")
            P.do("sync", lambda e, tt=tt: e.dma_start(
                out=ce8[:, tt * 128:(tt + 1) * 128],
                in_=self.FB[0:8, 64:192]),
                waits=(("l3r_ce", P.cnt("l3r_ce")),),
                inc="l3r_cst", inc_n=16)
        P.barrier()
        for ex in range(8):
            w13 = self.IN["L3_ffn_w13"][ex]
            w2 = self.IN["L3_ffn_w2"][ex]
            for k in range(8):
                base = self.WA if k < 4 else self.WB
                P.do("gpsimd", lambda e, k=k, base=base, w13=w13:
                     e.dma_start(
                         out=base[:, (k % 4) * 4096:(k % 4) * 4096 + 4096],
                         in_=w13[k * 128:(k + 1) * 128, :]),
                     inc="l3w_ld", inc_n=16,
                     waits=(("l3h_mm", P.cnt("l3h_mm")),) if k == 0 else ())
            P.wait("tensor", ("l3w_ld", P.cnt("l3w_ld")))
            for tb in range(8):
                t0 = tb * 512
                for ic in range(16):
                    bg = self.PS[ic % 2]
                    bu = self.PS[2 + ic % 2]
                    P.wait("tensor",
                           ("l3h_si", max(0, P.cnt("l3h_si") - 1)),
                           ("l3h_hm", max(0, P.cnt("l3h_hm") - 1)))
                    for k in range(8):
                        xb = self.XA if k < 4 else self.XB
                        wb = self.WA if k < 4 else self.WB
                        P.do("tensor", lambda e, k=k, ic=ic, t0=t0, xb=xb,
                             wb=wb, bg=bg: e.matmul(
                                 bg[:, 0:512],
                                 wb[:, (k % 4) * 4096 + ic * 128:
                                    (k % 4) * 4096 + (ic + 1) * 128],
                                 xb[:, (k % 4) * T + t0:
                                    (k % 4) * T + t0 + 512],
                                 start=(k == 0), stop=(k == 7)),
                             inc=None)
                    for k in range(8):
                        xb = self.XA if k < 4 else self.XB
                        wb = self.WA if k < 4 else self.WB
                        P.do("tensor", lambda e, k=k, ic=ic, t0=t0, xb=xb,
                             wb=wb, bu=bu: e.matmul(
                                 bu[:, 0:512],
                                 wb[:, (k % 4) * 4096 + 2048 + ic * 128:
                                    (k % 4) * 4096 + 2048 + (ic + 1) * 128],
                                 xb[:, (k % 4) * T + t0:
                                    (k % 4) * T + t0 + 512],
                                 start=(k == 0), stop=(k == 7)),
                             inc="l3h_mm" if k == 7 else None)
                    P.do("scalar", lambda e, bg=bg: e.activation(
                        self.FC[:, 0:512], bg[:, 0:512], AF.Silu),
                        waits=(("l3h_mm", P.cnt("l3h_mm")),
                               ("l3h_hm", P.cnt("l3h_hm"))),
                        inc="l3h_si")
                    P.do("vector", lambda e, bu=bu: e.tensor_tensor(
                        self.OUTS[:, 0:512], self.FC[:, 0:512],
                        bu[:, 0:512], ALU.mult),
                        waits=(("l3h_si", P.cnt("l3h_si")),
                               ("l3h_st", P.cnt("l3h_st"))),
                        inc="l3h_hm")
                    P.do("sync", lambda e, ic=ic, t0=t0: e.dma_start(
                        out=h_scr[ic * 128:(ic + 1) * 128, t0:t0 + 512],
                        in_=self.OUTS[:, 0:512]),
                        waits=(("l3h_hm", P.cnt("l3h_hm")),),
                        inc="l3h_st", inc_n=16)
            P.barrier()
            for k in range(16):
                P.do("gpsimd", lambda e, k=k, w2=w2: e.dma_start(
                    out=self.WB[:, k * 1024:(k + 1) * 1024],
                    in_=w2[k * 128:(k + 1) * 128, :]),
                    inc="l3y_wl", inc_n=16)
            for tb in range(8):
                t0 = tb * 512
                for k in range(16):
                    P.do("sync", lambda e, k=k, t0=t0: e.dma_start(
                        out=self.WA[:, 8192 + k * 512:8192 + (k + 1) * 512],
                        in_=h_scr[k * 128:(k + 1) * 128, t0:t0 + 512]),
                        inc="l3y_hl", inc_n=16,
                        waits=(("l3y_mm", P.cnt("l3y_mm")),)
                        if k == 0 else ())
                P.do("sync", lambda e, t0=t0, ex=ex: e.dma_start(
                    out=self.COEFR[0:1, 0:512], in_=ce8[ex, t0:t0 + 512]),
                    inc="l3y_cl", inc_n=16,
                    waits=(("l3y_cb", P.cnt("l3y_cb")),))
                P.do("tensor", lambda e: e.matmul(
                    self.PS[6][:, 0:512], self.ONESROWF[0:1, :],
                    self.COEFR[0:1, 0:512], start=True, stop=True),
                    waits=(("l3y_cl", P.cnt("l3y_cl")),
                           ("l3y_cbe", P.cnt("l3y_cbe"))),
                    inc="l3y_cb")
                P.do("vector", lambda e: e.tensor_copy(
                    self.FB[:, 512:1024], self.PS[6][:, 0:512]),
                    waits=(("l3y_cb", P.cnt("l3y_cb")),
                           ("l3y_mu", P.cnt("l3y_mu"))),
                    inc="l3y_cbe")
                P.wait("tensor", ("l3y_hl", P.cnt("l3y_hl")),
                       ("l3y_wl", P.cnt("l3y_wl")))
                for oc in range(8):
                    bank = self.PS[4 + (oc % 2)]
                    P.wait("tensor",
                           ("l3y_mu", max(0, P.cnt("l3y_mu") - 1)))
                    for k in range(16):
                        P.do("tensor", lambda e, k=k, oc=oc, bank=bank:
                             e.matmul(
                                 bank[:, 0:512],
                                 self.WB[:, k * 1024 + oc * 128:
                                         k * 1024 + (oc + 1) * 128],
                                 self.WA[:, 8192 + k * 512:
                                         8192 + (k + 1) * 512],
                                 start=(k == 0), stop=(k == 15)),
                             inc="l3y_mm" if k == 15 else None)
                    P.do("vector", lambda e, bank=bank: e.tensor_tensor(
                        self.FC[:, 512:1024], bank[:, 0:512],
                        self.FB[:, 512:1024], ALU.mult),
                        waits=(("l3y_mm", P.cnt("l3y_mm")),
                               ("l3y_ast", P.cnt("l3y_ast")),
                               ("l3y_cbe", P.cnt("l3y_cbe"))),
                        inc="l3y_mu")
                    if ex > 0:
                        P.do("gpsimd", lambda e, oc=oc, t0=t0: e.dma_start(
                            out=self.FC[:, 1024:1536],
                            in_=g_acc[oc * 128:(oc + 1) * 128, t0:t0 + 512]),
                            inc="l3y_al", inc_n=16,
                            waits=(("l3y_mu", P.cnt("l3y_mu")),))
                        P.do("vector", lambda e: e.tensor_tensor(
                            self.FC[:, 512:1024], self.FC[:, 512:1024],
                            self.FC[:, 1024:1536], ALU.add),
                            waits=(("l3y_al", P.cnt("l3y_al")),),
                            inc="l3y_mu")
                    P.do("sync", lambda e, oc=oc, t0=t0: e.dma_start(
                        out=g_acc[oc * 128:(oc + 1) * 128, t0:t0 + 512],
                        in_=self.FC[:, 512:1024]),
                        waits=(("l3y_mu", P.cnt("l3y_mu")),),
                        inc="l3y_ast", inc_n=16)
                    P.wait("vector", ("l3y_ast", P.cnt("l3y_ast")))
            P.barrier()
        for tb in range(8):
            t0 = tb * 512
            for k in range(8):
                P.do("sync", lambda e, k=k, t0=t0: e.dma_start(
                    out=self.FA[:, k * 512:(k + 1) * 512],
                    in_=g_acc[k * 128:(k + 1) * 128, t0:t0 + 512]),
                    inc="l3g_ld", inc_n=16,
                    waits=(("l3g_si", P.cnt("l3g_si")),) if k == 0 else ())
            P.wait("scalar", ("l3g_ld", P.cnt("l3g_ld")))
            for k in range(8):
                P.do("scalar", lambda e, k=k: e.activation(
                    self.OUTS[:, (k % 2) * 512:(k % 2) * 512 + 512],
                    self.FA[:, k * 512:(k + 1) * 512], AF.Silu),
                    waits=(("l3g_st", max(0, P.cnt("l3g_st") - 16)),),
                    inc="l3g_si")
                P.do("sync", lambda e, k=k, t0=t0: e.dma_start(
                    out=self.g_fm[k * 128:(k + 1) * 128, t0:t0 + 512],
                    in_=self.OUTS[:, (k % 2) * 512:(k % 2) * 512 + 512]),
                    waits=(("l3g_si", P.cnt("l3g_si")),),
                    inc="l3g_st", inc_n=16)
        P.barrier()

    def emit_residual(self):
        P = self.P
        for tb in range(T // 512):
            t0 = tb * 512
            for k in range(8):
                P.do("sync", lambda e, k=k, t0=t0: e.dma_start(
                    out=self.FA[:, k * 512:(k + 1) * 512],
                    in_=self.x_fm[k * 128:(k + 1) * 128, t0:t0 + 512]),
                    inc="rs_ld", inc_n=16)
                P.do("gpsimd", lambda e, k=k, t0=t0: e.dma_start(
                    out=self.XA[:, k * 512:(k + 1) * 512],
                    in_=self.s_fm[k * 128:(k + 1) * 128, t0:t0 + 512]),
                    inc="rs_ld", inc_n=16)
                P.do("gpsimd", lambda e, k=k, t0=t0: e.dma_start(
                    out=self.XB[:, k * 512:(k + 1) * 512],
                    in_=self.g_fm[k * 128:(k + 1) * 128, t0:t0 + 512]),
                    inc="rs_ld", inc_n=16)
            P.wait("vector", "rs_ld")
            for k in range(8):
                P.do("vector", lambda e, k=k: e.tensor_tensor(
                    self.FB[:, k * 512:(k + 1) * 512],
                    self.XA[:, k * 512:(k + 1) * 512],
                    self.XB[:, k * 512:(k + 1) * 512], ALU.mult),
                    inc="rs_v1")
            for k in range(8):
                P.do("vector", lambda e, k=k: e.tensor_tensor(
                    self.FA[:, k * 512:(k + 1) * 512],
                    self.FA[:, k * 512:(k + 1) * 512],
                    self.FB[:, k * 512:(k + 1) * 512], ALU.add),
                    waits=(("rs_v1", P.cnt("rs_v1") - 7 + k),),
                    inc="rs_v2")
            for k in range(8):
                P.do("sync", lambda e, k=k, t0=t0: e.dma_start(
                    out=self.x_fm[k * 128:(k + 1) * 128, t0:t0 + 512],
                    in_=self.FA[:, k * 512:(k + 1) * 512]),
                    waits=(("rs_v2", P.cnt("rs_v2") - 7 + k),),
                    inc="rs_st", inc_n=16)
            P.wait("sync", "rs_st")
            P.barrier()

    def emit_output(self):
        P = self.P
        for g in range(8):
            t0 = g * 512
            for k in range(8):
                P.do("sync", lambda e, k=k, t0=t0: e.dma_start(
                    out=self.FA[:, k * 512:(k + 1) * 512],
                    in_=self.x_fm[k * 128:(k + 1) * 128, t0:t0 + 512]),
                    inc="o_ld", inc_n=16,
                    waits=(("o_mm", max(0, P.cnt("o_mm") - 1)),))
            P.wait("tensor", "o_ld")
            for j in range(4):
                for half in range(2):
                    bank = self.PS[half]
                    P.wait("tensor", ("o_ev", max(0, P.cnt("o_ev") - 1)))
                    for k in range(4):
                        kk = half * 4 + k
                        P.do("tensor", lambda e, j=j, kk=kk, k=k, bank=bank:
                             e.matmul(
                                 bank[:, k * 128:(k + 1) * 128],
                                 self.FA[:, kk * 512 + j * 128:
                                         kk * 512 + (j + 1) * 128],
                                 self.IDENT[:, :], start=True, stop=True),
                             inc="o_mm" if k == 3 else None)
                    P.do("vector", lambda e, half=half, bank=bank: e.tensor_copy(
                        self.FB[:, half * 512:(half + 1) * 512], bank[:, :]),
                        waits=("o_mm",), inc="o_ev")
                P.do("sync", lambda e, j=j, t0=t0: e.dma_start(
                    out=self.y_out[t0 + j * 128: t0 + (j + 1) * 128, :],
                    in_=self.FB[:, 0:1024]),
                    waits=("o_ev",), inc="o_st", inc_n=16)
                P.wait("vector", ("o_st", P.cnt("o_st")))
        P.barrier()

    def finalize(self):
        P = self.P
        for scr_name in self.dbg_outputs:
            ap = self.SCR[scr_name]
            shape = list(ap.tensor.shape)
            o = self.nc.dram_tensor("dbg_" + scr_name, shape,
                                    ap.tensor.dtype, kind="ExternalOutput").ap()
            P.do("sync", lambda e, o=o, ap=ap: e.dma_start(out=o[:, :], in_=ap[:, :]),
                 inc="dbg_d", inc_n=16)
        P.wait("sync", "dbg_d") if self.dbg_outputs else None
        P.wait("sync", "o_st") if "o_st" in P.log_cnt else None
        P.finish(self.es)
        return self.nc


STAGES_FULL = ("init", "x_in", "weight_prep",
               "ln0", "l0_attn", "l0_ffn", "residual",
               "ln0", "l1_attn", "l1_ffn", "residual",
               "ln0", "l2_attn", "l2_ffn", "residual",
               "ln0", "l3_attn", "l3_ffn", "residual",
               "output")


def build_program(stages=STAGES_FULL, dbg=(), ext_scratch=()):
    b = Builder(ext_scratch=ext_scratch)
    for scr in dbg:
        pass
    emitted_dbg = set()
    for st in stages:
        if st == "init":
            b.emit_init()
        elif st == "x_in":
            b.emit_x_in()
        elif st == "weight_prep":
            b.emit_weight_prep()
        elif st == "ln0":
            b.emit_module_ln()
        elif st == "l1_ffn":
            b.emit_l1_ffn()
        elif st == "l0_attn":
            b.emit_l0_attn()
        elif st == "l0_ffn":
            b.emit_l0_ffn()
        elif st == "l1_attn":
            b.emit_l1_attn()
        elif st == "l3_ffn":
            b.emit_l3_ffn()
        elif st == "l2_ffn":
            b.emit_l2_ffn()
        elif st.startswith("l2_ffn:"):
            b.emit_l2_ffn(phases=tuple(range(int(st.split(":")[1]) + 1)))
        elif st == "l2_attn":
            b.emit_attention("l2a", b.tw["l2_qkv"], b.IN["L2_attn_b_qkv"],
                             1.0, b.tw["l2_o"], b.IN["L2_attn_b_o"], 1.0,
                             0.125, 1.0, True, b.s_fm)
        elif st == "l3_attn":
            import math as _m
            b.emit_attention("l3a", b.tw["l3_qkv"], None, 1.0 / 32.0,
                             b.tw["l3_o"], None, 1.0 / 32.0, 0.125,
                             4096.0 / _m.sqrt(4095.0) / 64.0, False, b.s_fm)
        elif st == "residual":
            b.emit_residual()
        elif st == "output":
            b.emit_output()
        else:
            raise ValueError(st)
    for scr in dbg:
        b.dbg(scr)
    nc = b.finalize()
    return nc



_cache = {}


def make_in_maps(inputs):
    ident = np.eye(128, dtype=np.float32)
    x_full = np.ascontiguousarray(inputs["L0_attn_x"])
    zero_ew = np.zeros_like(inputs["L1_ffn_expert_weights"])
    in_maps = []
    for c in range(NCORES):
        m = {}
        for k, v in inputs.items():
            if k == "L0_attn_x":
                m[k] = np.ascontiguousarray(x_full[c])
            elif k == "L1_ffn_expert_weights":
                m[k] = inputs[k] if c == 0 else zero_ew
            elif k in ("L1_ffn_scores", "L1_ffn_top_experts"):
                continue
            else:
                m[k] = v
        m["c_ident"] = ident
        in_maps.append(m)
    return in_maps


def kernel(**inputs):
    key = "full"
    if key not in _cache:
        _cache[key] = build_program()
    nc = _cache[key]
    in_maps = make_in_maps(inputs)
    res = run_bass_kernel_spmd(nc, in_maps, core_ids=list(range(NCORES)))
    y = np.stack([res.results[c]["y"] for c in range(NCORES)], axis=0)
    return y
